# version 03
# baseline (speedup 1.0000x reference)
import numpy as np
import concourse.bass as bass
import concourse.mybir as mybir
from contextlib import ExitStack

F32 = mybir.dt.float32
BF16 = mybir.dt.bfloat16
AF = mybir.ActivationFunctionType
ALU = mybir.AluOpType
AX = mybir.AxisListType

D = 2048
B = 2
S = 4096
NTOK = B * S
DSL = 256
N = 16
QT = 1024
NQ = S // QT
EPS = 1e-5
SCALING = 2.0


class Sem:
    def __init__(self, nc, ctx, name):
        self.h = ctx.enter_context(nc.semaphore(name))
        self.n = 0


class Prog:
    ENGS = ("sync", "vector", "tensor", "scalar", "gpsimd")

    def __init__(self):
        self.q = {e: [] for e in self.ENGS}

    def op(self, eng, fn, waits=(), inc=None, inc_k=1):
        tgt = None
        if inc is not None:
            inc.n += inc_k
            tgt = inc.n
        wl = tuple((s.h, v) for s, v in waits if v > 0)

        def emit(e, fn=fn, wl=wl, inc=inc, inc_k=inc_k):
            for h, v in wl:
                e.wait_ge(h, v)
            instr = fn(e)
            if inc is not None:
                instr.then_inc(inc.h, inc_k)
        self.q[eng].append(emit)
        return tgt

    def wait(self, eng, sem, val=None):
        v = sem.n if val is None else val
        if v > 0:
            self.q[eng].append(lambda e, h=sem.h, v=v: e.wait_ge(h, v))

    def barrier(self, sems, engines=None):
        engines = engines or self.ENGS
        counts = [(s, s.n) for s in sems]
        for eng in engines:
            for s, v in counts:
                if v > 0:
                    self.wait(eng, s, v)

    def finish(self, nc):
        block_ctx = nc.Block()
        block = block_ctx.__enter__()
        for name in self.ENGS:
            lst = self.q[name]
            getattr(block, name)(lambda e, lst=lst: [f(e) for f in lst])
        block_ctx.__exit__(None, None, None)


class L0:
    def __init__(self, nc):
        self.nc = nc
        self.ctx = ExitStack()
        self.pr = Prog()
        self._declare_dram()
        self._alloc()

    def _declare_dram(self):
        nc = self.nc
        def din(name, shape):
            return nc.dram_tensor(name, list(shape), F32, kind="ExternalInput").ap()
        self.x_fm = din("xfm", [B, D, S])
        self.xsl = din("xsl", [B, DSL, S])
        self.wv_d = din("wv", [D, DSL])
        self.wxc_d = din("wxc", [D, DSL])
        self.wbt_d = din("wbt", [D, N])
        self.wdl_d = din("wdl", [128, 16])
        self.wbase_d = din("wbase", [D, DSL])
        self.wr_d = din("wr", [D, 8])
        self.wlo_d = din("wlo", [D, 128])
        self.wbw_d = din("wbw", [128, DSL])
        self.cdn_d = din("cdn", [16, 4096])
        self.alog_d = din("alog", [128, 32])
        self.cw_d = din("cw", [128, 8])
        self.bv_d = din("bv", [128, 2])
        self.bxc_d = din("bxc", [128, 2])
        self.bbase_d = din("bbase", [128, 2])
        self.bbt_d = din("bbt", [16, 1])
        self.bpar_d = din("bpar", [16, 1])
        self.bdl_d = din("bdl", [1, 1])
        self.ident_d = din("ident", [128, 128])
        self.ones_row_d = din("ones_row", [1, 128])
        self.ones_col_d = din("ones_col", [128, 1])
        self.sel16_d = din("sel16", [8, 128])
        self.x1p = nc.dram_tensor("x1p", [DSL, NTOK], F32, kind="ExternalOutput").ap()
        self.dbg = {}
        if getattr(L0, "DEBUG", False):
            def dout(name, shape):
                self.dbg[name] = nc.dram_tensor(name, list(shape), F32, kind="ExternalOutput").ap()
            dout("g_xn", [2048, QT])
            dout("g_sv", [256, QT])
            dout("g_xcv", [256, QT])
            dout("g_btq", [16, QT])
            dout("g_drep", [128, QT])
            dout("g_hlo", [128, QT])
            dout("g_ohfm", [8, QT])
            dout("g_ya", [128, QT])
            dout("g_x1a", [256, QT])

    def _alloc(self):
        nc, ctx = self.nc, self.ctx
        sbuf = lambda name, shape, dt=F32: ctx.enter_context(nc.sbuf_tensor(name, list(shape), dt))
        self.wv = sbuf("z_wv", [128, 16 * DSL], BF16)
        self.wxc = sbuf("z_wxc", [128, 16 * DSL], BF16)
        self.wbase = sbuf("z_wbase", [128, 16 * DSL], BF16)
        self.wlo = sbuf("z_wlo", [128, 16 * 128], BF16)
        self.wbt = sbuf("z_wbt", [128, 16 * N], BF16)
        self.wdl = sbuf("z_wdl", [128, 16], BF16)
        self.wr = sbuf("z_wr", [128, 16 * 8], BF16)
        self.wbw = sbuf("z_wbw", [128, DSL], BF16)
        self.cdn = sbuf("z_cdn", [16, 4096], BF16)
        self.sel16 = sbuf("z_sel16", [8, 128], BF16)
        self.ones_row_b = sbuf("z_ones_row_b", [1, 128], BF16)
        self.ones_col_b = sbuf("z_ones_col_b", [128, 1], BF16)
        self.ones_row_f = sbuf("z_ones_row_f", [1, 128], F32)
        self.ident_f = sbuf("z_ident_f", [128, 128], F32)
        self.alog_f = sbuf("z_alog", [128, 32], F32)
        self.cw_f = sbuf("z_cw", [128, 8], F32)
        self.bv_f = sbuf("z_bv", [128, 2], F32)
        self.bxc_f = sbuf("z_bxc", [128, 2], F32)
        self.bbase_f = sbuf("z_bbase", [128, 2], F32)
        self.bbt_f = sbuf("z_bbt", [16, 1], F32)
        self.bpar_f = sbuf("z_bpar", [16, 1], F32)
        self.bdl_f = sbuf("z_bdl", [1, 1], F32)
        self.zcol = sbuf("z_zcol", [128, 1], F32)

        self.xbf = sbuf("z_xbf", [128, 16 * QT], BF16)
        self.sqv = sbuf("z_sqv", [128, 4 * QT], BF16)
        self.m_rep = sbuf("z_mrep", [128, QT], F32)
        self.r_rep = sbuf("z_rrep", [128, QT], F32)
        self.nrm_t = sbuf("z_nrmt", [128, 2 * QT], BF16)
        self.srow = sbuf("z_srow", [1, 3 * QT], F32)
        self.sv = sbuf("z_sv", [128, 2 * QT], BF16)
        self.sxc = sbuf("z_sxc", [128, 2 * (QT + 3)], BF16)
        self.xcv = sbuf("z_xcv", [128, 2 * QT], BF16)
        self.x1a = sbuf("z_x1a", [128, 2 * QT], F32)
        self.ya = sbuf("z_ya", [128, QT], F32)
        self.hlo = sbuf("z_hlo", [128, QT], BF16)
        self.lgs = sbuf("z_lgs", [32, QT], F32)
        self.dlrow = sbuf("z_dlrow", [1, 3 * QT], F32)
        self.drep = sbuf("z_drep", [128, QT], F32)
        self.d16 = sbuf("z_d16", [16, QT], F32)
        self.btq = sbuf("z_btq", [16, QT], F32)
        self.dbb = sbuf("z_dbb", [16, QT], BF16)
        self.da_t = sbuf("z_dat", [128, 2 * QT], F32)
        self.u_t = sbuf("z_ut", [128, 2 * QT], BF16)
        self.h_t = sbuf("z_ht", [128, 2 * QT], F32)
        self.hstate = sbuf("z_hstate", [128, 32], F32)
        self.oh_tm = sbuf("z_ohtm", [128, 8], F32)
        self.mx_c = sbuf("z_mxc", [128, 1], F32)
        self.oh_fm = sbuf("z_ohfm", [8, QT], BF16)
        self.hm = sbuf("z_hm", [128, QT], BF16)
        self.xres = sbuf("z_xres", [128, 2 * QT], F32)

        self.P8 = ctx.enter_context(nc.psum_tensor("P8", [128, 8 * 512], F32))

        ss = lambda name: Sem(nc, ctx, name)
        self.s_ld = ss("s_ld")
        self.s_x = ss("s_x")
        self.s_mm = ss("s_mm")
        self.s_ve = ss("s_ve")
        self.s_se = ss("s_se")
        self.s_st = ss("s_st")
        self.s_fr = ss("s_fr")

    def pa(self, i, w=512, p=128):
        return self.P8[0:p, i * 512: i * 512 + w]

    def dma(self, eng, out_ap, in_ap, sem, waits=()):
        return self.pr.op(eng, lambda e, o=out_ap, i=in_ap: e.dma_start(out=o, in_=i),
                          waits=waits, inc=sem, inc_k=16)

    def vop(self, fn, waits=(), fence=False, inc=None):
        w = list(waits)
        if fence and self.s_ve.n > 0:
            w.append((self.s_ve, self.s_ve.n))
        return self.pr.op("vector", fn, waits=w, inc=self.s_ve if inc is None else inc)

    def sop(self, fn, waits=(), fence=False):
        w = list(waits)
        if fence and self.s_se.n > 0:
            w.append((self.s_se, self.s_se.n))
        return self.pr.op("scalar", fn, waits=w, inc=self.s_se)

    def mm(self, fn, waits=(), inc=False, inc_sem=None):
        sem = inc_sem if inc_sem is not None else (self.s_mm if inc else None)
        return self.pr.op("tensor", fn, waits=waits, inc=sem)

    def build(self):
        self._setup()
        for b in range(B):
            for q in range(NQ):
                self._quarter(b, q)
        self.pr.wait("gpsimd", self.s_st)

    def _setup(self):
        casts = [
            (self.wv, self.wv_d, True), (self.wxc, self.wxc_d, True),
            (self.wbase, self.wbase_d, True), (self.wlo, self.wlo_d, True),
            (self.wbt, self.wbt_d, True), (self.wdl, self.wdl_d, False),
            (self.wr, self.wr_d, True),
            (self.wbw, self.wbw_d, False), (self.cdn, self.cdn_d, False),
            (self.sel16, self.sel16_d, False),
            (self.ones_row_b, self.ones_row_d, False),
            (self.ones_col_b, self.ones_col_d, False),
        ]
        for dst, src, rearr in casts:
            if rearr:
                self.dma("gpsimd", dst[:].rearrange("p (k m) -> p k m", k=16),
                         src.rearrange("(k p) m -> p k m", p=128), self.s_ld)
            else:
                self.dma("gpsimd", dst[:], src, self.s_ld)
        for dst, src in (
            (self.ones_row_f, self.ones_row_d), (self.ident_f, self.ident_d),
            (self.alog_f, self.alog_d), (self.cw_f, self.cw_d),
            (self.bv_f, self.bv_d), (self.bxc_f, self.bxc_d), (self.bbase_f, self.bbase_d),
            (self.bbt_f, self.bbt_d), (self.bpar_f, self.bpar_d), (self.bdl_f, self.bdl_d),
        ):
            self.dma("sync", dst[:], src, self.s_ld)
        n_ld = self.s_ld.n
        t = self.sop(lambda e: e.activation(self.alog_f[:], self.alog_f[:], AF.Exp),
                     waits=[(self.s_ld, n_ld)])
        self.vop(lambda e: e.tensor_scalar_mul(self.alog_f[:], self.alog_f[:], -1.0),
                 waits=[(self.s_se, t)])
        self.vop(lambda e: e.memset(self.zcol[:], 0.0))
        self.vop(lambda e: e.memset(self.hstate[:], 0.0))
        self.vop(lambda e: e.memset(self.sxc[:], 0.0))

    def _quarter(self, b, q):
        pr = self.pr
        tok0 = q * QT
        s_x, s_mm, s_ve, s_se, s_fr = self.s_x, self.s_mm, self.s_ve, self.s_se, self.s_fr

        pr.barrier([self.s_mm, self.s_ve, self.s_se, self.s_st, self.s_ld, self.s_x])

        if q == 0:
            self.vop(lambda e: e.memset(self.sxc[:, 0:3], 0.0), fence=True)
            self.vop(lambda e: e.memset(self.sxc[:, QT + 3: QT + 6], 0.0))

        x0 = s_x.n
        for k in range(16):
            self.dma("gpsimd", self.xbf[:, k * QT:(k + 1) * QT],
                     self.x_fm[b, k * 128:(k + 1) * 128, tok0:tok0 + QT], s_x)
        sq_done = {}
        sqmm_done = {}
        x_all = s_x.n
        for k in range(16):
            w = [(s_x, x_all)]
            if k >= 4:
                w.append((s_fr, sqmm_done[k - 4]))
            sq_done[k] = self.sop(lambda e, k=k: e.activation(
                self.sqv[:, (k % 4) * QT:(k % 4 + 1) * QT],
                self.xbf[:, k * QT:(k + 1) * QT], AF.Square), waits=w)
            for ts in range(2):
                self.mm(lambda e, k=k, ts=ts: e.matmul(
                    self.P8[0:1, ts * 512:(ts + 1) * 512], self.ones_col_b[:],
                    self.xbf[:, k * QT + ts * 512: k * QT + (ts + 1) * 512],
                    start=(k == 0), stop=(k == 15)),
                    waits=[(s_x, x_all)] if ts == 0 else [],
                    inc=(k == 15 and ts == 1))
            for ts in range(2):
                self.mm(lambda e, k=k, ts=ts: e.matmul(
                    self.P8[0:1, 1024 + ts * 512: 1024 + (ts + 1) * 512], self.ones_col_b[:],
                    self.sqv[:, (k % 4) * QT + ts * 512:(k % 4) * QT + (ts + 1) * 512],
                    start=(k == 0), stop=(k == 15)),
                    waits=[(s_se, sq_done[k])] if ts == 0 else [],
                    inc_sem=(s_fr if ts == 1 else None))
            sqmm_done[k] = s_fr.n
        t_stats_x = s_mm.n
        t_stats_sq = s_fr.n

        MROW, VROW, RROW = 0, 1, 2
        def row(i):
            return self.srow[0:1, i * QT:(i + 1) * QT]
        f = self.vop(lambda e: e.tensor_scalar_mul(row(MROW), self.P8[0:1, 0:QT], 1.0 / D),
                     waits=[(s_mm, t_stats_x), (s_fr, t_stats_sq)])
        f = self.vop(lambda e: e.tensor_scalar_mul(row(VROW), self.P8[0:1, QT:2 * QT], 1.0 / D))
        f = self.vop(lambda e: e.tensor_mul(row(RROW), row(MROW), row(MROW)), fence=True)
        f = self.vop(lambda e: e.tensor_sub(row(VROW), row(VROW), row(RROW)), fence=True)
        f = self.vop(lambda e: e.tensor_scalar_add(row(VROW), row(VROW), EPS), fence=True)
        fs = self.sop(lambda e: e.activation(row(RROW), row(VROW), AF.Sqrt),
                      waits=[(s_ve, f)])
        f = self.vop(lambda e: e.reciprocal(row(RROW), row(RROW)), waits=[(s_se, fs)], fence=True)
        for ts in range(2):
            self.mm(lambda e, ts=ts: e.matmul(
                self.P8[:, 2048 + ts * 512: 2048 + (ts + 1) * 512], self.ones_row_f[:],
                self.srow[0:1, MROW * QT + ts * 512: MROW * QT + (ts + 1) * 512],
                start=True, stop=True),
                waits=[(s_ve, f)] if ts == 0 else [], inc=(ts == 1))
        tb = s_mm.n
        fm = self.vop(lambda e: e.tensor_copy(self.m_rep[:], self.P8[:, 2048:2048 + QT]),
                      waits=[(s_mm, tb)])
        for ts in range(2):
            self.mm(lambda e, ts=ts: e.matmul(
                self.P8[:, 2048 + ts * 512: 2048 + (ts + 1) * 512], self.ones_row_f[:],
                self.srow[0:1, RROW * QT + ts * 512: RROW * QT + (ts + 1) * 512],
                start=True, stop=True),
                waits=[(s_ve, fm)] if ts == 0 else [], inc=(ts == 1))
        tb = s_mm.n
        self.vop(lambda e: e.tensor_copy(self.r_rep[:], self.P8[:, 2048:2048 + QT]),
                 waits=[(s_mm, tb)])

        for k in range(16):
            self.vop(lambda e, k=k: e.tensor_sub(
                self.nrm_t[:, (k % 2) * QT:(k % 2 + 1) * QT],
                self.xbf[:, k * QT:(k + 1) * QT], self.m_rep[:]), fence=True)
            self.vop(lambda e, k=k: e.tensor_mul(
                self.xbf[:, k * QT:(k + 1) * QT],
                self.nrm_t[:, (k % 2) * QT:(k % 2 + 1) * QT], self.r_rep[:]), fence=True)
        t_norm = s_ve.n

        if self.dbg and b == 0 and q == 0:
            tnow = s_ve.n
            for kk in range(16):
                self.dma("gpsimd", self.dbg["g_xn"][kk * 128:(kk + 1) * 128, :],
                         self.xbf[:, kk * QT:(kk + 1) * QT], self.s_st,
                         waits=[(s_ve, tnow)] if kk == 0 else [])
        if getattr(L0, "TRUNC", 0) == 1:
            return
        self.vop(lambda e: e.memset(self.lgs[:], -1e30))
        t_lgpad = s_ve.n
        mts = [("wv", 0, "v", 0), ("wv", 1, "v", 1),
               ("wxc", 0, "xc", 0), ("wxc", 1, "xc", 1),
               ("wbase", 0, "base", 0), ("wbase", 1, "base", 1),
               ("wlo", 0, "hl", 0),
               ("wbt", 0, "bt", 0), ("wdl", 0, "dl", 0), ("wr", 0, "lg", 0)]
        mwidth = {"v": 128, "xc": 128, "base": 128, "hl": 128, "bt": 16, "dl": 1, "lg": 8}
        for g0 in range(0, len(mts), 2):
            grp = mts[g0:g0 + 2]
            pr.wait("tensor", s_ve)
            pr.wait("tensor", s_se)
            for k in range(16):
                for gi, (wname, mcol, kind, xi) in enumerate(grp):
                    wsb = getattr(self, wname)
                    ww = mwidth[kind]
                    ws = wsb.shape[1] // 16
                    for ts in range(2):
                        is_last = (k == 15 and gi == len(grp) - 1 and ts == 1)
                        self.mm(lambda e, k=k, gi=gi, wsb=wsb, mcol=mcol, ww=ww, ts=ts, ws=ws: e.matmul(
                            self.P8[0:ww, (gi * 2 + ts) * 512:(gi * 2 + ts + 1) * 512],
                            wsb[:, k * ws + mcol * 128: k * ws + mcol * 128 + ww],
                            self.xbf[:, k * QT + ts * 512: k * QT + (ts + 1) * 512],
                            start=(k == 0), stop=(k == 15)),
                            waits=[(s_ve, t_norm)] if (g0 == 0 and k == 0 and gi == 0 and ts == 0) else [],
                            inc=is_last)
            t_grp = s_mm.n
            for gi, (wname, mcol, kind, xi) in enumerate(grp):
                for ts in range(2):
                    po = (gi * 2 + ts) * 512
                    if kind == "v":
                        self.sop(lambda e, xi=xi, po=po, ts=ts: e.activation(
                            self.sv[:, xi * QT + ts * 512: xi * QT + (ts + 1) * 512],
                            self.P8[0:128, po:po + 512], AF.Silu, bias=self.bv_f[:, xi:xi + 1]),
                            waits=[(s_mm, t_grp)])
                    elif kind == "xc":
                        self.sop(lambda e, xi=xi, po=po, ts=ts: e.activation(
                            self.sxc[:, xi * (QT + 3) + 3 + ts * 512: xi * (QT + 3) + 3 + (ts + 1) * 512],
                            self.P8[0:128, po:po + 512], AF.Silu, bias=self.bxc_f[:, xi:xi + 1]),
                            waits=[(s_mm, t_grp)])
                    elif kind == "base":
                        self.sop(lambda e, xi=xi, po=po, ts=ts: e.activation(
                            self.x1a[:, xi * QT + ts * 512: xi * QT + (ts + 1) * 512],
                            self.P8[0:128, po:po + 512], AF.Identity, bias=self.bbase_f[:, xi:xi + 1]),
                            waits=[(s_mm, t_grp)])
                    elif kind == "hl":
                        self.vop(lambda e, po=po, ts=ts: e.tensor_copy(
                            self.hlo[:, ts * 512:(ts + 1) * 512], self.P8[0:128, po:po + 512]),
                            waits=[(s_mm, t_grp)])
                    elif kind == "bt":
                        self.vop(lambda e, po=po, ts=ts: e.tensor_scalar(
                            self.btq[:, ts * 512:(ts + 1) * 512], self.P8[0:16, po:po + 512],
                            self.bbt_f[:], self.bpar_f[:], ALU.add, ALU.add),
                            waits=[(s_mm, t_grp)])
                    elif kind == "dl":
                        self.vop(lambda e, po=po, ts=ts: e.tensor_scalar(
                            self.dlrow[0:1, ts * 512:(ts + 1) * 512], self.P8[0:1, po:po + 512],
                            self.bdl_f[:], None, ALU.add),
                            waits=[(s_mm, t_grp)])
                    elif kind == "lg":
                        self.vop(lambda e, po=po, ts=ts: e.tensor_copy(
                            self.lgs[0:8, ts * 512:(ts + 1) * 512], self.P8[0:8, po:po + 512]),
                            waits=[(s_mm, t_grp), (s_ve, t_lgpad)])

        if self.dbg and b == 0 and q == 0:
            tse = s_se.n
            tve = s_ve.n
            for c2 in range(2):
                self.dma("gpsimd", self.dbg["g_sv"][c2 * 128:(c2 + 1) * 128, :],
                         self.sv[:, c2 * QT:(c2 + 1) * QT], self.s_st,
                         waits=[(s_se, tse), (s_ve, tve)] if c2 == 0 else [])
            self.dma("gpsimd", self.dbg["g_hlo"], self.hlo[:], self.s_st)
            self.dma("sync", self.dbg["g_btq"], self.btq[:], self.s_st)
        if getattr(L0, "TRUNC", 0) == 2:
            return
        DL, T1r, T2r = 0, 1, 2
        SP, DRow = DL, T1r
        def drow(i):
            return self.dlrow[0:1, i * QT:(i + 1) * QT]
        fs = self.sop(lambda e: e.activation(drow(T1r), drow(DL), AF.Abs),
                      waits=[(s_ve, s_ve.n)], fence=True)
        fs = self.sop(lambda e: e.activation(drow(T1r), drow(T1r), AF.Exp, bias=0.0, scale=-1.0),
                      fence=True)
        fv = self.vop(lambda e: e.tensor_scalar_add(drow(T1r), drow(T1r), 1.0),
                      waits=[(s_se, fs)], fence=True)
        fs = self.sop(lambda e: e.activation(drow(T1r), drow(T1r), AF.Ln),
                      waits=[(s_ve, fv)], fence=True)
        fv = self.vop(lambda e: e.tensor_scalar_max(drow(T2r), drow(DL), 0.0), fence=True)
        fv = self.vop(lambda e: e.tensor_add(drow(SP), drow(T1r), drow(T2r)),
                      waits=[(s_se, fs)], fence=True)
        fs = self.sop(lambda e: e.activation(drow(T1r), drow(SP), AF.Exp, bias=0.0, scale=-1.0),
                      waits=[(s_ve, fv)], fence=True)
        fv = self.vop(lambda e: e.tensor_scalar_add(drow(T2r), drow(T1r), 1.0),
                      waits=[(s_se, fs)], fence=True)
        fv = self.vop(lambda e: e.reciprocal(drow(T2r), drow(T2r)), fence=True)
        fv = self.vop(lambda e: e.tensor_mul(drow(DRow), drow(SP), drow(T2r)), fence=True)
        for ts in range(2):
            self.mm(lambda e, ts=ts: e.matmul(
                self.P8[:, 2048 + ts * 512: 2048 + (ts + 1) * 512], self.ones_row_f[:],
                self.dlrow[0:1, DRow * QT + ts * 512: DRow * QT + (ts + 1) * 512],
                start=True, stop=True),
                waits=[(s_ve, fv)] if ts == 0 else [], inc=(ts == 1))
        tb = s_mm.n
        fv = self.vop(lambda e: e.tensor_copy(self.drep[:], self.P8[:, 2048:2048 + QT]),
                      waits=[(s_mm, tb)])
        for ts in range(2):
            self.mm(lambda e, ts=ts: e.matmul(
                self.P8[0:16, 2048 + ts * 512: 2048 + (ts + 1) * 512], self.ones_row_f[0:1, 0:16],
                self.dlrow[0:1, DRow * QT + ts * 512: DRow * QT + (ts + 1) * 512],
                start=True, stop=True),
                waits=[(s_ve, fv)] if ts == 0 else [], inc=(ts == 1))
        tb = s_mm.n
        fv = self.vop(lambda e: e.tensor_copy(self.d16[:], self.P8[0:16, 2048:2048 + QT]),
                      waits=[(s_mm, tb)])
        t_dbb = self.vop(lambda e: e.tensor_mul(self.dbb[:], self.btq[:], self.d16[:]), fence=True)

        if self.dbg and b == 0 and q == 0:
            self.dma("sync", self.dbg["g_drep"], self.drep[:], self.s_st,
                     waits=[(s_ve, s_ve.n)])
        t_conv = {}
        for c2 in range(2):
            base = c2 * (QT + 3)
            self.vop(lambda e, c2=c2, base=base: e.tensor_scalar(
                self.xcv[:, c2 * QT:(c2 + 1) * QT], self.sxc[:, base:base + QT],
                self.cw_f[:, c2 * 4:c2 * 4 + 1], None, ALU.mult),
                waits=[(s_se, s_se.n)], fence=True)
            for kk in range(1, 4):
                self.vop(lambda e, c2=c2, base=base, kk=kk: e.scalar_tensor_tensor(
                    self.xcv[:, c2 * QT:(c2 + 1) * QT], self.sxc[:, base + kk:base + kk + QT],
                    self.cw_f[:, c2 * 4 + kk:c2 * 4 + kk + 1],
                    self.xcv[:, c2 * QT:(c2 + 1) * QT], ALU.mult, ALU.add), fence=True)
            t_conv[c2] = self.vop(lambda e, base=base: e.tensor_copy(
                self.sxc[:, base:base + 3], self.sxc[:, base + QT:base + QT + 3]), fence=True)

        if getattr(L0, "TRUNC", 0) == 3:
            return
        if self.dbg and b == 0 and q == 0:
            for c2 in range(2):
                self.dma("gpsimd", self.dbg["g_xcv"][c2 * 128:(c2 + 1) * 128, :],
                         self.xcv[:, c2 * QT:(c2 + 1) * QT], self.s_st,
                         waits=[(s_ve, s_ve.n)] if c2 == 0 else [])
        v_ready = s_ve.n
        prev_scan = {}
        for c2 in range(2):
            for n in range(N):
                ti = c2 * N + n
                slot = ti % 2
                acol = self.alog_f[:, c2 * 16 + n: c2 * 16 + n + 1]
                w = [(s_ve, v_ready)]
                if ti >= 2:
                    w.append((s_ve, prev_scan[slot]))
                fs = self.sop(lambda e, slot=slot, acol=acol: e.activation(
                    self.da_t[:, slot * QT:(slot + 1) * QT], self.drep[:],
                    AF.Exp, bias=self.zcol[:], scale=acol), waits=w)
                for ts in range(2):
                    w2 = []
                    if ts == 0:
                        w2 = [(s_ve, t_dbb), (s_ve, t_conv[c2])]
                        if ti >= 2:
                            w2.append((s_ve, prev_scan[slot]))
                    self.mm(lambda e, c2=c2, n=n, ts=ts, slot=slot: e.matmul(
                        self.P8[:, (slot * 2 + ts) * 512:(slot * 2 + ts + 1) * 512],
                        self.cdn[:, (c2 * 16 + n) * 128:(c2 * 16 + n + 1) * 128],
                        self.dbb[:, ts * 512:(ts + 1) * 512],
                        start=True, stop=True), waits=w2, inc=(ts == 1))
                tu = s_mm.n
                fu = self.vop(lambda e, slot=slot, c2=c2: e.tensor_mul(
                    self.u_t[:, slot * QT:(slot + 1) * QT],
                    self.P8[:, slot * 1024: slot * 1024 + QT],
                    self.xcv[:, c2 * QT:(c2 + 1) * QT]),
                    waits=[(s_mm, tu)], fence=True)
                init_ap = self.zcol[:] if q == 0 else self.hstate[:, ti:ti + 1]
                fsc = self.vop(lambda e, slot=slot, ti=ti, init_ap=init_ap: e.tensor_tensor_scan(
                    self.h_t[:, slot * QT:(slot + 1) * QT],
                    self.da_t[:, slot * QT:(slot + 1) * QT],
                    self.u_t[:, slot * QT:(slot + 1) * QT],
                    init_ap, ALU.mult, ALU.add),
                    waits=[(s_se, fs)], fence=True)
                self.vop(lambda e, slot=slot, ti=ti: e.tensor_copy(
                    self.hstate[:, ti:ti + 1],
                    self.h_t[:, (slot + 1) * QT - 1:(slot + 1) * QT]), fence=True)
                if n == 0:
                    fy = self.vop(lambda e, slot=slot: e.tensor_copy(
                        self.ya[:], self.h_t[:, slot * QT:(slot + 1) * QT]), fence=True)
                else:
                    fy = self.vop(lambda e, slot=slot: e.tensor_add(
                        self.ya[:], self.ya[:], self.h_t[:, slot * QT:(slot + 1) * QT]), fence=True)
                prev_scan[slot] = fy
            if self.dbg and b == 0 and q == 0 and c2 == 0:
                self.dma("sync", self.dbg["g_ya"], self.ya[:], self.s_st,
                         waits=[(s_ve, s_ve.n)])
            self.vop(lambda e, c2=c2: e.tensor_mul(
                self.nrm_t[:, 0:QT], self.ya[:], self.sv[:, c2 * QT:(c2 + 1) * QT]),
                waits=[(s_se, s_se.n)], fence=True)
            self.vop(lambda e, c2=c2: e.tensor_add(
                self.x1a[:, c2 * QT:(c2 + 1) * QT], self.x1a[:, c2 * QT:(c2 + 1) * QT],
                self.nrm_t[:, 0:QT]), fence=True)

        for tc in range(QT // 128):
            tt = self.mm(lambda e, tc=tc: e.transpose(
                self.P8[:, 2048:2080], self.lgs[:, tc * 128:(tc + 1) * 128],
                self.ident_f[0:32, 0:32]),
                waits=[(s_ve, s_ve.n)], inc=True)
            self.vop(lambda e: e.reduce_max(self.mx_c[:], self.P8[:, 2048:2056], axis=AX.X),
                     waits=[(s_mm, tt)], fence=True)
            fv = self.vop(lambda e: e.tensor_single_scalar(
                self.oh_tm[:], self.P8[:, 2048:2056], self.mx_c[:], ALU.is_ge), fence=True)
            tt2 = self.mm(lambda e: e.transpose(
                self.P8[0:8, 2560:2688], self.oh_tm[:], self.ident_f[:]),
                waits=[(s_ve, fv)], inc=True)
            self.vop(lambda e, tc=tc: e.tensor_copy(
                self.oh_fm[:, tc * 128:(tc + 1) * 128], self.P8[0:8, 2560:2688]),
                waits=[(s_mm, tt2)], fence=True)
        t_oh = s_ve.n
        for ts in range(2):
            self.mm(lambda e, ts=ts: e.matmul(
                self.P8[:, 2048 + ts * 512: 2048 + (ts + 1) * 512], self.sel16[:],
                self.oh_fm[:, ts * 512:(ts + 1) * 512], start=True, stop=True),
                waits=[(s_ve, t_oh)] if ts == 0 else [], inc=(ts == 1))
        tb = s_mm.n
        fv = self.vop(lambda e: e.tensor_mul(self.hm[:], self.hlo[:], self.P8[:, 2048:2048 + QT]),
                      waits=[(s_mm, tb)], fence=True)
        for mtc in range(2):
            for ts in range(2):
                self.mm(lambda e, mtc=mtc, ts=ts: e.matmul(
                    self.P8[:, (mtc * 2 + ts) * 512:(mtc * 2 + ts + 1) * 512],
                    self.wbw[:, mtc * 128:(mtc + 1) * 128],
                    self.hm[:, ts * 512:(ts + 1) * 512], start=True, stop=True),
                    waits=[(s_ve, fv)] if (mtc == 0 and ts == 0) else [],
                    inc=(mtc == 1 and ts == 1))
        tb = s_mm.n
        for mtc in range(2):
            self.vop(lambda e, mtc=mtc: e.scalar_tensor_tensor(
                self.x1a[:, mtc * QT:(mtc + 1) * QT],
                self.P8[:, mtc * 1024: mtc * 1024 + QT], SCALING,
                self.x1a[:, mtc * QT:(mtc + 1) * QT], ALU.mult, ALU.add),
                waits=[(s_mm, tb)], fence=True)

        if self.dbg and b == 0 and q == 0:
            self.dma("gpsimd", self.dbg["g_ohfm"], self.oh_fm[:], self.s_st,
                     waits=[(s_ve, s_ve.n)])
            for c2 in range(2):
                self.dma("sync", self.dbg["g_x1a"][c2 * 128:(c2 + 1) * 128, :],
                         self.x1a[:, c2 * QT:(c2 + 1) * QT], self.s_st)
        self.dma("sync", self.xres[:, 0:QT], self.xsl[b, 0:128, tok0:tok0 + QT], self.s_ld)
        self.dma("sync", self.xres[:, QT:2 * QT], self.xsl[b, 128:256, tok0:tok0 + QT], self.s_ld)
        tld = self.s_ld.n
        for c2 in range(2):
            self.vop(lambda e, c2=c2: e.tensor_add(
                self.x1a[:, c2 * QT:(c2 + 1) * QT], self.x1a[:, c2 * QT:(c2 + 1) * QT],
                self.xres[:, c2 * QT:(c2 + 1) * QT]),
                waits=[(self.s_ld, tld)], fence=True)
        for c2 in range(2):
            self.dma("sync", self.x1p[c2 * 128:(c2 + 1) * 128, b * S + tok0: b * S + tok0 + QT],
                     self.x1a[:, c2 * QT:(c2 + 1) * QT], self.s_st,
                     waits=[(self.s_ve, self.s_ve.n)])


def build_l0_program():
    nc = bass.Bass("TRN2", target_bir_lowering=False)
    l0 = L0(nc)
    l0.build()
    l0.pr.finish(nc)
    l0.ctx.close()
    return nc


def l0_in_maps(inp):
    x = np.asarray(inp["L0_attn_x"], np.float32)
    Win = np.asarray(inp["L0_attn_W_in"], np.float32)
    b_in = np.asarray(inp["L0_attn_b_in"], np.float32)
    A_log = np.asarray(inp["L0_attn_A_log"], np.float32)
    B_param = np.asarray(inp["L0_attn_B_param"], np.float32)
    C_param = np.asarray(inp["L0_attn_C_param"], np.float32)
    conv_w = np.asarray(inp["L0_attn_conv_w"], np.float32)
    W_base = np.asarray(inp["L0_ffn_W_base"], np.float32)
    b_base = np.asarray(inp["L0_ffn_b_base"], np.float32)
    router_W = np.asarray(inp["L0_ffn_router_W"], np.float32)
    A_lora = np.asarray(inp["L0_ffn_A"], np.float32)
    Bw = np.asarray(inp["L0_ffn_Bw"], np.float32)

    xfm = np.ascontiguousarray(x.transpose(0, 2, 1))
    wbaseT = np.ascontiguousarray(W_base.T)
    wrT = np.ascontiguousarray(router_W.T)
    wloT = np.ascontiguousarray(A_lora.reshape(128, D).T)
    bwT = np.ascontiguousarray(Bw.transpose(0, 2, 1).reshape(128, D))
    ident = np.eye(128, dtype=np.float32)
    ones_row = np.ones((1, 128), np.float32)
    ones_col = np.ones((128, 1), np.float32)
    sel16 = np.zeros((8, 128), np.float32)
    for e in range(8):
        sel16[e, e * 16:(e + 1) * 16] = 1.0

    maps = []
    for c in range(8):
        dsl = slice(c * DSL, (c + 1) * DSL)
        cdn = np.zeros((16, 4096), np.float32)
        Csl = C_param[dsl]
        for c2 in range(2):
            for n in range(16):
                cdn[n, (c2 * 16 + n) * 128:(c2 * 16 + n + 1) * 128] = Csl[c2 * 128:(c2 + 1) * 128, n]
        alog = np.ascontiguousarray(
            A_log[dsl].reshape(2, 128, 16).transpose(1, 0, 2).reshape(128, 32))
        cw = np.ascontiguousarray(
            conv_w[dsl, 0, :].reshape(2, 128, 4).transpose(1, 0, 2).reshape(128, 8))
        bcol = lambda v: np.ascontiguousarray(np.asarray(v)[dsl].reshape(2, 128).T)
        maps.append({
            "xfm": xfm,
            "xsl": np.ascontiguousarray(xfm[:, dsl, :]),
            "wv": np.ascontiguousarray(Win[:, dsl]),
            "wxc": np.ascontiguousarray(Win[:, 2048 + c * DSL: 2048 + (c + 1) * DSL]),
            "wbt": np.ascontiguousarray(Win[:, 4096:4112]),
            "wdl": np.ascontiguousarray(Win[:, 4112].reshape(16, 128).T),
            "wbase": np.ascontiguousarray(wbaseT[:, dsl]),
            "wr": wrT,
            "wlo": wloT,
            "wbw": np.ascontiguousarray(bwT[:, dsl]),
            "cdn": cdn,
            "alog": alog,
            "cw": cw,
            "bv": bcol(b_in[:2048]),
            "bxc": bcol(b_in[2048:4096]),
            "bbase": bcol(b_base),
            "bbt": np.ascontiguousarray(b_in[4096:4112].reshape(16, 1)),
            "bpar": np.ascontiguousarray(B_param.reshape(16, 1)),
            "bdl": np.ascontiguousarray(b_in[4112:4113].reshape(1, 1)),
            "ident": ident,
            "ones_row": ones_row,
            "ones_col": ones_col,
            "sel16": sel16,
        })
    return maps


"""Layer 1 builders: L1B (projections / Q,K,V) and later L1C (attention + MoE + residual).

PSUM map (L1B): P8 = 7 fp32 banks [128, 3584]:
  banks 0-3: proj ping-pong slots (slot*1024, 2 banks each) + ln stats rows [0:1, 0:2048]
  banks 4-5: ln broadcasts [*, 2048:3072]
  bank 6:    rope rot matmuls [*, 3072:3584]
  PT: bank 7 as bf16 [128, 512] for transposes.
"""

QT = 1024
NH = 16
DH = 128
Q_PROJ = 1024
KV_PROJ = 1365
KV_KC = 11
KV_LAST = 85
K_NOPE = 1792
UKV_OUT = 3840
ISQ = 0.08838834764831845


class L1B:
    def __init__(self, nc):
        self.nc = nc
        self.ctx = ExitStack()
        self.pr = Prog()
        self._declare()
        self._alloc()
        self.last_wring = {}
        self.last_psum = {}
        self.stage_busy = {}
        self.ropet_busy = 0

    def _declare(self):
        nc = self.nc
        def din(name, shape, dt=F32):
            return nc.dram_tensor(name, list(shape), dt, kind="ExternalInput").ap()
        self.x1f = din("x1f", [D, QT])
        self.wdq_d = din("wdq", [D, Q_PROJ])
        self.wuq_d = din("wuq", [Q_PROJ, K_NOPE])
        self.wqr_d = din("wqr", [Q_PROJ, 256])
        self.wdkv_d = din("wdkv", [D, KV_PROJ])
        self.wukv_d = din("wukv", [KV_PROJ, UKV_OUT])
        self.wkr_d = din("wkr", [D, 256])
        self.qlw_d = din("qlw", [128, 8])
        self.qlb_d = din("qlb", [128, 8])
        self.kvlw_d = din("kvlw", [128, KV_KC])
        self.kvlb_d = din("kvlb", [128, KV_KC])
        self.cosq_d = din("cosq", [128, QT])
        self.sinq_d = din("sinq", [128, QT])
        self.prot_d = din("prot", [128, 128])
        self.ident_d = din("ident", [128, 128])
        self.ones_row_d = din("ones_row", [1, 128])
        self.ones_col_d = din("ones_col", [128, 1])
        dout = lambda name, shape: nc.dram_tensor(name, list(shape), BF16, kind="ExternalOutput").ap()
        self.qout = dout("qout", [2048, QT])
        self.kout = dout("kout", [2048, QT])
        self.vout = dout("vout", [QT, 2048])
        self.xnout = dout("xnout", [D, QT])

    def _alloc(self):
        nc, ctx = self.nc, self.ctx
        sbuf = lambda name, shape, dt=F32: ctx.enter_context(nc.sbuf_tensor(name, list(shape), dt))
        self.xb = sbuf("z_xb", [128, 16 * QT], BF16)
        self.sq = sbuf("z_sq", [128, 4 * QT], BF16)
        self.srow = sbuf("z_srow", [1, 3 * QT], F32)
        self.m_rep = sbuf("z_mrep", [128, QT], F32)
        self.r_rep = sbuf("z_rrep", [128, QT], F32)
        self.nrm = sbuf("z_nrm", [128, QT], BF16)
        self.cqn = sbuf("z_cqn", [128, 8 * QT], BF16)
        self.ckvn = sbuf("z_ckvn", [128, KV_KC * QT], BF16)
        self.wring = sbuf("z_wring", [128, 2 * 16 * 128], BF16)
        self.stage = sbuf("z_stage", [128, 2 * QT], BF16)
        self.qr_raw = sbuf("z_qrraw", [128, 2 * QT], BF16)
        self.rope_t = sbuf("z_ropet", [128, 2 * QT], BF16)
        self.cos_f = sbuf("z_cosf", [128, QT], F32)
        self.sin_f = sbuf("z_sinf", [128, QT], F32)
        self.prot_b = sbuf("z_protb", [128, 128], BF16)
        self.ident_b = sbuf("z_identb", [128, 128], BF16)
        self.ones_row_f = sbuf("z_onesrowf", [1, 128], F32)
        self.ones_col_b = sbuf("z_onescolb", [128, 1], BF16)
        self.qlw = sbuf("z_qlw", [128, 8], F32)
        self.qlb = sbuf("z_qlb", [128, 8], F32)
        self.kvlw = sbuf("z_kvlw", [128, KV_KC], F32)
        self.kvlb = sbuf("z_kvlb", [128, KV_KC], F32)
        self.P8 = ctx.enter_context(nc.psum_tensor("P8", [128, 7 * 512], F32))
        self.PT = ctx.enter_context(nc.psum_tensor("PT", [128, 512], BF16))
        ss = lambda name: Sem(nc, ctx, name)
        self.s_ld = ss("s_ld")
        self.s_w = ss("s_w")
        self.s_fr = ss("s_fr")
        self.s_mm = ss("s_mm")
        self.s_ve = ss("s_ve")
        self.s_se = ss("s_se")
        self.s_st = ss("s_st")

    def dma(self, eng, out_ap, in_ap, sem, waits=()):
        return self.pr.op(eng, lambda e, o=out_ap, i=in_ap: e.dma_start(out=o, in_=i),
                          waits=waits, inc=sem, inc_k=16)

    def vop(self, fn, waits=(), fence=False):
        w = list(waits)
        if fence and self.s_ve.n > 0:
            w.append((self.s_ve, self.s_ve.n))
        return self.pr.op("vector", fn, waits=w, inc=self.s_ve)

    def sop(self, fn, waits=(), fence=False):
        w = list(waits)
        if fence and self.s_se.n > 0:
            w.append((self.s_se, self.s_se.n))
        return self.pr.op("scalar", fn, waits=w, inc=self.s_se)

    def mm(self, fn, waits=(), inc=False, inc_sem=None):
        sem = inc_sem if inc_sem is not None else (self.s_mm if inc else None)
        return self.pr.op("tensor", fn, waits=waits, inc=sem)

    def ln_fm(self, src, nk, last_kw, dtot, wcol=None, bcol=None):
        s_mm, s_ve, s_se, s_fr = self.s_mm, self.s_ve, self.s_se, self.s_fr
        ready_v = s_ve.n
        sq_done = {}
        sqmm = {}
        for k in range(nk):
            kw = 128 if k < nk - 1 else last_kw
            w = [(s_ve, ready_v)] if k == 0 else []
            if k >= 4:
                w.append((s_fr, sqmm[k - 4]))
            sq_done[k] = self.sop(lambda e, k=k, kw=kw: e.activation(
                self.sq[0:kw, (k % 4) * QT:(k % 4) * QT + QT],
                src[0:kw, k * QT:k * QT + QT], AF.Square), waits=w, fence=(k == 0))
            for ts in range(2):
                wl = [(s_ve, ready_v)] if (k == 0 and ts == 0) else []
                self.mm(lambda e, k=k, ts=ts, kw=kw: e.matmul(
                    self.P8[0:1, ts * 512:(ts + 1) * 512], self.ones_col_b[0:kw, :],
                    src[0:kw, k * QT + ts * 512: k * QT + (ts + 1) * 512],
                    start=(k == 0), stop=(k == nk - 1)),
                    waits=wl, inc=(k == nk - 1 and ts == 1))
            for ts in range(2):
                wl2 = [(s_se, sq_done[k])] if ts == 0 else []
                if k == 0 and ts == 0:
                    wl2.append((s_ve, ready_v))
                self.mm(lambda e, k=k, ts=ts, kw=kw: e.matmul(
                    self.P8[0:1, 1024 + ts * 512: 1024 + (ts + 1) * 512], self.ones_col_b[0:kw, :],
                    self.sq[0:kw, (k % 4) * QT + ts * 512:(k % 4) * QT + (ts + 1) * 512],
                    start=(k == 0), stop=(k == nk - 1)),
                    waits=wl2,
                    inc_sem=(s_fr if ts == 1 else None))
            sqmm[k] = s_fr.n
        t_x, t_sq = s_mm.n, s_fr.n
        MROW, VROW, RROW = 0, 1, 2
        row = lambda i: self.srow[0:1, i * QT:(i + 1) * QT]
        f = self.vop(lambda e: e.tensor_scalar_mul(row(MROW), self.P8[0:1, 0:QT], 1.0 / dtot),
                     waits=[(s_mm, t_x), (s_fr, t_sq)], fence=True)
        f = self.vop(lambda e: e.tensor_scalar_mul(row(VROW), self.P8[0:1, QT:2 * QT], 1.0 / dtot))
        f = self.vop(lambda e: e.tensor_mul(row(RROW), row(MROW), row(MROW)), fence=True)
        f = self.vop(lambda e: e.tensor_sub(row(VROW), row(VROW), row(RROW)), fence=True)
        f = self.vop(lambda e: e.tensor_scalar_add(row(VROW), row(VROW), EPS), fence=True)
        fs = self.sop(lambda e: e.activation(row(RROW), row(VROW), AF.Sqrt),
                      waits=[(s_ve, f)], fence=True)
        f = self.vop(lambda e: e.reciprocal(row(RROW), row(RROW)), waits=[(s_se, fs)], fence=True)
        for ts in range(2):
            self.mm(lambda e, ts=ts: e.matmul(
                self.P8[:, 2048 + ts * 512: 2048 + (ts + 1) * 512], self.ones_row_f[:],
                self.srow[0:1, MROW * QT + ts * 512: MROW * QT + (ts + 1) * 512],
                start=True, stop=True),
                waits=[(s_ve, f)] if ts == 0 else [], inc=(ts == 1))
        tb = s_mm.n
        fm = self.vop(lambda e: e.tensor_copy(self.m_rep[:], self.P8[:, 2048:2048 + QT]),
                      waits=[(s_mm, tb)])
        for ts in range(2):
            self.mm(lambda e, ts=ts: e.matmul(
                self.P8[:, 2048 + ts * 512: 2048 + (ts + 1) * 512], self.ones_row_f[:],
                self.srow[0:1, RROW * QT + ts * 512: RROW * QT + (ts + 1) * 512],
                start=True, stop=True),
                waits=[(s_ve, fm)] if ts == 0 else [], inc=(ts == 1))
        tb = s_mm.n
        f = self.vop(lambda e: e.tensor_copy(self.r_rep[:], self.P8[:, 2048:2048 + QT]),
                     waits=[(s_mm, tb)])
        for k in range(nk):
            kw = 128 if k < nk - 1 else last_kw
            self.vop(lambda e, k=k, kw=kw: e.tensor_sub(
                self.nrm[0:kw, 0:QT], src[0:kw, k * QT:k * QT + QT], self.m_rep[0:kw, :]),
                fence=True)
            if wcol is None:
                f = self.vop(lambda e, k=k, kw=kw: e.tensor_mul(
                    src[0:kw, k * QT:k * QT + QT], self.nrm[0:kw, 0:QT], self.r_rep[0:kw, :]),
                    fence=True)
            else:
                self.vop(lambda e, k=k, kw=kw: e.tensor_mul(
                    self.nrm[0:kw, 0:QT], self.nrm[0:kw, 0:QT], self.r_rep[0:kw, :]),
                    fence=True)
                f = self.vop(lambda e, k=k, kw=kw, wcol=wcol, bcol=bcol: e.tensor_scalar(
                    src[0:kw, k * QT:k * QT + QT], self.nrm[0:kw, 0:QT],
                    wcol[0:kw, k:k + 1], bcol[0:kw, k:k + 1], ALU.mult, ALU.add),
                    fence=True)
        return f

    def proj(self, w_dram, kdim, mdim, rhs, evac, rhs_ready):
        s_mm, s_w = self.s_mm, self.s_w
        nk = (kdim + 127) // 128
        nm = (mdim + 127) // 128
        for mc in range(nm):
            mw = min(128, mdim - mc * 128)
            slot = mc % 2
            w = []
            if slot in self.last_wring:
                w.append((s_mm, self.last_wring[slot]))
            nkf = kdim // 128
            wt = self.dma(
                "gpsimd",
                self.wring[:, slot * 2048: slot * 2048 + nkf * 128].rearrange(
                    "p (k m) -> p k m", m=128)[:, :, 0:mw],
                w_dram[0:nkf * 128, mc * 128:mc * 128 + mw].rearrange(
                    "(k p) m -> p k m", p=128),
                s_w, waits=w)
            if kdim % 128:
                kw = kdim % 128
                wt = self.dma("gpsimd",
                              self.wring[0:kw, (slot * 16 + nkf) * 128:(slot * 16 + nkf) * 128 + mw],
                              w_dram[nkf * 128:kdim, mc * 128:mc * 128 + mw],
                              s_w, waits=[])
            for k in range(nk):
                kw = min(128, kdim - k * 128)
                rap = rhs(k)
                for ts in range(2):
                    wl = []
                    if k == 0 and ts == 0:
                        wl.append((s_w, wt))
                        if slot in self.last_psum:
                            wl.append(self.last_psum[slot])
                        if mc == 0 and rhs_ready is not None:
                            wl.append(rhs_ready)
                    self.mm(lambda e, k=k, kw=kw, ts=ts, slot=slot, mw=mw, rap=rap: e.matmul(
                        self.P8[0:mw, slot * 1024 + ts * 512: slot * 1024 + (ts + 1) * 512],
                        self.wring[0:kw, (slot * 16 + k) * 128:(slot * 16 + k) * 128 + mw],
                        rap[:, ts * 512:(ts + 1) * 512],
                        start=(k == 0), stop=(k == nk - 1)),
                        waits=wl, inc=(k == nk - 1 and ts == 1))
            t_mm = s_mm.n
            self.last_wring[slot] = t_mm
            self.last_psum[slot] = evac(mc, mw, t_mm, slot)

    def evac_store(self, out_dram, row0, mc, mw, t_mm, slot):
        sslot = mc % 2
        w = [(self.s_mm, t_mm)]
        if sslot in self.stage_busy:
            w.append((self.s_st, self.stage_busy[sslot]))
        f = self.vop(lambda e, sslot=sslot, slot=slot, mw=mw: e.tensor_copy(
            self.stage[0:mw, sslot * QT: sslot * QT + QT],
            self.P8[0:mw, slot * 1024: slot * 1024 + QT]), waits=w, fence=True)
        st = self.dma("sync", out_dram[row0 + mc * 128: row0 + mc * 128 + mw, :],
                      self.stage[0:mw, sslot * QT: sslot * QT + QT], self.s_st,
                      waits=[(self.s_ve, f)])
        self.stage_busy[sslot] = st
        return (self.s_ve, f)

    def build(self):
        pr = self.pr
        s_ld, s_mm, s_ve, s_se, s_st = self.s_ld, self.s_mm, self.s_ve, self.s_se, self.s_st
        for k in range(16):
            self.dma("gpsimd", self.xb[:, k * QT:(k + 1) * QT],
                     self.x1f[k * 128:(k + 1) * 128, :], s_ld)
        for dst, src, cast in (
            (self.cos_f, self.cosq_d, False), (self.sin_f, self.sinq_d, False),
            (self.prot_b, self.prot_d, True), (self.ident_b, self.ident_d, True),
            (self.ones_row_f, self.ones_row_d, False), (self.ones_col_b, self.ones_col_d, True),
            (self.qlw, self.qlw_d, False), (self.qlb, self.qlb_d, False),
            (self.kvlw, self.kvlw_d, False), (self.kvlb, self.kvlb_d, False),
        ):
            self.dma("gpsimd" if cast else "sync", dst[:], src, s_ld)
        pr.barrier([s_ld])

        t_xn = self.ln_fm(self.xb, 16, 128, D)
        for k in range(16):
            self.dma("sync", self.xnout[k * 128:(k + 1) * 128, :], self.xb[:, k * QT:(k + 1) * QT],
                     s_st, waits=[(s_ve, t_xn)] if k == 0 else [])

        rhs_x = lambda k: self.xb[:, k * QT:(k + 1) * QT]
        rhs_cq = lambda k: self.cqn[:, k * QT:(k + 1) * QT]
        def rhs_ckv(k):
            kw = 128 if k < KV_KC - 1 else KV_LAST
            return self.ckvn[0:kw, k * QT:k * QT + QT]

        def evac_cq(mc, mw, t_mm, slot):
            f = self.vop(lambda e, mc=mc, mw=mw, slot=slot: e.tensor_copy(
                self.cqn[0:mw, mc * QT:mc * QT + QT],
                self.P8[0:mw, slot * 1024: slot * 1024 + QT]),
                waits=[(s_mm, t_mm)], fence=True)
            return (s_ve, f)
        self.proj(self.wdq_d, D, Q_PROJ, rhs_x, evac_cq, (s_ve, t_xn))
        t_cq = self.ln_fm(self.cqn, 8, 128, Q_PROJ, wcol=self.qlw, bcol=self.qlb)

        def evac_ckv(mc, mw, t_mm, slot):
            f = self.vop(lambda e, mc=mc, mw=mw, slot=slot: e.tensor_copy(
                self.ckvn[0:mw, mc * QT:mc * QT + QT],
                self.P8[0:mw, slot * 1024: slot * 1024 + QT]),
                waits=[(s_mm, t_mm)], fence=True)
            return (s_ve, f)
        self.proj(self.wdkv_d, D, KV_PROJ, rhs_x, evac_ckv, (s_ve, t_cq))
        t_ckv = self.ln_fm(self.ckvn, KV_KC, KV_LAST, KV_PROJ, wcol=self.kvlw, bcol=self.kvlb)

        self.proj(self.wuq_d, Q_PROJ, K_NOPE, rhs_cq,
                  lambda mc, mw, t, sl: self.evac_store(self.qout, 0, mc, mw, t, sl),
                  (s_ve, t_cq))

        def evac_qr(mc, mw, t_mm, slot):
            f = self.vop(lambda e, mc=mc, slot=slot: e.tensor_copy(
                self.qr_raw[:, mc * QT:(mc + 1) * QT],
                self.P8[0:128, slot * 1024: slot * 1024 + QT]),
                waits=[(self.s_mm, t_mm), (self.s_st, self.ropet_busy)], fence=True)
            return (s_ve, f)
        self.proj(self.wqr_d, Q_PROJ, 256, rhs_cq, evac_qr, None)
        self._rope(self.qout, 1792)

        self.proj(self.wkr_d, D, 256, rhs_x, evac_qr, None)
        self._rope(self.kout, 1792)

        def evac_kv(mc, mw, t_mm, slot):
            if mc < 14:
                return self.evac_store(self.kout, 0, mc, mw, t_mm, slot)
            f = self.vop(lambda e, mc=mc, slot=slot: e.tensor_copy(
                self.stage[:, (mc % 2) * QT:(mc % 2) * QT + QT],
                self.P8[0:128, slot * 1024: slot * 1024 + QT]),
                waits=[(s_mm, t_mm)] + ([(s_st, self.stage_busy[mc % 2])] if (mc % 2) in self.stage_busy else []),
                fence=True)
            fv = None
            for tq in range(8):
                wl = [(s_ve, f if tq == 0 else fv)]
                tt = self.mm(lambda e, mc=mc, tq=tq: e.transpose(
                    self.PT[:, 0:128],
                    self.stage[:, (mc % 2) * QT + tq * 128:(mc % 2) * QT + (tq + 1) * 128],
                    self.ident_b[:]), waits=wl, inc=True)
                fv = self.vop(lambda e, tq=tq: e.tensor_copy(
                    self.rope_t[:, tq * 128:(tq + 1) * 128], self.PT[:, 0:128]),
                    waits=[(s_mm, tt), (s_st, self.ropet_busy)] if tq == 0 else [(s_mm, tt)],
                    fence=True)
            vc = mc - 14
            st = self.dma("sync",
                          self.vout[:, vc * 128:(vc + 1) * 128].rearrange("(tq p) j -> p tq j", p=128),
                          self.rope_t[:, 0:QT].rearrange("p (tq j) -> p tq j", j=128),
                          s_st, waits=[(s_ve, fv)])
            self.ropet_busy = st
            self.stage_busy[mc % 2] = st
            return (s_ve, fv)
        self.proj(self.wukv_d, KV_PROJ, UKV_OUT, rhs_ckv, evac_kv, (s_ve, t_ckv))
        pr.wait("gpsimd", s_st)
        pr.wait("sync", s_st)

    def _rope(self, out_dram, row0):
        s_mm, s_ve = self.s_mm, self.s_ve
        for hh in range(2):
            fprev = None
            for half in range(2):
                tt = self.mm(lambda e, hh=hh, half=half: e.matmul(
                    self.P8[:, 3072:3584], self.prot_b[:],
                    self.qr_raw[:, hh * QT + half * 512: hh * QT + (half + 1) * 512],
                    start=True, stop=True),
                    waits=[(s_ve, self.s_ve.n)], inc=True)
                f = self.vop(lambda e, hh=hh, half=half: e.tensor_mul(
                    self.rope_t[:, half * 512:(half + 1) * 512],
                    self.qr_raw[:, hh * QT + half * 512: hh * QT + (half + 1) * 512],
                    self.cos_f[:, half * 512:(half + 1) * 512]),
                    waits=[(s_mm, tt), (self.s_st, self.ropet_busy)], fence=True)
                f = self.vop(lambda e, half=half: e.tensor_mul(
                    self.rope_t[:, QT + half * 512: QT + (half + 1) * 512],
                    self.P8[:, 3072:3584], self.sin_f[:, half * 512:(half + 1) * 512]),
                    fence=True)
                f = self.vop(lambda e, half=half: e.tensor_add(
                    self.rope_t[:, half * 512:(half + 1) * 512],
                    self.rope_t[:, half * 512:(half + 1) * 512],
                    self.rope_t[:, QT + half * 512: QT + (half + 1) * 512]), fence=True)
                fprev = f
            st = self.dma("sync", out_dram[row0 + hh * 128: row0 + (hh + 1) * 128, :],
                          self.rope_t[:, 0:QT], self.s_st, waits=[(s_ve, fprev)])
            self.ropet_busy = st


def build_l1b_program():
    nc = bass.Bass("TRN2", target_bir_lowering=False)
    b = L1B(nc)
    b.build()
    b.pr.finish(nc)
    b.ctx.close()
    return nc


def rope_tables():
    freqs = 1.0 / 10000.0 ** (np.arange(0, DH, 2) / DH)
    pos = np.arange(S)
    ang = pos[:, None] * freqs[None, :]
    cos = np.concatenate([np.cos(ang), np.cos(ang)], -1)
    sin = np.concatenate([np.sin(ang), np.sin(ang)], -1)
    return cos.astype(np.float32), sin.astype(np.float32)


def own_token_cols(core):
    lane = core % 4
    idx = []
    for j in range(8):
        p = 128 * (lane + 4 * j)
        idx.extend(range(p, p + 128))
    return np.array(idx)


def l1b_in_maps(inp, x1_fm_full):
    wdq = np.ascontiguousarray(inp["L1_attn_W_dq"].astype(np.float32))
    wuq = np.ascontiguousarray(inp["L1_attn_W_uq"].astype(np.float32))
    wqr = np.ascontiguousarray(inp["L1_attn_W_qr"].astype(np.float32))
    wdkv = np.ascontiguousarray(inp["L1_attn_W_dkv"].astype(np.float32))
    wukv = np.ascontiguousarray(inp["L1_attn_W_ukv"].astype(np.float32))
    wkr = np.ascontiguousarray(inp["L1_attn_W_kr"].astype(np.float32))
    pad = lambda v, n: np.concatenate([np.asarray(v, np.float32), np.zeros(n - len(v), np.float32)])
    qlw = np.ascontiguousarray(inp["L1_attn_q_ln_w"].astype(np.float32).reshape(8, 128).T)
    qlb = np.ascontiguousarray(inp["L1_attn_q_ln_b"].astype(np.float32).reshape(8, 128).T)
    kvlw = np.ascontiguousarray(pad(inp["L1_attn_kv_ln_w"], KV_KC * 128).reshape(KV_KC, 128).T)
    kvlb = np.ascontiguousarray(pad(inp["L1_attn_kv_ln_b"], KV_KC * 128).reshape(KV_KC, 128).T)
    cos, sin = rope_tables()
    prot = np.zeros((128, 128), np.float32)
    for m in range(64):
        prot[m + 64, m] = -1.0
        prot[m, m + 64] = 1.0
    ident = np.eye(128, dtype=np.float32)
    ones_row = np.ones((1, 128), np.float32)
    ones_col = np.ones((128, 1), np.float32)

    maps = []
    for c in range(8):
        bc = c // 4
        cols = own_token_cols(c)
        maps.append({
            "x1f": np.ascontiguousarray(x1_fm_full[bc][:, cols]),
            "wdq": wdq, "wuq": wuq, "wqr": wqr, "wdkv": wdkv, "wukv": wukv, "wkr": wkr,
            "qlw": qlw, "qlb": qlb, "kvlw": kvlw, "kvlb": kvlb,
            "cosq": np.ascontiguousarray(cos[cols].T),
            "sinq": np.ascontiguousarray(sin[cols].T),
            "prot": prot, "ident": ident, "ones_row": ones_row, "ones_col": ones_col,
        })
    return maps


class L1C1:
    NHC = 4
    QG = 512

    def __init__(self, nc):
        self.nc = nc
        self.ctx = ExitStack()
        self.pr = Prog()
        nc_ = nc
        def din(name, shape, dt=BF16):
            return nc_.dram_tensor(name, list(shape), dt, kind="ExternalInput").ap()
        self.qd = din("qallc", [512, S])
        self.kd = din("kallc", [512, S])
        self.vd = din("vallc", [S, 512])
        self.dmask_d = din("dmask", [128, 4 * 512])
        self.ones_row_d = din("ones_row", [1, 128], F32)
        self.ones_col_d = din("ones_col", [128, 1], F32)
        self.aout = nc_.dram_tensor("aout", [512, S], BF16, kind="ExternalOutput").ap()
        ctx = self.ctx
        sbuf = lambda name, shape, dt=BF16: ctx.enter_context(nc_.sbuf_tensor(name, list(shape), dt))
        self.qt = sbuf("z_qt", [128, 2 * 512])
        self.kt = sbuf("z_kt", [128, 2 * 512])
        self.vt = sbuf("z_vt", [128, 2 * 512])
        self.pt = sbuf("z_pt", [128, 2 * 512])
        self.dmask = sbuf("z_dmask", [128, 4 * 512])
        self.lrow = sbuf("z_lrow", [1, 512], F32)
        self.lrep = sbuf("z_lrep", [128, 512], F32)
        self.stage = sbuf("z_stage", [128, 2 * 512])
        self.ones_row_f = sbuf("z_onesrowf", [1, 128], F32)
        self.ones_col_b = sbuf("z_onescolb", [128, 1])
        self.P8 = ctx.enter_context(nc_.psum_tensor("P8", [128, 8 * 512], F32))
        ss = lambda name: Sem(nc_, ctx, name)
        self.s_ld = ss("s_ld")
        self.s_q = ss("s_q")
        self.s_kv = ss("s_kv")
        self.s_mm = ss("s_mm")
        self.s_ve = ss("s_ve")
        self.s_se = ss("s_se")
        self.s_st = ss("s_st")

    def dma(self, eng, out_ap, in_ap, sem, waits=()):
        return self.pr.op(eng, lambda e, o=out_ap, i=in_ap: e.dma_start(out=o, in_=i),
                          waits=waits, inc=sem, inc_k=16)

    def vop(self, fn, waits=(), fence=False):
        w = list(waits)
        if fence and self.s_ve.n > 0:
            w.append((self.s_ve, self.s_ve.n))
        return self.pr.op("vector", fn, waits=w, inc=self.s_ve)

    def sop(self, fn, waits=(), fence=False):
        w = list(waits)
        if fence and self.s_se.n > 0:
            w.append((self.s_se, self.s_se.n))
        return self.pr.op("scalar", fn, waits=w, inc=self.s_se)

    def mm(self, fn, waits=(), inc=False):
        return self.pr.op("tensor", fn, waits=waits, inc=self.s_mm if inc else None)

    def build(self):
        pr = self.pr
        s_q, s_kv, s_mm, s_ve, s_se, s_st = (self.s_q, self.s_kv, self.s_mm,
                                             self.s_ve, self.s_se, self.s_st)
        self.dma("sync", self.dmask[:], self.dmask_d, self.s_ld)
        self.dma("sync", self.ones_row_f[:], self.ones_row_d, self.s_ld)
        self.dma("gpsimd", self.ones_col_b[:], self.ones_col_d, self.s_ld)
        pr.barrier([self.s_ld])
        q_cons = {}
        kv_cons = {}
        kv_loaded = {}
        s_cons = {}
        p_cons = {}
        acc_free = 0
        lsum_free = 0
        lrep_free = 0
        stage_busy = {}
        units = [(h, g) for h in range(self.NHC) for g in range(8)]
        for ui, (h, g) in enumerate(units):
            qslot = ui % 2
            wq = [(s_mm, q_cons[qslot])] if qslot in q_cons else []
            tq = self.dma("gpsimd", self.qt[:, qslot * 512:(qslot + 1) * 512],
                          self.qd[h * 128:(h + 1) * 128, g * 512:(g + 1) * 512], s_q,
                          waits=wq)
            nkc = 4 * (g + 1)
            tls = None

            def load_group(kg, waits):
                gs_ = kg % 2
                self.dma("sync", self.kt[:, gs_ * 512:(gs_ + 1) * 512],
                         self.kd[h * 128:(h + 1) * 128, kg * 512:(kg + 1) * 512],
                         s_kv, waits=waits)
                kv_loaded[kg] = self.dma(
                    "sync",
                    self.vt[:, gs_ * 512:(gs_ + 1) * 512].rearrange("p (k c) -> p k c", k=4),
                    self.vd[kg * 512:(kg + 1) * 512, h * 128:(h + 1) * 128].rearrange(
                        "(k p) c -> p k c", p=128),
                    s_kv)

            for kg in range(min(2, g + 1)):
                gs_ = kg % 2
                load_group(kg, [(s_mm, kv_cons[gs_])] if gs_ in kv_cons else [])
            for kc in range(nkc):
                gslot = (kc // 4) % 2
                koff = gslot * 512 + (kc % 4) * 128
                tkv = kv_loaded[kc // 4]
                sslot = kc % 2
                wl = [(s_kv, tkv), (s_q, tq)]
                if sslot in s_cons:
                    wl.append((s_se, s_cons[sslot]))
                tsm = self.mm(lambda e, sslot=sslot, koff=koff, qslot=qslot: e.matmul(
                    self.P8[:, sslot * 512:(sslot + 1) * 512],
                    self.kt[:, koff:koff + 128],
                    self.qt[:, qslot * 512:(qslot + 1) * 512],
                    start=True, stop=True), waits=wl, inc=True)
                wexp = [(s_mm, tsm)]
                if sslot in p_cons:
                    wexp.append((s_mm, p_cons[sslot][0]))
                    if p_cons[sslot][1]:
                        wexp.append((s_ve, p_cons[sslot][1]))
                fe = self.sop(lambda e, sslot=sslot: e.activation(
                    self.pt[:, sslot * 512:(sslot + 1) * 512],
                    self.P8[:, sslot * 512:(sslot + 1) * 512],
                    AF.Exp, bias=0.0, scale=ISQ), waits=wexp)
                s_cons[sslot] = fe
                fmask = 0
                if kc >= 4 * g:
                    dm = kc - 4 * g
                    fmask = self.vop(lambda e, sslot=sslot, dm=dm: e.tensor_mul(
                        self.pt[:, sslot * 512:(sslot + 1) * 512],
                        self.pt[:, sslot * 512:(sslot + 1) * 512],
                        self.dmask[:, dm * 512:(dm + 1) * 512]),
                        waits=[(s_se, fe)], fence=True)
                wav = [(s_se, fe)]
                if fmask:
                    wav.append((s_ve, fmask))
                if kc == 0:
                    if acc_free:
                        wav.append((s_ve, acc_free))
                    if lsum_free:
                        wav.append((s_ve, lsum_free))
                self.mm(lambda e, sslot=sslot, koff=koff, kc=kc, nkc=nkc: e.matmul(
                    self.P8[:, 2 * 512:3 * 512],
                    self.vt[:, koff:koff + 128],
                    self.pt[:, sslot * 512:(sslot + 1) * 512],
                    start=(kc == 0), stop=(kc == nkc - 1)), waits=wav)
                tls = self.mm(lambda e, sslot=sslot, kc=kc, nkc=nkc: e.matmul(
                    self.P8[0:1, 3 * 512:3 * 512 + 512],
                    self.ones_col_b[:],
                    self.pt[:, sslot * 512:(sslot + 1) * 512],
                    start=(kc == 0), stop=(kc == nkc - 1)), inc=True)
                p_cons[sslot] = (tls, fmask)
                kv_cons[gslot] = tls
                if kc % 4 == 3 and kc // 4 + 2 <= g:
                    load_group(kc // 4 + 2, [(s_mm, tls)])
            q_cons[qslot] = tls
            f1 = self.vop(lambda e: e.tensor_copy(self.lrow[:], self.P8[0:1, 3 * 512:4 * 512]),
                          waits=[(s_mm, tls)], fence=True)
            lsum_free = f1
            f2 = self.vop(lambda e: e.reciprocal(self.lrow[:], self.lrow[:]), fence=True)
            wtb = [(s_ve, f2)]
            if lrep_free:
                wtb.append((s_ve, lrep_free))
            tb = self.mm(lambda e: e.matmul(
                self.P8[:, 4 * 512:5 * 512], self.ones_row_f[:], self.lrow[:],
                start=True, stop=True), waits=wtb, inc=True)
            f3 = self.vop(lambda e: e.tensor_copy(self.lrep[:], self.P8[:, 4 * 512:5 * 512]),
                          waits=[(s_mm, tb)], fence=True)
            lrep_free = f3
            sslot2 = ui % 2
            wst = [(self.s_st, stage_busy[sslot2])] if sslot2 in stage_busy else []
            f4 = self.vop(lambda e, sslot2=sslot2: e.tensor_mul(
                self.stage[:, sslot2 * 512:(sslot2 + 1) * 512],
                self.P8[:, 2 * 512:3 * 512], self.lrep[:]),
                waits=wst, fence=True)
            acc_free = f4
            st = self.dma("sync", self.aout[h * 128:(h + 1) * 128, g * 512:(g + 1) * 512],
                          self.stage[:, sslot2 * 512:(sslot2 + 1) * 512], s_st,
                          waits=[(s_ve, f4)])
            stage_busy[sslot2] = st
        pr.wait("gpsimd", s_st)
        pr.wait("sync", s_st)


def build_l1c1_program():
    nc = bass.Bass("TRN2", target_bir_lowering=False)
    c = L1C1(nc)
    c.build()
    c.pr.finish(nc)
    c.ctx.close()
    return nc


def l1c1_in_maps(q_fm, k_fm, v_tm):
    import ml_dtypes
    dmask = np.zeros((128, 4 * 512), np.float32)
    for kc in range(4):
        for jt in range(4):
            for kr in range(128):
                lo = kc * 128 + kr - jt * 128
                if lo <= 0:
                    dmask[kr, kc * 512 + jt * 128: kc * 512 + (jt + 1) * 128] = 1.0
                elif lo < 128:
                    dmask[kr, kc * 512 + jt * 128 + lo: kc * 512 + (jt + 1) * 128] = 1.0
    dmask = dmask.astype(ml_dtypes.bfloat16)
    ones_row = np.ones((1, 128), np.float32)
    ones_col = np.ones((128, 1), np.float32)
    maps = []
    for c in range(8):
        bc = c // 4
        h0 = (c % 4) * 4
        maps.append({
            "qallc": np.ascontiguousarray(q_fm[bc][h0 * 128:(h0 + 4) * 128, :]),
            "kallc": np.ascontiguousarray(k_fm[bc][h0 * 128:(h0 + 4) * 128, :]),
            "vallc": np.ascontiguousarray(v_tm[bc][:, h0 * 128:(h0 + 4) * 128]),
            "dmask": dmask, "ones_row": ones_row, "ones_col": ones_col,
        })
    return maps


NEXP = 32
DMOE = 768
DSH = 4096


class WRing:

    def __init__(self, outer, tensor, nslots, stride=2048):
        self.o = outer
        self.t = tensor
        self.n = nslots
        self.stride = stride
        self.rr = 0
        self.busy = {}

    def load(self, w_dram, k0, nkk, mc, mw, extra=()):
        slot = self.rr
        self.rr = (self.rr + 1) % self.n
        w = list(extra)
        if slot in self.busy:
            w.append((self.o.s_mm, self.busy[slot]))
        tv = self.o.dma(
            "gpsimd",
            self.t[:, slot * self.stride: slot * self.stride + nkk * mw].rearrange("p (k m) -> p k m", k=nkk),
            w_dram[k0 * 128:(k0 + nkk) * 128, mc * 128:mc * 128 + mw].rearrange(
                "(k p) m -> p k m", p=128),
            self.o.s_w, waits=w)
        return slot, tv

    def tile(self, slot, k, mw):
        return self.t[0:128, slot * self.stride + k * mw: slot * self.stride + (k + 1) * mw]

    def used(self, slot, t_mm):
        self.busy[slot] = t_mm


class L1C2:
    def __init__(self, nc):
        self.nc = nc
        self.ctx = ExitStack()
        self.pr = Prog()
        self._declare()
        self._alloc()

    def _declare(self):
        nc = self.nc
        def din(name, shape, dt=F32):
            return nc.dram_tensor(name, list(shape), dt, kind="ExternalInput").ap()
        self.attn_d = din("attnTc", [2048, QT], BF16)
        self.xn1_d = din("xn1c", [D, QT], BF16)
        self.x1_d = din("x1own", [D, QT])
        self.wo_d = din("wo", [2048, 2048])
        self.wgate_d = din("wgate", [D, NEXP])
        self.wsg_d = din("wsg", [D, DSH])
        self.wsu_d = din("wsu", [D, DSH])
        self.wsd_d = din("wsd", [DSH, D])
        self.wseg_d = din("wseg", [D, 128])
        self.gupt_d = din("gupt", [NEXP, D, 2 * DMOE])
        self.gdnt_d = din("gdnt", [NEXP, DMOE, D])
        self.e32_d = din("e32sel", [32, 32 * 128])
        self.ident_d = din("ident", [128, 128])
        self.ones_row_d = din("ones_row", [1, 128])
        self.ones_col_d = din("ones_col", [128, 1])
        self.outp = nc.dram_tensor("outp", [QT, D], F32, kind="ExternalOutput").ap()
        self.dbg = {}
        if getattr(L1C2, "DEBUG", False):
            for name in ("dacc_wo", "dacc_sh", "dacc_ex"):
                self.dbg[name] = nc.dram_tensor(name, [D, QT], F32, kind="ExternalOutput").ap()
            self.dbg["dcomb"] = nc.dram_tensor("dcomb", [32, QT], F32, kind="ExternalOutput").ap()

    def _alloc(self):
        nc, ctx = self.nc, self.ctx
        sbuf = lambda name, shape, dt=F32: ctx.enter_context(nc.sbuf_tensor(name, list(shape), dt))
        self.xn1 = sbuf("z_xn1", [128, 16 * QT], BF16)
        self.acc = sbuf("z_acc", [128, 16 * QT], F32)
        self.uni = sbuf("z_uni", [128, 16 * QT], BF16)
        self.wra = sbuf("z_wra", [128, 2 * 16 * 128], BF16)
        self.wrb = sbuf("z_wrb", [128, 2 * 16 * 128], BF16)
        self.wrd = sbuf("z_wrd", [128, 2 * 6 * 128], BF16)
        self.crep = sbuf("z_crep", [128, 2 * QT], F32)
        self.grep = sbuf("z_grep", [128, QT], F32)
        self.stage = sbuf("z_stage", [128, 2048], F32)
        self.x1r = sbuf("z_x1r", [128, 2 * QT], F32)
        self.lgs = sbuf("z_lgs", [32, QT], F32)
        self.comb_fm = sbuf("z_combfm", [32, QT], BF16)
        self.cwrk = sbuf("z_cwrk", [128, 4 * 32], F32)
        self.ccol = sbuf("z_ccol", [128, 4], F32)
        self.grow = sbuf("z_grow", [1, QT], F32)
        self.e32 = sbuf("z_e32", [32, 32 * 128], BF16)
        self.ident_f = sbuf("z_identf", [128, 128], F32)
        self.ones_row_f = sbuf("z_onesrowf", [1, 128], F32)
        self.ones_col_b = sbuf("z_onescolb", [128, 1], BF16)
        self.P8 = ctx.enter_context(nc.psum_tensor("P8", [128, 8 * 512], F32))
        ss = lambda name: Sem(nc, ctx, name)
        self.s_ld = ss("s_ld")
        self.s_w = ss("s_w")
        self.s_mm = ss("s_mm")
        self.s_ve = ss("s_ve")
        self.s_se = ss("s_se")
        self.s_st = ss("s_st")

    def dma(self, eng, out_ap, in_ap, sem, waits=()):
        return self.pr.op(eng, lambda e, o=out_ap, i=in_ap: e.dma_start(out=o, in_=i),
                          waits=waits, inc=sem, inc_k=16)

    def vop(self, fn, waits=(), fence=False):
        w = list(waits)
        if fence and self.s_ve.n > 0:
            w.append((self.s_ve, self.s_ve.n))
        return self.pr.op("vector", fn, waits=w, inc=self.s_ve)

    def sop(self, fn, waits=(), fence=False):
        w = list(waits)
        if fence and self.s_se.n > 0:
            w.append((self.s_se, self.s_se.n))
        return self.pr.op("scalar", fn, waits=w, inc=self.s_se)

    def mm(self, fn, waits=(), inc=False):
        return self.pr.op("tensor", fn, waits=waits, inc=self.s_mm if inc else None)

    def build(self):
        pr = self.pr
        s_mm, s_ve, s_se, s_st, s_w = self.s_mm, self.s_ve, self.s_se, self.s_st, self.s_w
        ra = WRing(self, self.wra, 2)
        rb = WRing(self, self.wrb, 2)
        for k in range(16):
            self.dma("gpsimd", self.xn1[:, k * QT:(k + 1) * QT],
                     self.xn1_d[k * 128:(k + 1) * 128, :], self.s_ld)
            self.dma("gpsimd", self.uni[:, k * QT:(k + 1) * QT],
                     self.attn_d[k * 128:(k + 1) * 128, :], self.s_ld)
        for dst, src, cast in ((self.e32, self.e32_d, True), (self.ident_f, self.ident_d, False),
                               (self.ones_row_f, self.ones_row_d, False),
                               (self.ones_col_b, self.ones_col_d, True)):
            self.dma("gpsimd" if cast else "sync", dst[:], src, self.s_ld)
        pr.barrier([self.s_ld])

        psum_free = {}
        for mc in range(16):
            slot = mc % 2
            wsl, wt = ra.load(self.wo_d, 0, 16, mc, 128)
            for k in range(16):
                for ts in range(2):
                    wl = []
                    if k == 0 and ts == 0:
                        wl.append((s_w, wt))
                        if slot in psum_free:
                            wl.append((s_ve, psum_free[slot]))
                    self.mm(lambda e, k=k, ts=ts, slot=slot, wsl=wsl: e.matmul(
                        self.P8[:, slot * 1024 + ts * 512: slot * 1024 + (ts + 1) * 512],
                        ra.tile(wsl, k, 128),
                        self.uni[:, k * QT + ts * 512: k * QT + (ts + 1) * 512],
                        start=(k == 0), stop=(k == 15)), waits=wl, inc=(k == 15 and ts == 1))
            t_mm = s_mm.n
            ra.used(wsl, t_mm)
            f = self.vop(lambda e, mc=mc, slot=slot: e.tensor_copy(
                self.acc[:, mc * QT:(mc + 1) * QT],
                self.P8[0:128, slot * 1024: slot * 1024 + QT]),
                waits=[(s_mm, t_mm)], fence=True)
            psum_free[slot] = f
        t_wo_done = s_ve.n
        if self.dbg:
            for kk in range(16):
                self.dma("sync", self.dbg["dacc_wo"][kk * 128:(kk + 1) * 128, :],
                         self.acc[:, kk * QT:(kk + 1) * QT], self.s_st,
                         waits=[(s_ve, t_wo_done)] if kk == 0 else [])

        wsl, wgt = ra.load(self.wgate_d, 0, 16, 0, 32)
        for k in range(16):
            for ts in range(2):
                wl = []
                if k == 0 and ts == 0:
                    wl = [(s_w, wgt), (s_ve, t_wo_done)]
                self.mm(lambda e, k=k, ts=ts, wsl=wsl: e.matmul(
                    self.P8[0:32, 2048 + ts * 512: 2048 + (ts + 1) * 512],
                    ra.tile(wsl, k, 32),
                    self.xn1[:, k * QT + ts * 512: k * QT + (ts + 1) * 512],
                    start=(k == 0), stop=(k == 15)), waits=wl, inc=(k == 15 and ts == 1))
        t_lg = s_mm.n
        ra.used(wsl, t_lg)
        f = self.vop(lambda e: e.tensor_copy(self.lgs[:], self.P8[0:32, 2048:2048 + QT]),
                     waits=[(s_mm, t_lg)], fence=True)
        MXC, A4C, SC, DENC = 0, 1, 2, 3
        col = lambda i: self.ccol[:, i:i + 1]
        PW, WW, SEL, INV = 0, 1, 2, 3
        wrk = lambda i: self.cwrk[:, i * 32:(i + 1) * 32]
        for tc in range(8):
            tt = self.mm(lambda e, tc=tc: e.transpose(
                self.P8[:, 3072:3104], self.lgs[:, tc * 128:(tc + 1) * 128],
                self.ident_f[0:32, 0:32]), waits=[(s_ve, s_ve.n)], inc=True)
            f = self.vop(lambda e: e.reduce_max(col(MXC), self.P8[:, 3072:3104], axis=AX.X),
                         waits=[(s_mm, tt)], fence=True)
            f = self.vop(lambda e: e.tensor_scalar_mul(col(MXC), col(MXC), -1.0), fence=True)
            fs = self.sop(lambda e: e.activation(wrk(PW), self.P8[:, 3072:3104], AF.Exp,
                                                 bias=col(MXC), accum_out=col(SC)),
                          waits=[(s_ve, f)], fence=True)
            f = self.vop(lambda e: e.tensor_copy(wrk(WW), wrk(PW)), waits=[(s_se, fs)], fence=True)
            f = self.vop(lambda e: e.memset(wrk(SEL), 0.0), fence=True)
            f = self.vop(lambda e: e.memset(col(A4C), 0.0), fence=True)
            for r4 in range(4):
                f = self.vop(lambda e: e.reduce_max(col(DENC), wrk(WW), axis=AX.X), fence=True)
                f = self.vop(lambda e: e.tensor_single_scalar(wrk(INV), wrk(WW), col(DENC),
                                                              ALU.is_ge), fence=True)
                f = self.vop(lambda e: e.tensor_add(wrk(SEL), wrk(SEL), wrk(INV)), fence=True)
                f = self.vop(lambda e: e.tensor_add(col(A4C), col(A4C), col(DENC)), fence=True)
                if r4 < 3:
                    f = self.vop(lambda e: e.tensor_scalar(wrk(INV), wrk(INV), -1.0, 1.0,
                                                           ALU.mult, ALU.add), fence=True)
                    f = self.vop(lambda e: e.tensor_mul(wrk(WW), wrk(WW), wrk(INV)), fence=True)
            f = self.vop(lambda e: e.scalar_tensor_tensor(col(DENC), col(SC), 1e-20, col(A4C),
                                                          ALU.mult, ALU.add), fence=True)
            f = self.vop(lambda e: e.reciprocal(col(DENC), col(DENC)), fence=True)
            f = self.vop(lambda e: e.tensor_mul(wrk(PW), wrk(PW), wrk(SEL)), fence=True)
            f = self.vop(lambda e: e.tensor_scalar(wrk(PW), wrk(PW), col(DENC), None, ALU.mult),
                         fence=True)
            tt2 = self.mm(lambda e: e.transpose(self.P8[0:32, 3584:3712], self.cwrk[:, 0:32],
                                                self.ident_f[:]), waits=[(s_ve, f)], inc=True)
            f = self.vop(lambda e, tc=tc: e.tensor_copy(
                self.comb_fm[:, tc * 128:(tc + 1) * 128], self.P8[0:32, 3584:3712]),
                waits=[(s_mm, tt2)], fence=True)
        t_comb = s_ve.n
        if self.dbg:
            cfm32 = self.vop(lambda e: e.tensor_copy(self.lgs[:], self.comb_fm[:]),
                             waits=[(s_ve, t_comb)], fence=True)
            self.dma("sync", self.dbg["dcomb"], self.lgs[:], self.s_st, waits=[(s_ve, cfm32)])

        wsl, wgt2 = ra.load(self.wseg_d, 0, 16, 0, 128)
        for k in range(16):
            for ts in range(2):
                wl = [(s_w, wgt2)] if (k == 0 and ts == 0) else []
                self.mm(lambda e, k=k, ts=ts, wsl=wsl: e.matmul(
                    self.P8[0:1, 3072 + ts * 512: 3072 + (ts + 1) * 512],
                    ra.tile(wsl, k, 128)[:, 0:1],
                    self.xn1[:, k * QT + ts * 512: k * QT + (ts + 1) * 512],
                    start=(k == 0), stop=(k == 15)), waits=wl, inc=(k == 15 and ts == 1))
        t_seg = s_mm.n
        ra.used(wsl, t_seg)
        fs = self.sop(lambda e: e.activation(self.grow[:], self.P8[0:1, 3072:3072 + QT],
                                             AF.Sigmoid), waits=[(s_mm, t_seg)], fence=True)
        for ts in range(2):
            self.mm(lambda e, ts=ts: e.matmul(
                self.P8[:, 3072 + ts * 512: 3072 + (ts + 1) * 512], self.ones_row_f[:],
                self.grow[0:1, ts * 512:(ts + 1) * 512], start=True, stop=True),
                waits=[(s_se, fs)] if ts == 0 else [], inc=(ts == 1))
        tb = s_mm.n
        f_grep = self.vop(lambda e: e.tensor_copy(self.grep[:], self.P8[:, 3072:3072 + QT]),
                          waits=[(s_mm, tb)], fence=True)

        for th in range(2):
            t0c = th * 512
            for mcp in range(32):
                slot = mcp % 2
                wslg, wg = ra.load(self.wsg_d, 0, 16, mcp, 128)
                wslu, wu = rb.load(self.wsu_d, 0, 16, mcp, 128)
                for k in range(16):
                    wl = []
                    if k == 0:
                        wl = [(s_w, wg)]
                        if (th, slot, "g") in psum_free:
                            wl.append((s_ve, psum_free[(th, slot, "g")]))
                        elif slot in psum_free:
                            wl.append((s_ve, psum_free[slot]))
                        if mcp == 0 and th == 0:
                            wl.append((s_ve, t_comb))
                        if mcp == 0 and th == 1:
                            wl.append((s_ve, self.s_ve.n))
                    self.mm(lambda e, k=k, slot=slot, wslg=wslg, t0c=t0c: e.matmul(
                        self.P8[:, slot * 1024: slot * 1024 + 512],
                        ra.tile(wslg, k, 128),
                        self.xn1[:, k * QT + t0c: k * QT + t0c + 512],
                        start=(k == 0), stop=(k == 15)), waits=wl)
                    wl2 = [(s_w, wu)] if k == 0 else []
                    self.mm(lambda e, k=k, slot=slot, wslu=wslu, t0c=t0c: e.matmul(
                        self.P8[:, slot * 1024 + 512: slot * 1024 + 1024],
                        rb.tile(wslu, k, 128),
                        self.xn1[:, k * QT + t0c: k * QT + t0c + 512],
                        start=(k == 0), stop=(k == 15)), waits=wl2, inc=(k == 15))
                t_gu = s_mm.n
                ra.used(wslg, t_gu)
                rb.used(wslu, t_gu)
                fs = self.sop(lambda e, slot=slot, mcp=mcp: e.activation(
                    self.uni[:, mcp * 512: mcp * 512 + 512],
                    self.P8[:, slot * 1024: slot * 1024 + 512], AF.Silu),
                    waits=[(s_mm, t_gu)], fence=True)
                fv = self.vop(lambda e, slot=slot, mcp=mcp: e.tensor_mul(
                    self.uni[:, mcp * 512: mcp * 512 + 512],
                    self.uni[:, mcp * 512: mcp * 512 + 512],
                    self.P8[:, slot * 1024 + 512: slot * 1024 + 1024]),
                    waits=[(s_se, fs)], fence=True)
                psum_free[(th, slot, "g")] = fv
                psum_free[slot] = fv
            t_sh = s_ve.n
            if not hasattr(self, "edn_free"):
                self.edn_free = {}
            for oc in range(16):
                slot = oc % 2
                used_slots = []
                for kg in range(2):
                    wsld, wd = ra.load(self.wsd_d, kg * 16, 16, oc, 128)
                    used_slots.append(wsld)
                    for k in range(16):
                        mcp = kg * 16 + k
                        wl = []
                        if k == 0:
                            wl = [(s_w, wd)]
                            if kg == 0:
                                wl.append((s_ve, t_sh))
                                if slot in self.edn_free:
                                    wl.append((s_ve, self.edn_free[slot]))
                        self.mm(lambda e, k=k, mcp=mcp, slot=slot, wsld=wsld: e.matmul(
                            self.P8[:, 2048 + slot * 512: 2048 + (slot + 1) * 512],
                            ra.tile(wsld, k, 128),
                            self.uni[:, mcp * 512: mcp * 512 + 512],
                            start=(mcp == 0), stop=(mcp == 31)),
                            waits=wl, inc=(mcp == 31))
                tmm = s_mm.n
                for ws_ in used_slots:
                    ra.used(ws_, tmm)
                f1 = self.vop(lambda e, oc=oc, slot=slot, t0c=t0c: e.tensor_mul(
                    self.stage[:, slot * 512: slot * 512 + 512],
                    self.P8[:, 2048 + slot * 512: 2048 + (slot + 1) * 512],
                    self.grep[:, t0c:t0c + 512]),
                    waits=[(s_mm, tmm), (s_ve, f_grep)], fence=True)
                f2 = self.vop(lambda e, oc=oc, slot=slot, t0c=t0c: e.tensor_add(
                    self.acc[:, oc * QT + t0c: oc * QT + t0c + 512],
                    self.acc[:, oc * QT + t0c: oc * QT + t0c + 512],
                    self.stage[:, slot * 512: slot * 512 + 512]), fence=True)
                self.edn_free[slot] = f2

        if self.dbg:
            vdone = self.s_ve.n
            for kk in range(16):
                self.dma("sync", self.dbg["dacc_sh"][kk * 128:(kk + 1) * 128, :],
                         self.acc[:, kk * QT:(kk + 1) * QT], self.s_st,
                         waits=[(s_ve, vdone)] if kk == 0 else [])

        rd = WRing(self, self.wrd, 2, stride=768)
        crep_free = {}
        for e_ in range(NEXP):
            cslot = e_ % 2
            wl = [(s_ve, crep_free[cslot])] if cslot in crep_free else []
            if 1 in getattr(self, "edn_free", {}):
                wl.append((s_ve, self.edn_free[1]))
            for ts in range(2):
                self.mm(lambda e, e_=e_, ts=ts: e.matmul(
                    self.P8[:, 3072 + ts * 512: 3072 + (ts + 1) * 512],
                    self.e32[:, e_ * 128:(e_ + 1) * 128],
                    self.comb_fm[:, ts * 512:(ts + 1) * 512], start=True, stop=True),
                    waits=wl if ts == 0 else [], inc=(ts == 1))
            tcr = s_mm.n
            fcr = self.vop(lambda e, cslot=cslot: e.tensor_copy(
                self.crep[:, cslot * QT:(cslot + 1) * QT], self.P8[:, 3072:3072 + QT]),
                waits=[(s_mm, tcr)], fence=True)
            for mp in range(6):
                slot = mp % 2
                wslg, wg = ra.load(self.gupt_d[e_], 0, 16, mp, 128)
                wslu, wu = rb.load(self.gupt_d[e_], 0, 16, 6 + mp, 128)
                for k in range(16):
                    for ts in range(2):
                        wl = []
                        if k == 0 and ts == 0:
                            wl = [(s_w, wg)]
                            if ("eg", slot) in psum_free:
                                wl.append((s_ve, psum_free[("eg", slot)]))
                            else:
                                wl.append((s_ve, self.s_ve.n))
                        self.mm(lambda e, k=k, ts=ts, slot=slot, wslg=wslg: e.matmul(
                            self.P8[:, slot * 1024 + ts * 512: slot * 1024 + (ts + 1) * 512],
                            ra.tile(wslg, k, 128),
                            self.xn1[:, k * QT + ts * 512: k * QT + (ts + 1) * 512],
                            start=(k == 0), stop=(k == 15)), waits=wl)
                        wl2 = [(s_w, wu)] if (k == 0 and ts == 0) else []
                        if k == 0 and ts == 0:
                            for sl2 in (0, 1):
                                if sl2 in getattr(self, "edn_free", {}):
                                    wl2.append((s_ve, self.edn_free[sl2]))
                        self.mm(lambda e, k=k, ts=ts, slot=slot, wslu=wslu: e.matmul(
                            self.P8[:, 2048 + slot * 1024 + ts * 512: 2048 + slot * 1024 + (ts + 1) * 512],
                            rb.tile(wslu, k, 128),
                            self.xn1[:, k * QT + ts * 512: k * QT + (ts + 1) * 512],
                            start=(k == 0), stop=(k == 15)), waits=wl2,
                            inc=(k == 15 and ts == 1))
                t_gu = s_mm.n
                ra.used(wslg, t_gu)
                rb.used(wslu, t_gu)
                fs = self.sop(lambda e, slot=slot, mp=mp: e.activation(
                    self.uni[:, mp * QT: mp * QT + QT],
                    self.P8[:, slot * 1024: slot * 1024 + QT], AF.Silu),
                    waits=[(s_mm, t_gu)] + ([(s_mm, self.hconsumed.get((e_ - 1, mp), 0))] if hasattr(self, "hconsumed") and e_ > 0 else []),
                    fence=True)
                fv = self.vop(lambda e, slot=slot, mp=mp: e.tensor_mul(
                    self.uni[:, mp * QT: mp * QT + QT],
                    self.uni[:, mp * QT: mp * QT + QT],
                    self.P8[:, 2048 + slot * 1024: 2048 + slot * 1024 + QT]),
                    waits=[(s_se, fs)], fence=True)
                fv = self.vop(lambda e, mp=mp, cslot=cslot: e.tensor_mul(
                    self.uni[:, mp * QT: mp * QT + QT],
                    self.uni[:, mp * QT: mp * QT + QT],
                    self.crep[:, cslot * QT:(cslot + 1) * QT]),
                    waits=[(s_ve, fcr)], fence=True)
                psum_free[("eg", slot)] = fv
            t_h = s_ve.n
            crep_free[cslot] = t_h
            if not hasattr(self, "hconsumed"):
                self.hconsumed = {}
            for oc in range(16):
                slot = oc % 2
                wsld, wd = rd.load(self.gdnt_d[e_], 0, 6, oc, 128)
                for mp in range(6):
                    for ts in range(2):
                        wl = []
                        if mp == 0 and ts == 0:
                            wl = [(s_w, wd), (s_ve, t_h)]
                            if slot in self.edn_free:
                                wl.append((s_ve, self.edn_free[slot]))
                        self.mm(lambda e, mp=mp, ts=ts, slot=slot, wsld=wsld: e.matmul(
                            self.P8[:, 2048 + slot * 1024 + ts * 512: 2048 + slot * 1024 + (ts + 1) * 512],
                            rd.tile(wsld, mp, 128),
                            self.uni[:, mp * QT + ts * 512: mp * QT + (ts + 1) * 512],
                            start=(mp == 0), stop=(mp == 5)),
                            waits=wl, inc=(mp == 5 and ts == 1))
                tdn = s_mm.n
                rd.used(wsld, tdn)
                for mp in range(6):
                    self.hconsumed[(e_, mp)] = tdn
                f2 = self.vop(lambda e, oc=oc, slot=slot: e.tensor_add(
                    self.acc[:, oc * QT:(oc + 1) * QT],
                    self.acc[:, oc * QT:(oc + 1) * QT],
                    self.P8[:, 2048 + slot * 1024: 2048 + slot * 1024 + QT]),
                    waits=[(s_mm, tdn)], fence=True)
                self.edn_free[slot] = f2

        if self.dbg:
            vdone = self.s_ve.n
            for kk in range(16):
                self.dma("sync", self.dbg["dacc_ex"][kk * 128:(kk + 1) * 128, :],
                         self.acc[:, kk * QT:(kk + 1) * QT], self.s_st,
                         waits=[(s_ve, vdone)] if kk == 0 else [])

        tld = {}
        v_all_done = s_ve.n
        for oc in range(16):
            slot = oc % 2
            wl = [(s_ve, tld[slot])] if slot in tld else [(s_ve, v_all_done)]
            t = self.dma("sync", self.x1r[:, slot * QT:(slot + 1) * QT],
                         self.x1_d[oc * 128:(oc + 1) * 128, :], self.s_ld, waits=wl)
            f = self.vop(lambda e, oc=oc, slot=slot: e.tensor_add(
                self.acc[:, oc * QT:(oc + 1) * QT], self.acc[:, oc * QT:(oc + 1) * QT],
                self.x1r[:, slot * QT:(slot + 1) * QT]),
                waits=[(self.s_ld, t)], fence=True)
            tld[slot] = f
        t_fin = s_ve.n
        stage_busy = 0
        for jt in range(8):
            fprev = None
            for oc in range(16):
                tt = self.mm(lambda e, jt=jt, oc=oc: e.transpose(
                    self.P8[:, 3072:3200], self.acc[:, oc * QT + jt * 128: oc * QT + (jt + 1) * 128],
                    self.ident_f[:]),
                    waits=[(s_ve, t_fin)] + ([(s_ve, fprev)] if fprev else []) +
                          ([(s_st, stage_busy)] if (oc == 0 and stage_busy) else []),
                    inc=True)
                fprev = self.vop(lambda e, oc=oc: e.tensor_copy(
                    self.stage[:, oc * 128:(oc + 1) * 128], self.P8[:, 3072:3200]),
                    waits=[(s_mm, tt)], fence=True)
            stage_busy = self.dma("sync", self.outp[jt * 128:(jt + 1) * 128, :], self.stage[:],
                                  self.s_st, waits=[(s_ve, fprev)])
        pr.wait("gpsimd", self.s_st)
        pr.wait("sync", self.s_st)


def build_l1c2_program():
    nc = bass.Bass("TRN2", target_bir_lowering=False)
    c = L1C2(nc)
    c.build()
    c.pr.finish(nc)
    c.ctx.close()
    return nc


def l1c2_in_maps(inp, attn_fm, xn1_fm, x1_fm_full):
    wo = np.ascontiguousarray(inp["L1_attn_W_o"].astype(np.float32).T)
    wgate = np.ascontiguousarray(inp["L1_ffn_gate_w"].astype(np.float32).T)
    wsg = np.ascontiguousarray(inp["L1_ffn_shared_gate_w"].astype(np.float32).T)
    wsu = np.ascontiguousarray(inp["L1_ffn_shared_up_w"].astype(np.float32).T)
    wsd = np.ascontiguousarray(inp["L1_ffn_shared_down_w"].astype(np.float32).T)
    wseg = np.zeros((D, 128), np.float32)
    wseg[:, 0] = inp["L1_ffn_shared_expert_gate_w"].astype(np.float32)[0]
    gupt = np.ascontiguousarray(inp["L1_ffn_gate_up_proj"].astype(np.float32).transpose(0, 2, 1))
    gdnt = np.ascontiguousarray(inp["L1_ffn_down_proj"].astype(np.float32).transpose(0, 2, 1))
    e32 = np.zeros((32, 32 * 128), np.float32)
    for e_ in range(32):
        e32[e_, e_ * 128:(e_ + 1) * 128] = 1.0
    ident = np.eye(128, dtype=np.float32)
    ones_row = np.ones((1, 128), np.float32)
    ones_col = np.ones((128, 1), np.float32)
    maps = []
    for c in range(8):
        bc = c // 4
        cols = own_token_cols(c)
        maps.append({
            "attnTc": np.ascontiguousarray(attn_fm[bc][:, cols]),
            "xn1c": np.ascontiguousarray(xn1_fm[bc][:, cols]),
            "x1own": np.ascontiguousarray(x1_fm_full[bc][:, cols]),
            "wo": wo, "wgate": wgate, "wsg": wsg, "wsu": wsu, "wsd": wsd, "wseg": wseg,
            "gupt": gupt, "gdnt": gdnt, "e32sel": e32,
            "ident": ident, "ones_row": ones_row, "ones_col": ones_col,
        })
    return maps


from concourse.bass_utils import run_bass_kernel_spmd

_PROGS = {}


def _prog(name, builder):
    if name not in _PROGS:
        _PROGS[name] = builder()
    return _PROGS[name]


def kernel(**inputs):
    import ml_dtypes
    inp = {k: np.asarray(v) for k, v in inputs.items()}
    cores = list(range(8))

    nc_a = _prog("a", build_l0_program)
    res_a = run_bass_kernel_spmd(nc_a, l0_in_maps(inp), core_ids=cores)
    x1_fm = np.zeros((2, D, S), np.float32)
    for c in range(8):
        sl = res_a.results[c]["x1p"]
        x1_fm[0][c * DSL:(c + 1) * DSL, :] = sl[:, :S]
        x1_fm[1][c * DSL:(c + 1) * DSL, :] = sl[:, S:]

    nc_b = _prog("b", build_l1b_program)
    res_b = run_bass_kernel_spmd(nc_b, l1b_in_maps(inp, x1_fm), core_ids=cores)
    BFD = ml_dtypes.bfloat16
    q_fm = [np.zeros((2048, S), BFD) for _ in range(2)]
    k_fm = [np.zeros((2048, S), BFD) for _ in range(2)]
    v_tm = [np.zeros((S, 2048), BFD) for _ in range(2)]
    xn1_fm = [np.zeros((D, S), BFD) for _ in range(2)]
    for c in range(8):
        bc = c // 4
        cols = own_token_cols(c)
        rr = res_b.results[c]
        q_fm[bc][:, cols] = rr["qout"]
        k_fm[bc][:, cols] = rr["kout"]
        v_tm[bc][cols, :] = rr["vout"]
        xn1_fm[bc][:, cols] = rr["xnout"]

    nc_c1 = _prog("c1", build_l1c1_program)
    res_c1 = run_bass_kernel_spmd(nc_c1, l1c1_in_maps(q_fm, k_fm, v_tm), core_ids=cores)
    attn_fm = [np.zeros((2048, S), BFD) for _ in range(2)]
    for c in range(8):
        bc = c // 4
        h0 = (c % 4) * 4
        attn_fm[bc][h0 * 128:(h0 + 4) * 128, :] = res_c1.results[c]["aout"]

    nc_c2 = _prog("c2", build_l1c2_program)
    res_c2 = run_bass_kernel_spmd(nc_c2, l1c2_in_maps(inp, attn_fm, xn1_fm, x1_fm),
                                  core_ids=cores)
    out = np.zeros((2, S, D), np.float32)
    for c in range(8):
        bc = c // 4
        cols = own_token_cols(c)
        out[bc][cols, :] = res_c2.results[c]["outp"]
    return out


# version 04 (final)
# speedup vs baseline: 1.1438x; 1.1438x over previous
import numpy as np
import concourse.bass as bass
import concourse.mybir as mybir
from concourse.bass_core_rust import IndirectOffsetOnAxis
from contextlib import ExitStack

F32 = mybir.dt.float32
BF16 = mybir.dt.bfloat16
AF = mybir.ActivationFunctionType
ALU = mybir.AluOpType
AX = mybir.AxisListType

D = 2048
B = 2
S = 4096
NTOK = B * S
DSL = 256
N = 16
QT = 1024
NQ = S // QT
EPS = 1e-5
SCALING = 2.0


class Sem:
    def __init__(self, nc, ctx, name):
        self.h = ctx.enter_context(nc.semaphore(name))
        self.n = 0


class Prog:
    ENGS = ("sync", "vector", "tensor", "scalar", "gpsimd")

    def __init__(self):
        self.q = {e: [] for e in self.ENGS}

    def op(self, eng, fn, waits=(), inc=None, inc_k=1):
        tgt = None
        if inc is not None:
            inc.n += inc_k
            tgt = inc.n
        wl = tuple((s.h, v) for s, v in waits if v > 0)

        def emit(e, fn=fn, wl=wl, inc=inc, inc_k=inc_k):
            for h, v in wl:
                e.wait_ge(h, v)
            instr = fn(e)
            if inc is not None:
                instr.then_inc(inc.h, inc_k)
        self.q[eng].append(emit)
        return tgt

    def wait(self, eng, sem, val=None):
        v = sem.n if val is None else val
        if v > 0:
            self.q[eng].append(lambda e, h=sem.h, v=v: e.wait_ge(h, v))

    def barrier(self, sems, engines=None):
        engines = engines or self.ENGS
        counts = [(s, s.n) for s in sems]
        for eng in engines:
            for s, v in counts:
                if v > 0:
                    self.wait(eng, s, v)

    def finish(self, nc):
        block_ctx = nc.Block()
        block = block_ctx.__enter__()
        for name in self.ENGS:
            lst = self.q[name]
            getattr(block, name)(lambda e, lst=lst: [f(e) for f in lst])
        block_ctx.__exit__(None, None, None)


class L0:
    def __init__(self, nc):
        self.nc = nc
        self.ctx = ExitStack()
        self.pr = Prog()
        self._declare_dram()
        self._alloc()

    def _declare_dram(self):
        nc = self.nc
        def din(name, shape):
            return nc.dram_tensor(name, list(shape), F32, kind="ExternalInput").ap()
        self.x_fm = din("xfm", [B, D, S])
        self.xsl = din("xsl", [B, DSL, S])
        self.wv_d = din("wv", [D, DSL])
        self.wxc_d = din("wxc", [D, DSL])
        self.wbt_d = din("wbt", [D, N])
        self.wdl_d = din("wdl", [128, 16])
        self.wbase_d = din("wbase", [D, DSL])
        self.wr_d = din("wr", [D, 8])
        self.wlo_d = din("wlo", [D, 128])
        self.wbw_d = din("wbw", [128, DSL])
        self.cdn_d = din("cdn", [16, 4096])
        self.alog_d = din("alog", [128, 32])
        self.cw_d = din("cw", [128, 8])
        self.bv_d = din("bv", [128, 2])
        self.bxc_d = din("bxc", [128, 2])
        self.bbase_d = din("bbase", [128, 2])
        self.bbt_d = din("bbt", [16, 1])
        self.bpar_d = din("bpar", [16, 1])
        self.bdl_d = din("bdl", [1, 1])
        self.ident_d = din("ident", [128, 128])
        self.ones_row_d = din("ones_row", [1, 128])
        self.ones_col_d = din("ones_col", [128, 1])
        self.sel16_d = din("sel16", [8, 128])
        self.x1p = nc.dram_tensor("x1p", [DSL, NTOK], F32, kind="ExternalOutput").ap()
        self.dbg = {}
        if getattr(L0, "DEBUG", False):
            def dout(name, shape):
                self.dbg[name] = nc.dram_tensor(name, list(shape), F32, kind="ExternalOutput").ap()
            dout("g_xn", [2048, QT])
            dout("g_sv", [256, QT])
            dout("g_xcv", [256, QT])
            dout("g_btq", [16, QT])
            dout("g_drep", [128, QT])
            dout("g_hlo", [128, QT])
            dout("g_ohfm", [8, QT])
            dout("g_ya", [128, QT])
            dout("g_x1a", [256, QT])

    def _alloc(self):
        nc, ctx = self.nc, self.ctx
        sbuf = lambda name, shape, dt=F32: ctx.enter_context(nc.sbuf_tensor(name, list(shape), dt))
        self.wv = sbuf("z_wv", [128, 16 * DSL], BF16)
        self.wxc = sbuf("z_wxc", [128, 16 * DSL], BF16)
        self.wbase = sbuf("z_wbase", [128, 16 * DSL], BF16)
        self.wlo = sbuf("z_wlo", [128, 16 * 128], BF16)
        self.wbt = sbuf("z_wbt", [128, 16 * N], BF16)
        self.wdl = sbuf("z_wdl", [128, 16], BF16)
        self.wr = sbuf("z_wr", [128, 16 * 8], BF16)
        self.wbw = sbuf("z_wbw", [128, DSL], BF16)
        self.cdn = sbuf("z_cdn", [16, 4096], BF16)
        self.sel16 = sbuf("z_sel16", [8, 128], BF16)
        self.ones_row_b = sbuf("z_ones_row_b", [1, 128], BF16)
        self.ones_col_b = sbuf("z_ones_col_b", [128, 1], BF16)
        self.ones_row_f = sbuf("z_ones_row_f", [1, 128], F32)
        self.ident_f = sbuf("z_ident_f", [128, 128], F32)
        self.alog_f = sbuf("z_alog", [128, 32], F32)
        self.cw_f = sbuf("z_cw", [128, 8], F32)
        self.bv_f = sbuf("z_bv", [128, 2], F32)
        self.bxc_f = sbuf("z_bxc", [128, 2], F32)
        self.bbase_f = sbuf("z_bbase", [128, 2], F32)
        self.bbt_f = sbuf("z_bbt", [16, 1], F32)
        self.bpar_f = sbuf("z_bpar", [16, 1], F32)
        self.bdl_f = sbuf("z_bdl", [1, 1], F32)
        self.zcol = sbuf("z_zcol", [128, 1], F32)

        self.xbf = sbuf("z_xbf", [128, 16 * QT], BF16)
        self.sqv = sbuf("z_sqv", [128, 4 * QT], BF16)
        self.m_rep = sbuf("z_mrep", [128, QT], F32)
        self.r_rep = sbuf("z_rrep", [128, QT], F32)
        self.nrm_t = sbuf("z_nrmt", [128, 2 * QT], BF16)
        self.srow = sbuf("z_srow", [1, 3 * QT], F32)
        self.sv = sbuf("z_sv", [128, 2 * QT], BF16)
        self.sxc = sbuf("z_sxc", [128, 2 * (QT + 3)], BF16)
        self.xcv = sbuf("z_xcv", [128, 2 * QT], BF16)
        self.x1a = sbuf("z_x1a", [128, 2 * QT], F32)
        self.ya = sbuf("z_ya", [128, QT], F32)
        self.hlo = sbuf("z_hlo", [128, QT], BF16)
        self.lgs = sbuf("z_lgs", [32, QT], F32)
        self.dlrow = sbuf("z_dlrow", [1, 3 * QT], F32)
        self.drep = sbuf("z_drep", [128, QT], F32)
        self.d16 = sbuf("z_d16", [16, QT], F32)
        self.btq = sbuf("z_btq", [16, QT], F32)
        self.dbb = sbuf("z_dbb", [16, QT], BF16)
        self.da_t = sbuf("z_dat", [128, 2 * QT], F32)
        self.u_t = sbuf("z_ut", [128, 2 * QT], BF16)
        self.h_t = sbuf("z_ht", [128, 2 * QT], F32)
        self.hstate = sbuf("z_hstate", [128, 32], F32)
        self.oh_tm = sbuf("z_ohtm", [128, 8], F32)
        self.mx_c = sbuf("z_mxc", [128, 1], F32)
        self.oh_fm = sbuf("z_ohfm", [8, QT], BF16)
        self.hm = sbuf("z_hm", [128, QT], BF16)
        self.xres = sbuf("z_xres", [128, 2 * QT], F32)

        self.P8 = ctx.enter_context(nc.psum_tensor("P8", [128, 8 * 512], F32))

        ss = lambda name: Sem(nc, ctx, name)
        self.s_ld = ss("s_ld")
        self.s_x = ss("s_x")
        self.s_mm = ss("s_mm")
        self.s_ve = ss("s_ve")
        self.s_se = ss("s_se")
        self.s_st = ss("s_st")
        self.s_fr = ss("s_fr")

    def pa(self, i, w=512, p=128):
        return self.P8[0:p, i * 512: i * 512 + w]

    def dma(self, eng, out_ap, in_ap, sem, waits=()):
        return self.pr.op(eng, lambda e, o=out_ap, i=in_ap: e.dma_start(out=o, in_=i),
                          waits=waits, inc=sem, inc_k=16)

    def vop(self, fn, waits=(), fence=False, inc=None):
        w = list(waits)
        if fence and self.s_ve.n > 0:
            w.append((self.s_ve, self.s_ve.n))
        return self.pr.op("vector", fn, waits=w, inc=self.s_ve if inc is None else inc)

    def sop(self, fn, waits=(), fence=False):
        w = list(waits)
        if fence and self.s_se.n > 0:
            w.append((self.s_se, self.s_se.n))
        return self.pr.op("scalar", fn, waits=w, inc=self.s_se)

    def mm(self, fn, waits=(), inc=False, inc_sem=None):
        sem = inc_sem if inc_sem is not None else (self.s_mm if inc else None)
        return self.pr.op("tensor", fn, waits=waits, inc=sem)

    def build(self):
        self._setup()
        for b in range(B):
            for q in range(NQ):
                self._quarter(b, q)
        self.pr.wait("gpsimd", self.s_st)

    def _setup(self):
        casts = [
            (self.wv, self.wv_d, True), (self.wxc, self.wxc_d, True),
            (self.wbase, self.wbase_d, True), (self.wlo, self.wlo_d, True),
            (self.wbt, self.wbt_d, True), (self.wdl, self.wdl_d, False),
            (self.wr, self.wr_d, True),
            (self.wbw, self.wbw_d, False), (self.cdn, self.cdn_d, False),
            (self.sel16, self.sel16_d, False),
            (self.ones_row_b, self.ones_row_d, False),
            (self.ones_col_b, self.ones_col_d, False),
        ]
        for dst, src, rearr in casts:
            if rearr:
                self.dma("gpsimd", dst[:].rearrange("p (k m) -> p k m", k=16),
                         src.rearrange("(k p) m -> p k m", p=128), self.s_ld)
            else:
                self.dma("gpsimd", dst[:], src, self.s_ld)
        for dst, src in (
            (self.ones_row_f, self.ones_row_d), (self.ident_f, self.ident_d),
            (self.alog_f, self.alog_d), (self.cw_f, self.cw_d),
            (self.bv_f, self.bv_d), (self.bxc_f, self.bxc_d), (self.bbase_f, self.bbase_d),
            (self.bbt_f, self.bbt_d), (self.bpar_f, self.bpar_d), (self.bdl_f, self.bdl_d),
        ):
            self.dma("sync", dst[:], src, self.s_ld)
        n_ld = self.s_ld.n
        t = self.sop(lambda e: e.activation(self.alog_f[:], self.alog_f[:], AF.Exp),
                     waits=[(self.s_ld, n_ld)])
        self.vop(lambda e: e.tensor_scalar_mul(self.alog_f[:], self.alog_f[:], -1.0),
                 waits=[(self.s_se, t)])
        self.vop(lambda e: e.memset(self.zcol[:], 0.0))
        self.vop(lambda e: e.memset(self.hstate[:], 0.0))
        self.vop(lambda e: e.memset(self.sxc[:], 0.0))

    def _quarter(self, b, q):
        pr = self.pr
        tok0 = q * QT
        s_x, s_mm, s_ve, s_se, s_fr = self.s_x, self.s_mm, self.s_ve, self.s_se, self.s_fr

        pr.barrier([self.s_mm, self.s_ve, self.s_se, self.s_st, self.s_ld, self.s_x])

        if q == 0:
            self.vop(lambda e: e.memset(self.sxc[:, 0:3], 0.0), fence=True)
            self.vop(lambda e: e.memset(self.sxc[:, QT + 3: QT + 6], 0.0))

        x0 = s_x.n
        for k in range(16):
            self.dma("gpsimd", self.xbf[:, k * QT:(k + 1) * QT],
                     self.x_fm[b, k * 128:(k + 1) * 128, tok0:tok0 + QT], s_x)
        sq_done = {}
        sqmm_done = {}
        x_all = s_x.n
        for k in range(16):
            w = [(s_x, x_all)]
            if k >= 4:
                w.append((s_fr, sqmm_done[k - 4]))
            sq_done[k] = self.sop(lambda e, k=k: e.activation(
                self.sqv[:, (k % 4) * QT:(k % 4 + 1) * QT],
                self.xbf[:, k * QT:(k + 1) * QT], AF.Square), waits=w)
            for ts in range(2):
                self.mm(lambda e, k=k, ts=ts: e.matmul(
                    self.P8[0:1, ts * 512:(ts + 1) * 512], self.ones_col_b[:],
                    self.xbf[:, k * QT + ts * 512: k * QT + (ts + 1) * 512],
                    start=(k == 0), stop=(k == 15)),
                    waits=[(s_x, x_all)] if ts == 0 else [],
                    inc=(k == 15 and ts == 1))
            for ts in range(2):
                self.mm(lambda e, k=k, ts=ts: e.matmul(
                    self.P8[0:1, 1024 + ts * 512: 1024 + (ts + 1) * 512], self.ones_col_b[:],
                    self.sqv[:, (k % 4) * QT + ts * 512:(k % 4) * QT + (ts + 1) * 512],
                    start=(k == 0), stop=(k == 15)),
                    waits=[(s_se, sq_done[k])] if ts == 0 else [],
                    inc_sem=(s_fr if ts == 1 else None))
            sqmm_done[k] = s_fr.n
        t_stats_x = s_mm.n
        t_stats_sq = s_fr.n

        MROW, VROW, RROW = 0, 1, 2
        def row(i):
            return self.srow[0:1, i * QT:(i + 1) * QT]
        f = self.vop(lambda e: e.tensor_scalar_mul(row(MROW), self.P8[0:1, 0:QT], 1.0 / D),
                     waits=[(s_mm, t_stats_x), (s_fr, t_stats_sq)])
        f = self.vop(lambda e: e.tensor_scalar_mul(row(VROW), self.P8[0:1, QT:2 * QT], 1.0 / D))
        f = self.vop(lambda e: e.tensor_mul(row(RROW), row(MROW), row(MROW)), fence=True)
        f = self.vop(lambda e: e.tensor_sub(row(VROW), row(VROW), row(RROW)), fence=True)
        f = self.vop(lambda e: e.tensor_scalar_add(row(VROW), row(VROW), EPS), fence=True)
        fs = self.sop(lambda e: e.activation(row(RROW), row(VROW), AF.Sqrt),
                      waits=[(s_ve, f)])
        f = self.vop(lambda e: e.reciprocal(row(RROW), row(RROW)), waits=[(s_se, fs)], fence=True)
        for ts in range(2):
            self.mm(lambda e, ts=ts: e.matmul(
                self.P8[:, 2048 + ts * 512: 2048 + (ts + 1) * 512], self.ones_row_f[:],
                self.srow[0:1, MROW * QT + ts * 512: MROW * QT + (ts + 1) * 512],
                start=True, stop=True),
                waits=[(s_ve, f)] if ts == 0 else [], inc=(ts == 1))
        tb = s_mm.n
        fm = self.vop(lambda e: e.tensor_copy(self.m_rep[:], self.P8[:, 2048:2048 + QT]),
                      waits=[(s_mm, tb)])
        for ts in range(2):
            self.mm(lambda e, ts=ts: e.matmul(
                self.P8[:, 2048 + ts * 512: 2048 + (ts + 1) * 512], self.ones_row_f[:],
                self.srow[0:1, RROW * QT + ts * 512: RROW * QT + (ts + 1) * 512],
                start=True, stop=True),
                waits=[(s_ve, fm)] if ts == 0 else [], inc=(ts == 1))
        tb = s_mm.n
        self.vop(lambda e: e.tensor_copy(self.r_rep[:], self.P8[:, 2048:2048 + QT]),
                 waits=[(s_mm, tb)])

        for k in range(16):
            self.vop(lambda e, k=k: e.tensor_sub(
                self.nrm_t[:, (k % 2) * QT:(k % 2 + 1) * QT],
                self.xbf[:, k * QT:(k + 1) * QT], self.m_rep[:]), fence=True)
            self.vop(lambda e, k=k: e.tensor_mul(
                self.xbf[:, k * QT:(k + 1) * QT],
                self.nrm_t[:, (k % 2) * QT:(k % 2 + 1) * QT], self.r_rep[:]), fence=True)
        t_norm = s_ve.n

        if self.dbg and b == 0 and q == 0:
            tnow = s_ve.n
            for kk in range(16):
                self.dma("gpsimd", self.dbg["g_xn"][kk * 128:(kk + 1) * 128, :],
                         self.xbf[:, kk * QT:(kk + 1) * QT], self.s_st,
                         waits=[(s_ve, tnow)] if kk == 0 else [])
        if getattr(L0, "TRUNC", 0) == 1:
            return
        self.vop(lambda e: e.memset(self.lgs[:], -1e30))
        t_lgpad = s_ve.n
        mts = [("wv", 0, "v", 0), ("wv", 1, "v", 1),
               ("wxc", 0, "xc", 0), ("wxc", 1, "xc", 1),
               ("wbase", 0, "base", 0), ("wbase", 1, "base", 1),
               ("wlo", 0, "hl", 0),
               ("wbt", 0, "bt", 0), ("wdl", 0, "dl", 0), ("wr", 0, "lg", 0)]
        mwidth = {"v": 128, "xc": 128, "base": 128, "hl": 128, "bt": 16, "dl": 1, "lg": 8}
        for g0 in range(0, len(mts), 2):
            grp = mts[g0:g0 + 2]
            pr.wait("tensor", s_ve)
            pr.wait("tensor", s_se)
            for k in range(16):
                for gi, (wname, mcol, kind, xi) in enumerate(grp):
                    wsb = getattr(self, wname)
                    ww = mwidth[kind]
                    ws = wsb.shape[1] // 16
                    for ts in range(2):
                        is_last = (k == 15 and gi == len(grp) - 1 and ts == 1)
                        self.mm(lambda e, k=k, gi=gi, wsb=wsb, mcol=mcol, ww=ww, ts=ts, ws=ws: e.matmul(
                            self.P8[0:ww, (gi * 2 + ts) * 512:(gi * 2 + ts + 1) * 512],
                            wsb[:, k * ws + mcol * 128: k * ws + mcol * 128 + ww],
                            self.xbf[:, k * QT + ts * 512: k * QT + (ts + 1) * 512],
                            start=(k == 0), stop=(k == 15)),
                            waits=[(s_ve, t_norm)] if (g0 == 0 and k == 0 and gi == 0 and ts == 0) else [],
                            inc=is_last)
            t_grp = s_mm.n
            for gi, (wname, mcol, kind, xi) in enumerate(grp):
                for ts in range(2):
                    po = (gi * 2 + ts) * 512
                    if kind == "v":
                        self.sop(lambda e, xi=xi, po=po, ts=ts: e.activation(
                            self.sv[:, xi * QT + ts * 512: xi * QT + (ts + 1) * 512],
                            self.P8[0:128, po:po + 512], AF.Silu, bias=self.bv_f[:, xi:xi + 1]),
                            waits=[(s_mm, t_grp)])
                    elif kind == "xc":
                        self.sop(lambda e, xi=xi, po=po, ts=ts: e.activation(
                            self.sxc[:, xi * (QT + 3) + 3 + ts * 512: xi * (QT + 3) + 3 + (ts + 1) * 512],
                            self.P8[0:128, po:po + 512], AF.Silu, bias=self.bxc_f[:, xi:xi + 1]),
                            waits=[(s_mm, t_grp)])
                    elif kind == "base":
                        self.sop(lambda e, xi=xi, po=po, ts=ts: e.activation(
                            self.x1a[:, xi * QT + ts * 512: xi * QT + (ts + 1) * 512],
                            self.P8[0:128, po:po + 512], AF.Identity, bias=self.bbase_f[:, xi:xi + 1]),
                            waits=[(s_mm, t_grp)])
                    elif kind == "hl":
                        self.vop(lambda e, po=po, ts=ts: e.tensor_copy(
                            self.hlo[:, ts * 512:(ts + 1) * 512], self.P8[0:128, po:po + 512]),
                            waits=[(s_mm, t_grp)])
                    elif kind == "bt":
                        self.vop(lambda e, po=po, ts=ts: e.tensor_scalar(
                            self.btq[:, ts * 512:(ts + 1) * 512], self.P8[0:16, po:po + 512],
                            self.bbt_f[:], self.bpar_f[:], ALU.add, ALU.add),
                            waits=[(s_mm, t_grp)])
                    elif kind == "dl":
                        self.vop(lambda e, po=po, ts=ts: e.tensor_scalar(
                            self.dlrow[0:1, ts * 512:(ts + 1) * 512], self.P8[0:1, po:po + 512],
                            self.bdl_f[:], None, ALU.add),
                            waits=[(s_mm, t_grp)])
                    elif kind == "lg":
                        self.vop(lambda e, po=po, ts=ts: e.tensor_copy(
                            self.lgs[0:8, ts * 512:(ts + 1) * 512], self.P8[0:8, po:po + 512]),
                            waits=[(s_mm, t_grp), (s_ve, t_lgpad)])

        if self.dbg and b == 0 and q == 0:
            tse = s_se.n
            tve = s_ve.n
            for c2 in range(2):
                self.dma("gpsimd", self.dbg["g_sv"][c2 * 128:(c2 + 1) * 128, :],
                         self.sv[:, c2 * QT:(c2 + 1) * QT], self.s_st,
                         waits=[(s_se, tse), (s_ve, tve)] if c2 == 0 else [])
            self.dma("gpsimd", self.dbg["g_hlo"], self.hlo[:], self.s_st)
            self.dma("sync", self.dbg["g_btq"], self.btq[:], self.s_st)
        if getattr(L0, "TRUNC", 0) == 2:
            return
        DL, T1r, T2r = 0, 1, 2
        SP, DRow = DL, T1r
        def drow(i):
            return self.dlrow[0:1, i * QT:(i + 1) * QT]
        fs = self.sop(lambda e: e.activation(drow(T1r), drow(DL), AF.Abs),
                      waits=[(s_ve, s_ve.n)], fence=True)
        fs = self.sop(lambda e: e.activation(drow(T1r), drow(T1r), AF.Exp, bias=0.0, scale=-1.0),
                      fence=True)
        fv = self.vop(lambda e: e.tensor_scalar_add(drow(T1r), drow(T1r), 1.0),
                      waits=[(s_se, fs)], fence=True)
        fs = self.sop(lambda e: e.activation(drow(T1r), drow(T1r), AF.Ln),
                      waits=[(s_ve, fv)], fence=True)
        fv = self.vop(lambda e: e.tensor_scalar_max(drow(T2r), drow(DL), 0.0), fence=True)
        fv = self.vop(lambda e: e.tensor_add(drow(SP), drow(T1r), drow(T2r)),
                      waits=[(s_se, fs)], fence=True)
        fs = self.sop(lambda e: e.activation(drow(T1r), drow(SP), AF.Exp, bias=0.0, scale=-1.0),
                      waits=[(s_ve, fv)], fence=True)
        fv = self.vop(lambda e: e.tensor_scalar_add(drow(T2r), drow(T1r), 1.0),
                      waits=[(s_se, fs)], fence=True)
        fv = self.vop(lambda e: e.reciprocal(drow(T2r), drow(T2r)), fence=True)
        fv = self.vop(lambda e: e.tensor_mul(drow(DRow), drow(SP), drow(T2r)), fence=True)
        for ts in range(2):
            self.mm(lambda e, ts=ts: e.matmul(
                self.P8[:, 2048 + ts * 512: 2048 + (ts + 1) * 512], self.ones_row_f[:],
                self.dlrow[0:1, DRow * QT + ts * 512: DRow * QT + (ts + 1) * 512],
                start=True, stop=True),
                waits=[(s_ve, fv)] if ts == 0 else [], inc=(ts == 1))
        tb = s_mm.n
        fv = self.vop(lambda e: e.tensor_copy(self.drep[:], self.P8[:, 2048:2048 + QT]),
                      waits=[(s_mm, tb)])
        for ts in range(2):
            self.mm(lambda e, ts=ts: e.matmul(
                self.P8[0:16, 2048 + ts * 512: 2048 + (ts + 1) * 512], self.ones_row_f[0:1, 0:16],
                self.dlrow[0:1, DRow * QT + ts * 512: DRow * QT + (ts + 1) * 512],
                start=True, stop=True),
                waits=[(s_ve, fv)] if ts == 0 else [], inc=(ts == 1))
        tb = s_mm.n
        fv = self.vop(lambda e: e.tensor_copy(self.d16[:], self.P8[0:16, 2048:2048 + QT]),
                      waits=[(s_mm, tb)])
        t_dbb = self.vop(lambda e: e.tensor_mul(self.dbb[:], self.btq[:], self.d16[:]), fence=True)

        if self.dbg and b == 0 and q == 0:
            self.dma("sync", self.dbg["g_drep"], self.drep[:], self.s_st,
                     waits=[(s_ve, s_ve.n)])
        t_conv = {}
        for c2 in range(2):
            base = c2 * (QT + 3)
            self.vop(lambda e, c2=c2, base=base: e.tensor_scalar(
                self.xcv[:, c2 * QT:(c2 + 1) * QT], self.sxc[:, base:base + QT],
                self.cw_f[:, c2 * 4:c2 * 4 + 1], None, ALU.mult),
                waits=[(s_se, s_se.n)], fence=True)
            for kk in range(1, 4):
                self.vop(lambda e, c2=c2, base=base, kk=kk: e.scalar_tensor_tensor(
                    self.xcv[:, c2 * QT:(c2 + 1) * QT], self.sxc[:, base + kk:base + kk + QT],
                    self.cw_f[:, c2 * 4 + kk:c2 * 4 + kk + 1],
                    self.xcv[:, c2 * QT:(c2 + 1) * QT], ALU.mult, ALU.add), fence=True)
            t_conv[c2] = self.vop(lambda e, base=base: e.tensor_copy(
                self.sxc[:, base:base + 3], self.sxc[:, base + QT:base + QT + 3]), fence=True)

        if getattr(L0, "TRUNC", 0) == 3:
            return
        if self.dbg and b == 0 and q == 0:
            for c2 in range(2):
                self.dma("gpsimd", self.dbg["g_xcv"][c2 * 128:(c2 + 1) * 128, :],
                         self.xcv[:, c2 * QT:(c2 + 1) * QT], self.s_st,
                         waits=[(s_ve, s_ve.n)] if c2 == 0 else [])
        v_ready = s_ve.n
        prev_scan = {}
        for c2 in range(2):
            for n in range(N):
                ti = c2 * N + n
                slot = ti % 2
                acol = self.alog_f[:, c2 * 16 + n: c2 * 16 + n + 1]
                w = [(s_ve, v_ready)]
                if ti >= 2:
                    w.append((s_ve, prev_scan[slot]))
                fs = self.sop(lambda e, slot=slot, acol=acol: e.activation(
                    self.da_t[:, slot * QT:(slot + 1) * QT], self.drep[:],
                    AF.Exp, bias=self.zcol[:], scale=acol), waits=w)
                for ts in range(2):
                    w2 = []
                    if ts == 0:
                        w2 = [(s_ve, t_dbb), (s_ve, t_conv[c2])]
                        if ti >= 2:
                            w2.append((s_ve, prev_scan[slot]))
                    self.mm(lambda e, c2=c2, n=n, ts=ts, slot=slot: e.matmul(
                        self.P8[:, (slot * 2 + ts) * 512:(slot * 2 + ts + 1) * 512],
                        self.cdn[:, (c2 * 16 + n) * 128:(c2 * 16 + n + 1) * 128],
                        self.dbb[:, ts * 512:(ts + 1) * 512],
                        start=True, stop=True), waits=w2, inc=(ts == 1))
                tu = s_mm.n
                fu = self.vop(lambda e, slot=slot, c2=c2: e.tensor_mul(
                    self.u_t[:, slot * QT:(slot + 1) * QT],
                    self.P8[:, slot * 1024: slot * 1024 + QT],
                    self.xcv[:, c2 * QT:(c2 + 1) * QT]),
                    waits=[(s_mm, tu)], fence=True)
                init_ap = self.zcol[:] if q == 0 else self.hstate[:, ti:ti + 1]
                fsc = self.vop(lambda e, slot=slot, ti=ti, init_ap=init_ap: e.tensor_tensor_scan(
                    self.h_t[:, slot * QT:(slot + 1) * QT],
                    self.da_t[:, slot * QT:(slot + 1) * QT],
                    self.u_t[:, slot * QT:(slot + 1) * QT],
                    init_ap, ALU.mult, ALU.add),
                    waits=[(s_se, fs)], fence=True)
                self.vop(lambda e, slot=slot, ti=ti: e.tensor_copy(
                    self.hstate[:, ti:ti + 1],
                    self.h_t[:, (slot + 1) * QT - 1:(slot + 1) * QT]), fence=True)
                if n == 0:
                    fy = self.vop(lambda e, slot=slot: e.tensor_copy(
                        self.ya[:], self.h_t[:, slot * QT:(slot + 1) * QT]), fence=True)
                else:
                    fy = self.vop(lambda e, slot=slot: e.tensor_add(
                        self.ya[:], self.ya[:], self.h_t[:, slot * QT:(slot + 1) * QT]), fence=True)
                prev_scan[slot] = fy
            if self.dbg and b == 0 and q == 0 and c2 == 0:
                self.dma("sync", self.dbg["g_ya"], self.ya[:], self.s_st,
                         waits=[(s_ve, s_ve.n)])
            self.vop(lambda e, c2=c2: e.tensor_mul(
                self.nrm_t[:, 0:QT], self.ya[:], self.sv[:, c2 * QT:(c2 + 1) * QT]),
                waits=[(s_se, s_se.n)], fence=True)
            self.vop(lambda e, c2=c2: e.tensor_add(
                self.x1a[:, c2 * QT:(c2 + 1) * QT], self.x1a[:, c2 * QT:(c2 + 1) * QT],
                self.nrm_t[:, 0:QT]), fence=True)

        for tc in range(QT // 128):
            tt = self.mm(lambda e, tc=tc: e.transpose(
                self.P8[:, 2048:2080], self.lgs[:, tc * 128:(tc + 1) * 128],
                self.ident_f[0:32, 0:32]),
                waits=[(s_ve, s_ve.n)], inc=True)
            self.vop(lambda e: e.reduce_max(self.mx_c[:], self.P8[:, 2048:2056], axis=AX.X),
                     waits=[(s_mm, tt)], fence=True)
            fv = self.vop(lambda e: e.tensor_single_scalar(
                self.oh_tm[:], self.P8[:, 2048:2056], self.mx_c[:], ALU.is_ge), fence=True)
            tt2 = self.mm(lambda e: e.transpose(
                self.P8[0:8, 2560:2688], self.oh_tm[:], self.ident_f[:]),
                waits=[(s_ve, fv)], inc=True)
            self.vop(lambda e, tc=tc: e.tensor_copy(
                self.oh_fm[:, tc * 128:(tc + 1) * 128], self.P8[0:8, 2560:2688]),
                waits=[(s_mm, tt2)], fence=True)
        t_oh = s_ve.n
        for ts in range(2):
            self.mm(lambda e, ts=ts: e.matmul(
                self.P8[:, 2048 + ts * 512: 2048 + (ts + 1) * 512], self.sel16[:],
                self.oh_fm[:, ts * 512:(ts + 1) * 512], start=True, stop=True),
                waits=[(s_ve, t_oh)] if ts == 0 else [], inc=(ts == 1))
        tb = s_mm.n
        fv = self.vop(lambda e: e.tensor_mul(self.hm[:], self.hlo[:], self.P8[:, 2048:2048 + QT]),
                      waits=[(s_mm, tb)], fence=True)
        for mtc in range(2):
            for ts in range(2):
                self.mm(lambda e, mtc=mtc, ts=ts: e.matmul(
                    self.P8[:, (mtc * 2 + ts) * 512:(mtc * 2 + ts + 1) * 512],
                    self.wbw[:, mtc * 128:(mtc + 1) * 128],
                    self.hm[:, ts * 512:(ts + 1) * 512], start=True, stop=True),
                    waits=[(s_ve, fv)] if (mtc == 0 and ts == 0) else [],
                    inc=(mtc == 1 and ts == 1))
        tb = s_mm.n
        for mtc in range(2):
            self.vop(lambda e, mtc=mtc: e.scalar_tensor_tensor(
                self.x1a[:, mtc * QT:(mtc + 1) * QT],
                self.P8[:, mtc * 1024: mtc * 1024 + QT], SCALING,
                self.x1a[:, mtc * QT:(mtc + 1) * QT], ALU.mult, ALU.add),
                waits=[(s_mm, tb)], fence=True)

        if self.dbg and b == 0 and q == 0:
            self.dma("gpsimd", self.dbg["g_ohfm"], self.oh_fm[:], self.s_st,
                     waits=[(s_ve, s_ve.n)])
            for c2 in range(2):
                self.dma("sync", self.dbg["g_x1a"][c2 * 128:(c2 + 1) * 128, :],
                         self.x1a[:, c2 * QT:(c2 + 1) * QT], self.s_st)
        self.dma("sync", self.xres[:, 0:QT], self.xsl[b, 0:128, tok0:tok0 + QT], self.s_ld)
        self.dma("sync", self.xres[:, QT:2 * QT], self.xsl[b, 128:256, tok0:tok0 + QT], self.s_ld)
        tld = self.s_ld.n
        for c2 in range(2):
            self.vop(lambda e, c2=c2: e.tensor_add(
                self.x1a[:, c2 * QT:(c2 + 1) * QT], self.x1a[:, c2 * QT:(c2 + 1) * QT],
                self.xres[:, c2 * QT:(c2 + 1) * QT]),
                waits=[(self.s_ld, tld)], fence=True)
        for c2 in range(2):
            self.dma("sync", self.x1p[c2 * 128:(c2 + 1) * 128, b * S + tok0: b * S + tok0 + QT],
                     self.x1a[:, c2 * QT:(c2 + 1) * QT], self.s_st,
                     waits=[(self.s_ve, self.s_ve.n)])


def build_l0_program():
    nc = bass.Bass("TRN2", target_bir_lowering=False)
    l0 = L0(nc)
    l0.build()
    l0.pr.finish(nc)
    l0.ctx.close()
    return nc


def l0_in_maps(inp):
    x = np.asarray(inp["L0_attn_x"], np.float32)
    Win = np.asarray(inp["L0_attn_W_in"], np.float32)
    b_in = np.asarray(inp["L0_attn_b_in"], np.float32)
    A_log = np.asarray(inp["L0_attn_A_log"], np.float32)
    B_param = np.asarray(inp["L0_attn_B_param"], np.float32)
    C_param = np.asarray(inp["L0_attn_C_param"], np.float32)
    conv_w = np.asarray(inp["L0_attn_conv_w"], np.float32)
    W_base = np.asarray(inp["L0_ffn_W_base"], np.float32)
    b_base = np.asarray(inp["L0_ffn_b_base"], np.float32)
    router_W = np.asarray(inp["L0_ffn_router_W"], np.float32)
    A_lora = np.asarray(inp["L0_ffn_A"], np.float32)
    Bw = np.asarray(inp["L0_ffn_Bw"], np.float32)

    xfm = np.ascontiguousarray(x.transpose(0, 2, 1))
    wbaseT = np.ascontiguousarray(W_base.T)
    wrT = np.ascontiguousarray(router_W.T)
    wloT = np.ascontiguousarray(A_lora.reshape(128, D).T)
    bwT = np.ascontiguousarray(Bw.transpose(0, 2, 1).reshape(128, D))
    ident = np.eye(128, dtype=np.float32)
    ones_row = np.ones((1, 128), np.float32)
    ones_col = np.ones((128, 1), np.float32)
    sel16 = np.zeros((8, 128), np.float32)
    for e in range(8):
        sel16[e, e * 16:(e + 1) * 16] = 1.0

    maps = []
    for c in range(8):
        dsl = slice(c * DSL, (c + 1) * DSL)
        cdn = np.zeros((16, 4096), np.float32)
        Csl = C_param[dsl]
        for c2 in range(2):
            for n in range(16):
                cdn[n, (c2 * 16 + n) * 128:(c2 * 16 + n + 1) * 128] = Csl[c2 * 128:(c2 + 1) * 128, n]
        alog = np.ascontiguousarray(
            A_log[dsl].reshape(2, 128, 16).transpose(1, 0, 2).reshape(128, 32))
        cw = np.ascontiguousarray(
            conv_w[dsl, 0, :].reshape(2, 128, 4).transpose(1, 0, 2).reshape(128, 8))
        bcol = lambda v: np.ascontiguousarray(np.asarray(v)[dsl].reshape(2, 128).T)
        maps.append({
            "xfm": xfm,
            "xsl": np.ascontiguousarray(xfm[:, dsl, :]),
            "wv": np.ascontiguousarray(Win[:, dsl]),
            "wxc": np.ascontiguousarray(Win[:, 2048 + c * DSL: 2048 + (c + 1) * DSL]),
            "wbt": np.ascontiguousarray(Win[:, 4096:4112]),
            "wdl": np.ascontiguousarray(Win[:, 4112].reshape(16, 128).T),
            "wbase": np.ascontiguousarray(wbaseT[:, dsl]),
            "wr": wrT,
            "wlo": wloT,
            "wbw": np.ascontiguousarray(bwT[:, dsl]),
            "cdn": cdn,
            "alog": alog,
            "cw": cw,
            "bv": bcol(b_in[:2048]),
            "bxc": bcol(b_in[2048:4096]),
            "bbase": bcol(b_base),
            "bbt": np.ascontiguousarray(b_in[4096:4112].reshape(16, 1)),
            "bpar": np.ascontiguousarray(B_param.reshape(16, 1)),
            "bdl": np.ascontiguousarray(b_in[4112:4113].reshape(1, 1)),
            "ident": ident,
            "ones_row": ones_row,
            "ones_col": ones_col,
            "sel16": sel16,
        })
    return maps


"""Layer 1 builders: L1B (projections / Q,K,V) and later L1C (attention + MoE + residual).

PSUM map (L1B): P8 = 7 fp32 banks [128, 3584]:
  banks 0-3: proj ping-pong slots (slot*1024, 2 banks each) + ln stats rows [0:1, 0:2048]
  banks 4-5: ln broadcasts [*, 2048:3072]
  bank 6:    rope rot matmuls [*, 3072:3584]
  PT: bank 7 as bf16 [128, 512] for transposes.
"""

QT = 1024
NH = 16
DH = 128
Q_PROJ = 1024
KV_PROJ = 1365
KV_KC = 11
KV_LAST = 85
K_NOPE = 1792
UKV_OUT = 3840
ISQ = 0.08838834764831845


class L1B:
    def __init__(self, nc):
        self.nc = nc
        self.ctx = ExitStack()
        self.pr = Prog()
        self._declare()
        self._alloc()
        self.last_wring = {}
        self.last_psum = {}
        self.stage_busy = {}
        self.ropet_busy = 0

    def _declare(self):
        nc = self.nc
        def din(name, shape, dt=F32):
            return nc.dram_tensor(name, list(shape), dt, kind="ExternalInput").ap()
        self.x1f = din("x1f", [D, QT])
        self.wdq_d = din("wdq", [D, Q_PROJ])
        self.wuq_d = din("wuq", [Q_PROJ, K_NOPE])
        self.wqr_d = din("wqr", [Q_PROJ, 256])
        self.wdkv_d = din("wdkv", [D, KV_PROJ])
        self.wukv_d = din("wukv", [KV_PROJ, UKV_OUT])
        self.wkr_d = din("wkr", [D, 256])
        self.wgate_d = din("wgate", [D, 32])
        self.qlw_d = din("qlw", [128, 8])
        self.qlb_d = din("qlb", [128, 8])
        self.kvlw_d = din("kvlw", [128, KV_KC])
        self.kvlb_d = din("kvlb", [128, KV_KC])
        self.cosq_d = din("cosq", [128, QT])
        self.sinq_d = din("sinq", [128, QT])
        self.prot_d = din("prot", [128, 128])
        self.ident_d = din("ident", [128, 128])
        self.ones_row_d = din("ones_row", [1, 128])
        self.ones_col_d = din("ones_col", [128, 1])
        dout = lambda name, shape: nc.dram_tensor(name, list(shape), BF16, kind="ExternalOutput").ap()
        self.qout = dout("qout", [2048, QT])
        self.kout = dout("kout", [2048, QT])
        self.vout = dout("vout", [QT, 2048])
        self.xnout = dout("xnout", [D, QT])
        self.combo = dout("combo", [32, QT])

    def _alloc(self):
        nc, ctx = self.nc, self.ctx
        sbuf = lambda name, shape, dt=F32: ctx.enter_context(nc.sbuf_tensor(name, list(shape), dt))
        self.xb = sbuf("z_xb", [128, 16 * QT], BF16)
        self.sq = sbuf("z_sq", [128, 4 * QT], BF16)
        self.srow = sbuf("z_srow", [1, 3 * QT], F32)
        self.m_rep = sbuf("z_mrep", [128, QT], F32)
        self.r_rep = sbuf("z_rrep", [128, QT], F32)
        self.nrm = sbuf("z_nrm", [128, QT], BF16)
        self.cqn = sbuf("z_cqn", [128, 8 * QT], BF16)
        self.ckvn = sbuf("z_ckvn", [128, KV_KC * QT], BF16)
        self.wring = sbuf("z_wring", [128, 2 * 16 * 128], BF16)
        self.stage = sbuf("z_stage", [128, 2 * QT], BF16)
        self.qr_raw = sbuf("z_qrraw", [128, 2 * QT], BF16)
        self.rope_t = sbuf("z_ropet", [128, 2 * QT], BF16)
        self.cos_f = sbuf("z_cosf", [128, QT], F32)
        self.sin_f = sbuf("z_sinf", [128, QT], F32)
        self.prot_b = sbuf("z_protb", [128, 128], BF16)
        self.ident_b = sbuf("z_identb", [128, 128], BF16)
        self.ident_f = sbuf("z_identf", [128, 128], F32)
        self.ones_row_f = sbuf("z_onesrowf", [1, 128], F32)
        self.ones_col_b = sbuf("z_onescolb", [128, 1], BF16)
        self.lgs = sbuf("z_lgs", [32, QT], F32)
        self.cwrk = sbuf("z_cwrk", [128, 4 * 32], F32)
        self.ccol = sbuf("z_ccol", [128, 4], F32)
        self.comb_fm = sbuf("z_combfm", [32, QT], BF16)
        self.qlw = sbuf("z_qlw", [128, 8], F32)
        self.qlb = sbuf("z_qlb", [128, 8], F32)
        self.kvlw = sbuf("z_kvlw", [128, KV_KC], F32)
        self.kvlb = sbuf("z_kvlb", [128, KV_KC], F32)
        self.P8 = ctx.enter_context(nc.psum_tensor("P8", [128, 7 * 512], F32))
        self.PT = ctx.enter_context(nc.psum_tensor("PT", [128, 512], BF16))
        ss = lambda name: Sem(nc, ctx, name)
        self.s_ld = ss("s_ld")
        self.s_w = ss("s_w")
        self.s_fr = ss("s_fr")
        self.s_mm = ss("s_mm")
        self.s_ve = ss("s_ve")
        self.s_se = ss("s_se")
        self.s_st = ss("s_st")

    def dma(self, eng, out_ap, in_ap, sem, waits=()):
        return self.pr.op(eng, lambda e, o=out_ap, i=in_ap: e.dma_start(out=o, in_=i),
                          waits=waits, inc=sem, inc_k=16)

    def vop(self, fn, waits=(), fence=False):
        w = list(waits)
        if fence and self.s_ve.n > 0:
            w.append((self.s_ve, self.s_ve.n))
        return self.pr.op("vector", fn, waits=w, inc=self.s_ve)

    def sop(self, fn, waits=(), fence=False):
        w = list(waits)
        if fence and self.s_se.n > 0:
            w.append((self.s_se, self.s_se.n))
        return self.pr.op("scalar", fn, waits=w, inc=self.s_se)

    def mm(self, fn, waits=(), inc=False, inc_sem=None):
        sem = inc_sem if inc_sem is not None else (self.s_mm if inc else None)
        return self.pr.op("tensor", fn, waits=waits, inc=sem)

    def ln_fm(self, src, nk, last_kw, dtot, wcol=None, bcol=None):
        s_mm, s_ve, s_se, s_fr = self.s_mm, self.s_ve, self.s_se, self.s_fr
        ready_v = s_ve.n
        sq_done = {}
        sqmm = {}
        for k in range(nk):
            kw = 128 if k < nk - 1 else last_kw
            w = [(s_ve, ready_v)] if k == 0 else []
            if k >= 4:
                w.append((s_fr, sqmm[k - 4]))
            sq_done[k] = self.sop(lambda e, k=k, kw=kw: e.activation(
                self.sq[0:kw, (k % 4) * QT:(k % 4) * QT + QT],
                src[0:kw, k * QT:k * QT + QT], AF.Square), waits=w, fence=(k == 0))
            for ts in range(2):
                wl = [(s_ve, ready_v)] if (k == 0 and ts == 0) else []
                self.mm(lambda e, k=k, ts=ts, kw=kw: e.matmul(
                    self.P8[0:1, ts * 512:(ts + 1) * 512], self.ones_col_b[0:kw, :],
                    src[0:kw, k * QT + ts * 512: k * QT + (ts + 1) * 512],
                    start=(k == 0), stop=(k == nk - 1)),
                    waits=wl, inc=(k == nk - 1 and ts == 1))
            for ts in range(2):
                wl2 = [(s_se, sq_done[k])] if ts == 0 else []
                if k == 0 and ts == 0:
                    wl2.append((s_ve, ready_v))
                self.mm(lambda e, k=k, ts=ts, kw=kw: e.matmul(
                    self.P8[0:1, 1024 + ts * 512: 1024 + (ts + 1) * 512], self.ones_col_b[0:kw, :],
                    self.sq[0:kw, (k % 4) * QT + ts * 512:(k % 4) * QT + (ts + 1) * 512],
                    start=(k == 0), stop=(k == nk - 1)),
                    waits=wl2,
                    inc_sem=(s_fr if ts == 1 else None))
            sqmm[k] = s_fr.n
        t_x, t_sq = s_mm.n, s_fr.n
        MROW, VROW, RROW = 0, 1, 2
        row = lambda i: self.srow[0:1, i * QT:(i + 1) * QT]
        f = self.vop(lambda e: e.tensor_scalar_mul(row(MROW), self.P8[0:1, 0:QT], 1.0 / dtot),
                     waits=[(s_mm, t_x), (s_fr, t_sq)], fence=True)
        f = self.vop(lambda e: e.tensor_scalar_mul(row(VROW), self.P8[0:1, QT:2 * QT], 1.0 / dtot))
        f = self.vop(lambda e: e.tensor_mul(row(RROW), row(MROW), row(MROW)), fence=True)
        f = self.vop(lambda e: e.tensor_sub(row(VROW), row(VROW), row(RROW)), fence=True)
        f = self.vop(lambda e: e.tensor_scalar_add(row(VROW), row(VROW), EPS), fence=True)
        fs = self.sop(lambda e: e.activation(row(RROW), row(VROW), AF.Sqrt),
                      waits=[(s_ve, f)], fence=True)
        f = self.vop(lambda e: e.reciprocal(row(RROW), row(RROW)), waits=[(s_se, fs)], fence=True)
        for ts in range(2):
            self.mm(lambda e, ts=ts: e.matmul(
                self.P8[:, 2048 + ts * 512: 2048 + (ts + 1) * 512], self.ones_row_f[:],
                self.srow[0:1, MROW * QT + ts * 512: MROW * QT + (ts + 1) * 512],
                start=True, stop=True),
                waits=[(s_ve, f)] if ts == 0 else [], inc=(ts == 1))
        tb = s_mm.n
        fm = self.vop(lambda e: e.tensor_copy(self.m_rep[:], self.P8[:, 2048:2048 + QT]),
                      waits=[(s_mm, tb)])
        for ts in range(2):
            self.mm(lambda e, ts=ts: e.matmul(
                self.P8[:, 2048 + ts * 512: 2048 + (ts + 1) * 512], self.ones_row_f[:],
                self.srow[0:1, RROW * QT + ts * 512: RROW * QT + (ts + 1) * 512],
                start=True, stop=True),
                waits=[(s_ve, fm)] if ts == 0 else [], inc=(ts == 1))
        tb = s_mm.n
        f = self.vop(lambda e: e.tensor_copy(self.r_rep[:], self.P8[:, 2048:2048 + QT]),
                     waits=[(s_mm, tb)])
        for k in range(nk):
            kw = 128 if k < nk - 1 else last_kw
            self.vop(lambda e, k=k, kw=kw: e.tensor_sub(
                self.nrm[0:kw, 0:QT], src[0:kw, k * QT:k * QT + QT], self.m_rep[0:kw, :]),
                fence=True)
            if wcol is None:
                f = self.vop(lambda e, k=k, kw=kw: e.tensor_mul(
                    src[0:kw, k * QT:k * QT + QT], self.nrm[0:kw, 0:QT], self.r_rep[0:kw, :]),
                    fence=True)
            else:
                self.vop(lambda e, k=k, kw=kw: e.tensor_mul(
                    self.nrm[0:kw, 0:QT], self.nrm[0:kw, 0:QT], self.r_rep[0:kw, :]),
                    fence=True)
                f = self.vop(lambda e, k=k, kw=kw, wcol=wcol, bcol=bcol: e.tensor_scalar(
                    src[0:kw, k * QT:k * QT + QT], self.nrm[0:kw, 0:QT],
                    wcol[0:kw, k:k + 1], bcol[0:kw, k:k + 1], ALU.mult, ALU.add),
                    fence=True)
        return f

    def proj(self, w_dram, kdim, mdim, rhs, evac, rhs_ready):
        s_mm, s_w = self.s_mm, self.s_w
        nk = (kdim + 127) // 128
        nm = (mdim + 127) // 128
        for mc in range(nm):
            mw = min(128, mdim - mc * 128)
            slot = mc % 2
            w = []
            if slot in self.last_wring:
                w.append((s_mm, self.last_wring[slot]))
            nkf = kdim // 128
            wt = self.dma(
                "gpsimd",
                self.wring[:, slot * 2048: slot * 2048 + nkf * 128].rearrange(
                    "p (k m) -> p k m", m=128)[:, :, 0:mw],
                w_dram[0:nkf * 128, mc * 128:mc * 128 + mw].rearrange(
                    "(k p) m -> p k m", p=128),
                s_w, waits=w)
            if kdim % 128:
                kw = kdim % 128
                wt = self.dma("gpsimd",
                              self.wring[0:kw, (slot * 16 + nkf) * 128:(slot * 16 + nkf) * 128 + mw],
                              w_dram[nkf * 128:kdim, mc * 128:mc * 128 + mw],
                              s_w, waits=[])
            for k in range(nk):
                kw = min(128, kdim - k * 128)
                rap = rhs(k)
                for ts in range(2):
                    wl = []
                    if k == 0 and ts == 0:
                        wl.append((s_w, wt))
                        if slot in self.last_psum:
                            wl.append(self.last_psum[slot])
                        if mc == 0 and rhs_ready is not None:
                            wl.append(rhs_ready)
                    self.mm(lambda e, k=k, kw=kw, ts=ts, slot=slot, mw=mw, rap=rap: e.matmul(
                        self.P8[0:mw, slot * 1024 + ts * 512: slot * 1024 + (ts + 1) * 512],
                        self.wring[0:kw, (slot * 16 + k) * 128:(slot * 16 + k) * 128 + mw],
                        rap[:, ts * 512:(ts + 1) * 512],
                        start=(k == 0), stop=(k == nk - 1)),
                        waits=wl, inc=(k == nk - 1 and ts == 1))
            t_mm = s_mm.n
            self.last_wring[slot] = t_mm
            self.last_psum[slot] = evac(mc, mw, t_mm, slot)

    def evac_store(self, out_dram, row0, mc, mw, t_mm, slot):
        sslot = mc % 2
        w = [(self.s_mm, t_mm)]
        if sslot in self.stage_busy:
            w.append((self.s_st, self.stage_busy[sslot]))
        f = self.vop(lambda e, sslot=sslot, slot=slot, mw=mw: e.tensor_copy(
            self.stage[0:mw, sslot * QT: sslot * QT + QT],
            self.P8[0:mw, slot * 1024: slot * 1024 + QT]), waits=w, fence=True)
        st = self.dma("sync", out_dram[row0 + mc * 128: row0 + mc * 128 + mw, :],
                      self.stage[0:mw, sslot * QT: sslot * QT + QT], self.s_st,
                      waits=[(self.s_ve, f)])
        self.stage_busy[sslot] = st
        return (self.s_ve, f)

    def build(self):
        pr = self.pr
        s_ld, s_mm, s_ve, s_se, s_st = self.s_ld, self.s_mm, self.s_ve, self.s_se, self.s_st
        for k in range(16):
            self.dma("gpsimd", self.xb[:, k * QT:(k + 1) * QT],
                     self.x1f[k * 128:(k + 1) * 128, :], s_ld)
        for dst, src, cast in (
            (self.cos_f, self.cosq_d, False), (self.sin_f, self.sinq_d, False),
            (self.prot_b, self.prot_d, True), (self.ident_b, self.ident_d, True),
            (self.ident_f, self.ident_d, False),
            (self.ones_row_f, self.ones_row_d, False), (self.ones_col_b, self.ones_col_d, True),
            (self.qlw, self.qlw_d, False), (self.qlb, self.qlb_d, False),
            (self.kvlw, self.kvlw_d, False), (self.kvlb, self.kvlb_d, False),
        ):
            self.dma("gpsimd" if cast else "sync", dst[:], src, s_ld)
        pr.barrier([s_ld])

        t_xn = self.ln_fm(self.xb, 16, 128, D)
        for k in range(16):
            self.dma("sync", self.xnout[k * 128:(k + 1) * 128, :], self.xb[:, k * QT:(k + 1) * QT],
                     s_st, waits=[(s_ve, t_xn)] if k == 0 else [])

        rhs_x = lambda k: self.xb[:, k * QT:(k + 1) * QT]
        rhs_cq = lambda k: self.cqn[:, k * QT:(k + 1) * QT]
        def rhs_ckv(k):
            kw = 128 if k < KV_KC - 1 else KV_LAST
            return self.ckvn[0:kw, k * QT:k * QT + QT]

        def evac_cq(mc, mw, t_mm, slot):
            f = self.vop(lambda e, mc=mc, mw=mw, slot=slot: e.tensor_copy(
                self.cqn[0:mw, mc * QT:mc * QT + QT],
                self.P8[0:mw, slot * 1024: slot * 1024 + QT]),
                waits=[(s_mm, t_mm)], fence=True)
            return (s_ve, f)
        self.proj(self.wdq_d, D, Q_PROJ, rhs_x, evac_cq, (s_ve, t_xn))
        t_cq = self.ln_fm(self.cqn, 8, 128, Q_PROJ, wcol=self.qlw, bcol=self.qlb)

        def evac_ckv(mc, mw, t_mm, slot):
            f = self.vop(lambda e, mc=mc, mw=mw, slot=slot: e.tensor_copy(
                self.ckvn[0:mw, mc * QT:mc * QT + QT],
                self.P8[0:mw, slot * 1024: slot * 1024 + QT]),
                waits=[(s_mm, t_mm)], fence=True)
            return (s_ve, f)
        self.proj(self.wdkv_d, D, KV_PROJ, rhs_x, evac_ckv, (s_ve, t_cq))
        t_ckv = self.ln_fm(self.ckvn, KV_KC, KV_LAST, KV_PROJ, wcol=self.kvlw, bcol=self.kvlb)

        self.proj(self.wuq_d, Q_PROJ, K_NOPE, rhs_cq,
                  lambda mc, mw, t, sl: self.evac_store(self.qout, 0, mc, mw, t, sl),
                  (s_ve, t_cq))

        def evac_qr(mc, mw, t_mm, slot):
            f = self.vop(lambda e, mc=mc, slot=slot: e.tensor_copy(
                self.qr_raw[:, mc * QT:(mc + 1) * QT],
                self.P8[0:128, slot * 1024: slot * 1024 + QT]),
                waits=[(self.s_mm, t_mm), (self.s_st, self.ropet_busy)], fence=True)
            return (s_ve, f)
        self.proj(self.wqr_d, Q_PROJ, 256, rhs_cq, evac_qr, None)
        self._rope(self.qout, 1792)

        self.proj(self.wkr_d, D, 256, rhs_x, evac_qr, None)
        self._rope(self.kout, 1792)

        def evac_kv(mc, mw, t_mm, slot):
            if mc < 14:
                return self.evac_store(self.kout, 0, mc, mw, t_mm, slot)
            f = self.vop(lambda e, mc=mc, slot=slot: e.tensor_copy(
                self.stage[:, (mc % 2) * QT:(mc % 2) * QT + QT],
                self.P8[0:128, slot * 1024: slot * 1024 + QT]),
                waits=[(s_mm, t_mm)] + ([(s_st, self.stage_busy[mc % 2])] if (mc % 2) in self.stage_busy else []),
                fence=True)
            fv = None
            for tq in range(8):
                wl = [(s_ve, f if tq == 0 else fv)]
                tt = self.mm(lambda e, mc=mc, tq=tq: e.transpose(
                    self.PT[:, 0:128],
                    self.stage[:, (mc % 2) * QT + tq * 128:(mc % 2) * QT + (tq + 1) * 128],
                    self.ident_b[:]), waits=wl, inc=True)
                fv = self.vop(lambda e, tq=tq: e.tensor_copy(
                    self.rope_t[:, tq * 128:(tq + 1) * 128], self.PT[:, 0:128]),
                    waits=[(s_mm, tt), (s_st, self.ropet_busy)] if tq == 0 else [(s_mm, tt)],
                    fence=True)
            vc = mc - 14
            st = self.dma("sync",
                          self.vout[:, vc * 128:(vc + 1) * 128].rearrange("(tq p) j -> p tq j", p=128),
                          self.rope_t[:, 0:QT].rearrange("p (tq j) -> p tq j", j=128),
                          s_st, waits=[(s_ve, fv)])
            self.ropet_busy = st
            self.stage_busy[mc % 2] = st
            return (s_ve, fv)
        self.proj(self.wukv_d, KV_PROJ, UKV_OUT, rhs_ckv, evac_kv, (s_ve, t_ckv))

        wsl_g = 0
        wg = []
        if wsl_g in self.last_wring:
            wg.append((self.s_mm, self.last_wring[wsl_g]))
        wgt = self.dma("gpsimd",
                       self.wring[:, 0:16 * 32].rearrange("p (k m) -> p k m", k=16),
                       self.wgate_d.rearrange("(k p) m -> p k m", p=128),
                       self.s_w, waits=wg)
        for k in range(16):
            for ts in range(2):
                wl = []
                if k == 0 and ts == 0:
                    wl = [(self.s_w, wgt), (s_ve, s_ve.n), (s_se, s_se.n)]
                self.mm(lambda e, k=k, ts=ts: e.matmul(
                    self.P8[0:32, 2048 + ts * 512: 2048 + (ts + 1) * 512],
                    self.wring[0:128, k * 32:(k + 1) * 32],
                    self.xb[:, k * QT + ts * 512: k * QT + (ts + 1) * 512],
                    start=(k == 0), stop=(k == 15)), waits=wl, inc=(k == 15 and ts == 1))
        t_lg = s_mm.n
        self.last_wring[0] = t_lg
        f = self.vop(lambda e: e.tensor_copy(self.lgs[:], self.P8[0:32, 2048:2048 + QT]),
                     waits=[(s_mm, t_lg)], fence=True)
        col = lambda i: self.ccol[:, i:i + 1]
        wrk = lambda i: self.cwrk[:, i * 32:(i + 1) * 32]
        MXC, A4C, SC, DENC = 0, 1, 2, 3
        PW, WW, SEL, INV = 0, 1, 2, 3
        for tc in range(QT // 128):
            tt = self.mm(lambda e, tc=tc: e.transpose(
                self.P8[:, 3072:3104], self.lgs[:, tc * 128:(tc + 1) * 128],
                self.ident_f[0:32, 0:32]), waits=[(s_ve, s_ve.n)], inc=True)
            f = self.vop(lambda e: e.reduce_max(col(MXC), self.P8[:, 3072:3104], axis=AX.X),
                         waits=[(s_mm, tt)], fence=True)
            f = self.vop(lambda e: e.tensor_scalar_mul(col(MXC), col(MXC), -1.0), fence=True)
            fs = self.sop(lambda e: e.activation(wrk(PW), self.P8[:, 3072:3104], AF.Exp,
                                                 bias=col(MXC), accum_out=col(SC)),
                          waits=[(s_ve, f)], fence=True)
            f = self.vop(lambda e: e.tensor_copy(wrk(WW), wrk(PW)), waits=[(s_se, fs)], fence=True)
            f = self.vop(lambda e: e.memset(wrk(SEL), 0.0), fence=True)
            f = self.vop(lambda e: e.memset(col(A4C), 0.0), fence=True)
            for r4 in range(4):
                f = self.vop(lambda e: e.reduce_max(col(DENC), wrk(WW), axis=AX.X), fence=True)
                f = self.vop(lambda e: e.tensor_single_scalar(wrk(INV), wrk(WW), col(DENC),
                                                              ALU.is_ge), fence=True)
                f = self.vop(lambda e: e.tensor_add(wrk(SEL), wrk(SEL), wrk(INV)), fence=True)
                f = self.vop(lambda e: e.tensor_add(col(A4C), col(A4C), col(DENC)), fence=True)
                if r4 < 3:
                    f = self.vop(lambda e: e.tensor_scalar(wrk(INV), wrk(INV), -1.0, 1.0,
                                                           ALU.mult, ALU.add), fence=True)
                    f = self.vop(lambda e: e.tensor_mul(wrk(WW), wrk(WW), wrk(INV)), fence=True)
            f = self.vop(lambda e: e.scalar_tensor_tensor(col(DENC), col(SC), 1e-20, col(A4C),
                                                          ALU.mult, ALU.add), fence=True)
            f = self.vop(lambda e: e.reciprocal(col(DENC), col(DENC)), fence=True)
            f = self.vop(lambda e: e.tensor_mul(wrk(PW), wrk(PW), wrk(SEL)), fence=True)
            f = self.vop(lambda e: e.tensor_scalar(wrk(PW), wrk(PW), col(DENC), None, ALU.mult),
                         fence=True)
            tt2 = self.mm(lambda e: e.transpose(self.P8[0:32, 3104:3232], self.cwrk[:, 0:32],
                                                self.ident_f[:]), waits=[(s_ve, f)], inc=True)
            f = self.vop(lambda e, tc=tc: e.tensor_copy(
                self.comb_fm[:, tc * 128:(tc + 1) * 128], self.P8[0:32, 3104:3232]),
                waits=[(s_mm, tt2)], fence=True)
        self.dma("sync", self.combo, self.comb_fm[:], s_st, waits=[(s_ve, s_ve.n)])
        pr.wait("gpsimd", s_st)
        pr.wait("sync", s_st)

    def _rope(self, out_dram, row0):
        s_mm, s_ve = self.s_mm, self.s_ve
        for hh in range(2):
            fprev = None
            for half in range(2):
                tt = self.mm(lambda e, hh=hh, half=half: e.matmul(
                    self.P8[:, 3072:3584], self.prot_b[:],
                    self.qr_raw[:, hh * QT + half * 512: hh * QT + (half + 1) * 512],
                    start=True, stop=True),
                    waits=[(s_ve, self.s_ve.n)], inc=True)
                f = self.vop(lambda e, hh=hh, half=half: e.tensor_mul(
                    self.rope_t[:, half * 512:(half + 1) * 512],
                    self.qr_raw[:, hh * QT + half * 512: hh * QT + (half + 1) * 512],
                    self.cos_f[:, half * 512:(half + 1) * 512]),
                    waits=[(s_mm, tt), (self.s_st, self.ropet_busy)], fence=True)
                f = self.vop(lambda e, half=half: e.tensor_mul(
                    self.rope_t[:, QT + half * 512: QT + (half + 1) * 512],
                    self.P8[:, 3072:3584], self.sin_f[:, half * 512:(half + 1) * 512]),
                    fence=True)
                f = self.vop(lambda e, half=half: e.tensor_add(
                    self.rope_t[:, half * 512:(half + 1) * 512],
                    self.rope_t[:, half * 512:(half + 1) * 512],
                    self.rope_t[:, QT + half * 512: QT + (half + 1) * 512]), fence=True)
                fprev = f
            st = self.dma("sync", out_dram[row0 + hh * 128: row0 + (hh + 1) * 128, :],
                          self.rope_t[:, 0:QT], self.s_st, waits=[(s_ve, fprev)])
            self.ropet_busy = st


def build_l1b_program():
    nc = bass.Bass("TRN2", target_bir_lowering=False)
    b = L1B(nc)
    b.build()
    b.pr.finish(nc)
    b.ctx.close()
    return nc


def rope_tables():
    freqs = 1.0 / 10000.0 ** (np.arange(0, DH, 2) / DH)
    pos = np.arange(S)
    ang = pos[:, None] * freqs[None, :]
    cos = np.concatenate([np.cos(ang), np.cos(ang)], -1)
    sin = np.concatenate([np.sin(ang), np.sin(ang)], -1)
    return cos.astype(np.float32), sin.astype(np.float32)


def own_token_cols(core):
    lane = core % 4
    idx = []
    for j in range(8):
        p = 128 * (lane + 4 * j)
        idx.extend(range(p, p + 128))
    return np.array(idx)


def l1b_in_maps(inp, x1_fm_full):
    wgate_b = np.ascontiguousarray(inp["L1_ffn_gate_w"].astype(np.float32).T)
    wdq = np.ascontiguousarray(inp["L1_attn_W_dq"].astype(np.float32))
    wuq = np.ascontiguousarray(inp["L1_attn_W_uq"].astype(np.float32))
    wqr = np.ascontiguousarray(inp["L1_attn_W_qr"].astype(np.float32))
    wdkv = np.ascontiguousarray(inp["L1_attn_W_dkv"].astype(np.float32))
    wukv = np.ascontiguousarray(inp["L1_attn_W_ukv"].astype(np.float32))
    wkr = np.ascontiguousarray(inp["L1_attn_W_kr"].astype(np.float32))
    pad = lambda v, n: np.concatenate([np.asarray(v, np.float32), np.zeros(n - len(v), np.float32)])
    qlw = np.ascontiguousarray(inp["L1_attn_q_ln_w"].astype(np.float32).reshape(8, 128).T)
    qlb = np.ascontiguousarray(inp["L1_attn_q_ln_b"].astype(np.float32).reshape(8, 128).T)
    kvlw = np.ascontiguousarray(pad(inp["L1_attn_kv_ln_w"], KV_KC * 128).reshape(KV_KC, 128).T)
    kvlb = np.ascontiguousarray(pad(inp["L1_attn_kv_ln_b"], KV_KC * 128).reshape(KV_KC, 128).T)
    cos, sin = rope_tables()
    prot = np.zeros((128, 128), np.float32)
    for m in range(64):
        prot[m + 64, m] = -1.0
        prot[m, m + 64] = 1.0
    ident = np.eye(128, dtype=np.float32)
    ones_row = np.ones((1, 128), np.float32)
    ones_col = np.ones((128, 1), np.float32)

    maps = []
    for c in range(8):
        bc = c // 4
        cols = own_token_cols(c)
        maps.append({
            "x1f": np.ascontiguousarray(x1_fm_full[bc][:, cols]),
            "wdq": wdq, "wuq": wuq, "wqr": wqr, "wdkv": wdkv, "wukv": wukv, "wkr": wkr,
            "wgate": wgate_b,
            "qlw": qlw, "qlb": qlb, "kvlw": kvlw, "kvlb": kvlb,
            "cosq": np.ascontiguousarray(cos[cols].T),
            "sinq": np.ascontiguousarray(sin[cols].T),
            "prot": prot, "ident": ident, "ones_row": ones_row, "ones_col": ones_col,
        })
    return maps


class L1C1:
    NHC = 4
    QG = 512

    def __init__(self, nc):
        self.nc = nc
        self.ctx = ExitStack()
        self.pr = Prog()
        nc_ = nc
        def din(name, shape, dt=BF16):
            return nc_.dram_tensor(name, list(shape), dt, kind="ExternalInput").ap()
        self.qd = din("qallc", [512, S])
        self.kd = din("kallc", [512, S])
        self.vd = din("vallc", [S, 512])
        self.dmask_d = din("dmask", [128, 4 * 512])
        self.ones_row_d = din("ones_row", [1, 128], F32)
        self.ones_col_d = din("ones_col", [128, 1], F32)
        self.aout = nc_.dram_tensor("aout", [512, S], BF16, kind="ExternalOutput").ap()
        ctx = self.ctx
        sbuf = lambda name, shape, dt=BF16: ctx.enter_context(nc_.sbuf_tensor(name, list(shape), dt))
        self.qt = sbuf("z_qt", [128, 2 * 512])
        self.kt = sbuf("z_kt", [128, 2 * 512])
        self.vt = sbuf("z_vt", [128, 2 * 512])
        self.pt = sbuf("z_pt", [128, 2 * 512])
        self.dmask = sbuf("z_dmask", [128, 4 * 512])
        self.lrow = sbuf("z_lrow", [1, 512], F32)
        self.lrep = sbuf("z_lrep", [128, 512], F32)
        self.stage = sbuf("z_stage", [128, 2 * 512])
        self.ones_row_f = sbuf("z_onesrowf", [1, 128], F32)
        self.ones_col_b = sbuf("z_onescolb", [128, 1])
        self.P8 = ctx.enter_context(nc_.psum_tensor("P8", [128, 8 * 512], F32))
        ss = lambda name: Sem(nc_, ctx, name)
        self.s_ld = ss("s_ld")
        self.s_q = ss("s_q")
        self.s_kv = ss("s_kv")
        self.s_mm = ss("s_mm")
        self.s_ve = ss("s_ve")
        self.s_se = ss("s_se")
        self.s_st = ss("s_st")

    def dma(self, eng, out_ap, in_ap, sem, waits=()):
        return self.pr.op(eng, lambda e, o=out_ap, i=in_ap: e.dma_start(out=o, in_=i),
                          waits=waits, inc=sem, inc_k=16)

    def vop(self, fn, waits=(), fence=False):
        w = list(waits)
        if fence and self.s_ve.n > 0:
            w.append((self.s_ve, self.s_ve.n))
        return self.pr.op("vector", fn, waits=w, inc=self.s_ve)

    def sop(self, fn, waits=(), fence=False):
        w = list(waits)
        if fence and self.s_se.n > 0:
            w.append((self.s_se, self.s_se.n))
        return self.pr.op("scalar", fn, waits=w, inc=self.s_se)

    def mm(self, fn, waits=(), inc=False):
        return self.pr.op("tensor", fn, waits=waits, inc=self.s_mm if inc else None)

    def build(self):
        pr = self.pr
        s_q, s_kv, s_mm, s_ve, s_se, s_st = (self.s_q, self.s_kv, self.s_mm,
                                             self.s_ve, self.s_se, self.s_st)
        self.dma("sync", self.dmask[:], self.dmask_d, self.s_ld)
        self.dma("sync", self.ones_row_f[:], self.ones_row_d, self.s_ld)
        self.dma("gpsimd", self.ones_col_b[:], self.ones_col_d, self.s_ld)
        pr.barrier([self.s_ld])
        q_cons = {}
        kv_cons = {}
        kv_loaded = {}
        s_cons = {}
        p_cons = {}
        acc_free = 0
        lsum_free = 0
        lrep_free = 0
        stage_busy = {}
        units = [(h, g) for h in range(self.NHC) for g in range(8)]
        for ui, (h, g) in enumerate(units):
            qslot = ui % 2
            wq = [(s_mm, q_cons[qslot])] if qslot in q_cons else []
            tq = self.dma("gpsimd", self.qt[:, qslot * 512:(qslot + 1) * 512],
                          self.qd[h * 128:(h + 1) * 128, g * 512:(g + 1) * 512], s_q,
                          waits=wq)
            nkc = 4 * (g + 1)
            tls = None

            def load_group(kg, waits):
                gs_ = kg % 2
                self.dma("sync", self.kt[:, gs_ * 512:(gs_ + 1) * 512],
                         self.kd[h * 128:(h + 1) * 128, kg * 512:(kg + 1) * 512],
                         s_kv, waits=waits)
                kv_loaded[kg] = self.dma(
                    "sync",
                    self.vt[:, gs_ * 512:(gs_ + 1) * 512].rearrange("p (k c) -> p k c", k=4),
                    self.vd[kg * 512:(kg + 1) * 512, h * 128:(h + 1) * 128].rearrange(
                        "(k p) c -> p k c", p=128),
                    s_kv)

            for kg in range(min(2, g + 1)):
                gs_ = kg % 2
                load_group(kg, [(s_mm, kv_cons[gs_])] if gs_ in kv_cons else [])
            for kc in range(nkc):
                gslot = (kc // 4) % 2
                koff = gslot * 512 + (kc % 4) * 128
                tkv = kv_loaded[kc // 4]
                sslot = kc % 2
                wl = [(s_kv, tkv), (s_q, tq)]
                if sslot in s_cons:
                    wl.append((s_se, s_cons[sslot]))
                tsm = self.mm(lambda e, sslot=sslot, koff=koff, qslot=qslot: e.matmul(
                    self.P8[:, sslot * 512:(sslot + 1) * 512],
                    self.kt[:, koff:koff + 128],
                    self.qt[:, qslot * 512:(qslot + 1) * 512],
                    start=True, stop=True), waits=wl, inc=True)
                wexp = [(s_mm, tsm)]
                if sslot in p_cons:
                    wexp.append((s_mm, p_cons[sslot][0]))
                    if p_cons[sslot][1]:
                        wexp.append((s_ve, p_cons[sslot][1]))
                fe = self.sop(lambda e, sslot=sslot: e.activation(
                    self.pt[:, sslot * 512:(sslot + 1) * 512],
                    self.P8[:, sslot * 512:(sslot + 1) * 512],
                    AF.Exp, bias=0.0, scale=ISQ), waits=wexp)
                s_cons[sslot] = fe
                fmask = 0
                if kc >= 4 * g:
                    dm = kc - 4 * g
                    fmask = self.vop(lambda e, sslot=sslot, dm=dm: e.tensor_mul(
                        self.pt[:, sslot * 512:(sslot + 1) * 512],
                        self.pt[:, sslot * 512:(sslot + 1) * 512],
                        self.dmask[:, dm * 512:(dm + 1) * 512]),
                        waits=[(s_se, fe)], fence=True)
                wav = [(s_se, fe)]
                if fmask:
                    wav.append((s_ve, fmask))
                if kc == 0:
                    if acc_free:
                        wav.append((s_ve, acc_free))
                    if lsum_free:
                        wav.append((s_ve, lsum_free))
                self.mm(lambda e, sslot=sslot, koff=koff, kc=kc, nkc=nkc: e.matmul(
                    self.P8[:, 2 * 512:3 * 512],
                    self.vt[:, koff:koff + 128],
                    self.pt[:, sslot * 512:(sslot + 1) * 512],
                    start=(kc == 0), stop=(kc == nkc - 1)), waits=wav)
                tls = self.mm(lambda e, sslot=sslot, kc=kc, nkc=nkc: e.matmul(
                    self.P8[0:1, 3 * 512:3 * 512 + 512],
                    self.ones_col_b[:],
                    self.pt[:, sslot * 512:(sslot + 1) * 512],
                    start=(kc == 0), stop=(kc == nkc - 1)), inc=True)
                p_cons[sslot] = (tls, fmask)
                kv_cons[gslot] = tls
                if kc % 4 == 3 and kc // 4 + 2 <= g:
                    load_group(kc // 4 + 2, [(s_mm, tls)])
            q_cons[qslot] = tls
            f1 = self.vop(lambda e: e.tensor_copy(self.lrow[:], self.P8[0:1, 3 * 512:4 * 512]),
                          waits=[(s_mm, tls)], fence=True)
            lsum_free = f1
            f2 = self.vop(lambda e: e.reciprocal(self.lrow[:], self.lrow[:]), fence=True)
            wtb = [(s_ve, f2)]
            if lrep_free:
                wtb.append((s_ve, lrep_free))
            tb = self.mm(lambda e: e.matmul(
                self.P8[:, 4 * 512:5 * 512], self.ones_row_f[:], self.lrow[:],
                start=True, stop=True), waits=wtb, inc=True)
            f3 = self.vop(lambda e: e.tensor_copy(self.lrep[:], self.P8[:, 4 * 512:5 * 512]),
                          waits=[(s_mm, tb)], fence=True)
            lrep_free = f3
            sslot2 = ui % 2
            wst = [(self.s_st, stage_busy[sslot2])] if sslot2 in stage_busy else []
            f4 = self.vop(lambda e, sslot2=sslot2: e.tensor_mul(
                self.stage[:, sslot2 * 512:(sslot2 + 1) * 512],
                self.P8[:, 2 * 512:3 * 512], self.lrep[:]),
                waits=wst, fence=True)
            acc_free = f4
            st = self.dma("sync", self.aout[h * 128:(h + 1) * 128, g * 512:(g + 1) * 512],
                          self.stage[:, sslot2 * 512:(sslot2 + 1) * 512], s_st,
                          waits=[(s_ve, f4)])
            stage_busy[sslot2] = st
        pr.wait("gpsimd", s_st)
        pr.wait("sync", s_st)


def build_l1c1_program():
    nc = bass.Bass("TRN2", target_bir_lowering=False)
    c = L1C1(nc)
    c.build()
    c.pr.finish(nc)
    c.ctx.close()
    return nc


def l1c1_in_maps(q_fm, k_fm, v_tm):
    import ml_dtypes
    dmask = np.zeros((128, 4 * 512), np.float32)
    for kc in range(4):
        for jt in range(4):
            for kr in range(128):
                lo = kc * 128 + kr - jt * 128
                if lo <= 0:
                    dmask[kr, kc * 512 + jt * 128: kc * 512 + (jt + 1) * 128] = 1.0
                elif lo < 128:
                    dmask[kr, kc * 512 + jt * 128 + lo: kc * 512 + (jt + 1) * 128] = 1.0
    dmask = dmask.astype(ml_dtypes.bfloat16)
    ones_row = np.ones((1, 128), np.float32)
    ones_col = np.ones((128, 1), np.float32)
    maps = []
    for c in range(8):
        bc = c // 4
        h0 = (c % 4) * 4
        maps.append({
            "qallc": np.ascontiguousarray(q_fm[bc][h0 * 128:(h0 + 4) * 128, :]),
            "kallc": np.ascontiguousarray(k_fm[bc][h0 * 128:(h0 + 4) * 128, :]),
            "vallc": np.ascontiguousarray(v_tm[bc][:, h0 * 128:(h0 + 4) * 128]),
            "dmask": dmask, "ones_row": ones_row, "ones_col": ones_col,
        })
    return maps


NEXP = 32
DMOE = 768
DSH = 4096


class WRing:

    def __init__(self, outer, tensor, nslots, stride=2048):
        self.o = outer
        self.t = tensor
        self.n = nslots
        self.stride = stride
        self.rr = 0
        self.busy = {}

    def load(self, w_dram, k0, nkk, mc, mw, extra=()):
        slot = self.rr
        self.rr = (self.rr + 1) % self.n
        w = list(extra)
        if slot in self.busy:
            w.append((self.o.s_mm, self.busy[slot]))
        tv = self.o.dma(
            "gpsimd",
            self.t[:, slot * self.stride: slot * self.stride + nkk * mw].rearrange("p (k m) -> p k m", k=nkk),
            w_dram[k0 * 128:(k0 + nkk) * 128, mc * 128:mc * 128 + mw].rearrange(
                "(k p) m -> p k m", p=128),
            self.o.s_w, waits=w)
        return slot, tv

    def tile(self, slot, k, mw):
        return self.t[0:128, slot * self.stride + k * mw: slot * self.stride + (k + 1) * mw]

    def used(self, slot, t_mm):
        self.busy[slot] = t_mm


class L1C2:
    def __init__(self, nc):
        self.nc = nc
        self.ctx = ExitStack()
        self.pr = Prog()
        self._declare()
        self._alloc()

    def _declare(self):
        nc = self.nc
        def din(name, shape, dt=F32):
            return nc.dram_tensor(name, list(shape), dt, kind="ExternalInput").ap()
        self.attn_d = din("attnTc", [2048, QT], BF16)
        self.xn1_d = din("xn1c", [D, QT], BF16)
        self.x1_d = din("x1own", [D, QT])
        self.wo_d = din("wo", [2048, 2048])
        self.wgate_d = din("wgate", [D, NEXP])
        self.wsg_d = din("wsg", [D, DSH])
        self.wsu_d = din("wsu", [D, DSH])
        self.wsd_d = din("wsd", [DSH, D])
        self.wseg_d = din("wseg", [D, 128])
        self.gupt_d = din("gupt", [NEXP, D, 2 * DMOE])
        self.gdnt_d = din("gdnt", [NEXP, DMOE, D])
        self.e32_d = din("e32sel", [32, 32 * 128])
        self.ident_d = din("ident", [128, 128])
        self.ones_row_d = din("ones_row", [1, 128])
        self.ones_col_d = din("ones_col", [128, 1])
        self.outp = nc.dram_tensor("outp", [QT, D], F32, kind="ExternalOutput").ap()
        self.dbg = {}
        if getattr(L1C2, "DEBUG", False):
            for name in ("dacc_wo", "dacc_sh", "dacc_ex"):
                self.dbg[name] = nc.dram_tensor(name, [D, QT], F32, kind="ExternalOutput").ap()
            self.dbg["dcomb"] = nc.dram_tensor("dcomb", [32, QT], F32, kind="ExternalOutput").ap()

    def _alloc(self):
        nc, ctx = self.nc, self.ctx
        sbuf = lambda name, shape, dt=F32: ctx.enter_context(nc.sbuf_tensor(name, list(shape), dt))
        self.xn1 = sbuf("z_xn1", [128, 16 * QT], BF16)
        self.acc = sbuf("z_acc", [128, 16 * QT], F32)
        self.uni = sbuf("z_uni", [128, 16 * QT], BF16)
        self.wra = sbuf("z_wra", [128, 2 * 16 * 128], BF16)
        self.wrb = sbuf("z_wrb", [128, 2 * 16 * 128], BF16)
        self.wrd = sbuf("z_wrd", [128, 2 * 6 * 128], BF16)
        self.crep = sbuf("z_crep", [128, 2 * QT], F32)
        self.grep = sbuf("z_grep", [128, QT], F32)
        self.stage = sbuf("z_stage", [128, 2048], F32)
        self.x1r = sbuf("z_x1r", [128, 2 * QT], F32)
        self.lgs = sbuf("z_lgs", [32, QT], F32)
        self.comb_fm = sbuf("z_combfm", [32, QT], BF16)
        self.cwrk = sbuf("z_cwrk", [128, 4 * 32], F32)
        self.ccol = sbuf("z_ccol", [128, 4], F32)
        self.grow = sbuf("z_grow", [1, QT], F32)
        self.e32 = sbuf("z_e32", [32, 32 * 128], BF16)
        self.ident_f = sbuf("z_identf", [128, 128], F32)
        self.ones_row_f = sbuf("z_onesrowf", [1, 128], F32)
        self.ones_col_b = sbuf("z_onescolb", [128, 1], BF16)
        self.P8 = ctx.enter_context(nc.psum_tensor("P8", [128, 8 * 512], F32))
        ss = lambda name: Sem(nc, ctx, name)
        self.s_ld = ss("s_ld")
        self.s_w = ss("s_w")
        self.s_mm = ss("s_mm")
        self.s_ve = ss("s_ve")
        self.s_se = ss("s_se")
        self.s_st = ss("s_st")

    def dma(self, eng, out_ap, in_ap, sem, waits=()):
        return self.pr.op(eng, lambda e, o=out_ap, i=in_ap: e.dma_start(out=o, in_=i),
                          waits=waits, inc=sem, inc_k=16)

    def vop(self, fn, waits=(), fence=False):
        w = list(waits)
        if fence and self.s_ve.n > 0:
            w.append((self.s_ve, self.s_ve.n))
        return self.pr.op("vector", fn, waits=w, inc=self.s_ve)

    def sop(self, fn, waits=(), fence=False):
        w = list(waits)
        if fence and self.s_se.n > 0:
            w.append((self.s_se, self.s_se.n))
        return self.pr.op("scalar", fn, waits=w, inc=self.s_se)

    def mm(self, fn, waits=(), inc=False):
        return self.pr.op("tensor", fn, waits=waits, inc=self.s_mm if inc else None)

    def build(self):
        pr = self.pr
        s_mm, s_ve, s_se, s_st, s_w = self.s_mm, self.s_ve, self.s_se, self.s_st, self.s_w
        ra = WRing(self, self.wra, 2)
        rb = WRing(self, self.wrb, 2)
        for k in range(16):
            self.dma("gpsimd", self.xn1[:, k * QT:(k + 1) * QT],
                     self.xn1_d[k * 128:(k + 1) * 128, :], self.s_ld)
            self.dma("gpsimd", self.uni[:, k * QT:(k + 1) * QT],
                     self.attn_d[k * 128:(k + 1) * 128, :], self.s_ld)
        for dst, src, cast in ((self.e32, self.e32_d, True), (self.ident_f, self.ident_d, False),
                               (self.ones_row_f, self.ones_row_d, False),
                               (self.ones_col_b, self.ones_col_d, True)):
            self.dma("gpsimd" if cast else "sync", dst[:], src, self.s_ld)
        pr.barrier([self.s_ld])

        psum_free = {}
        for mc in range(16):
            slot = mc % 2
            wsl, wt = ra.load(self.wo_d, 0, 16, mc, 128)
            for k in range(16):
                for ts in range(2):
                    wl = []
                    if k == 0 and ts == 0:
                        wl.append((s_w, wt))
                        if slot in psum_free:
                            wl.append((s_ve, psum_free[slot]))
                    self.mm(lambda e, k=k, ts=ts, slot=slot, wsl=wsl: e.matmul(
                        self.P8[:, slot * 1024 + ts * 512: slot * 1024 + (ts + 1) * 512],
                        ra.tile(wsl, k, 128),
                        self.uni[:, k * QT + ts * 512: k * QT + (ts + 1) * 512],
                        start=(k == 0), stop=(k == 15)), waits=wl, inc=(k == 15 and ts == 1))
            t_mm = s_mm.n
            ra.used(wsl, t_mm)
            f = self.vop(lambda e, mc=mc, slot=slot: e.tensor_copy(
                self.acc[:, mc * QT:(mc + 1) * QT],
                self.P8[0:128, slot * 1024: slot * 1024 + QT]),
                waits=[(s_mm, t_mm)], fence=True)
            psum_free[slot] = f
        t_wo_done = s_ve.n
        if self.dbg:
            for kk in range(16):
                self.dma("sync", self.dbg["dacc_wo"][kk * 128:(kk + 1) * 128, :],
                         self.acc[:, kk * QT:(kk + 1) * QT], self.s_st,
                         waits=[(s_ve, t_wo_done)] if kk == 0 else [])

        wsl, wgt = ra.load(self.wgate_d, 0, 16, 0, 32)
        for k in range(16):
            for ts in range(2):
                wl = []
                if k == 0 and ts == 0:
                    wl = [(s_w, wgt), (s_ve, t_wo_done)]
                self.mm(lambda e, k=k, ts=ts, wsl=wsl: e.matmul(
                    self.P8[0:32, 2048 + ts * 512: 2048 + (ts + 1) * 512],
                    ra.tile(wsl, k, 32),
                    self.xn1[:, k * QT + ts * 512: k * QT + (ts + 1) * 512],
                    start=(k == 0), stop=(k == 15)), waits=wl, inc=(k == 15 and ts == 1))
        t_lg = s_mm.n
        ra.used(wsl, t_lg)
        f = self.vop(lambda e: e.tensor_copy(self.lgs[:], self.P8[0:32, 2048:2048 + QT]),
                     waits=[(s_mm, t_lg)], fence=True)
        MXC, A4C, SC, DENC = 0, 1, 2, 3
        col = lambda i: self.ccol[:, i:i + 1]
        PW, WW, SEL, INV = 0, 1, 2, 3
        wrk = lambda i: self.cwrk[:, i * 32:(i + 1) * 32]
        for tc in range(8):
            tt = self.mm(lambda e, tc=tc: e.transpose(
                self.P8[:, 3072:3104], self.lgs[:, tc * 128:(tc + 1) * 128],
                self.ident_f[0:32, 0:32]), waits=[(s_ve, s_ve.n)], inc=True)
            f = self.vop(lambda e: e.reduce_max(col(MXC), self.P8[:, 3072:3104], axis=AX.X),
                         waits=[(s_mm, tt)], fence=True)
            f = self.vop(lambda e: e.tensor_scalar_mul(col(MXC), col(MXC), -1.0), fence=True)
            fs = self.sop(lambda e: e.activation(wrk(PW), self.P8[:, 3072:3104], AF.Exp,
                                                 bias=col(MXC), accum_out=col(SC)),
                          waits=[(s_ve, f)], fence=True)
            f = self.vop(lambda e: e.tensor_copy(wrk(WW), wrk(PW)), waits=[(s_se, fs)], fence=True)
            f = self.vop(lambda e: e.memset(wrk(SEL), 0.0), fence=True)
            f = self.vop(lambda e: e.memset(col(A4C), 0.0), fence=True)
            for r4 in range(4):
                f = self.vop(lambda e: e.reduce_max(col(DENC), wrk(WW), axis=AX.X), fence=True)
                f = self.vop(lambda e: e.tensor_single_scalar(wrk(INV), wrk(WW), col(DENC),
                                                              ALU.is_ge), fence=True)
                f = self.vop(lambda e: e.tensor_add(wrk(SEL), wrk(SEL), wrk(INV)), fence=True)
                f = self.vop(lambda e: e.tensor_add(col(A4C), col(A4C), col(DENC)), fence=True)
                if r4 < 3:
                    f = self.vop(lambda e: e.tensor_scalar(wrk(INV), wrk(INV), -1.0, 1.0,
                                                           ALU.mult, ALU.add), fence=True)
                    f = self.vop(lambda e: e.tensor_mul(wrk(WW), wrk(WW), wrk(INV)), fence=True)
            f = self.vop(lambda e: e.scalar_tensor_tensor(col(DENC), col(SC), 1e-20, col(A4C),
                                                          ALU.mult, ALU.add), fence=True)
            f = self.vop(lambda e: e.reciprocal(col(DENC), col(DENC)), fence=True)
            f = self.vop(lambda e: e.tensor_mul(wrk(PW), wrk(PW), wrk(SEL)), fence=True)
            f = self.vop(lambda e: e.tensor_scalar(wrk(PW), wrk(PW), col(DENC), None, ALU.mult),
                         fence=True)
            tt2 = self.mm(lambda e: e.transpose(self.P8[0:32, 3584:3712], self.cwrk[:, 0:32],
                                                self.ident_f[:]), waits=[(s_ve, f)], inc=True)
            f = self.vop(lambda e, tc=tc: e.tensor_copy(
                self.comb_fm[:, tc * 128:(tc + 1) * 128], self.P8[0:32, 3584:3712]),
                waits=[(s_mm, tt2)], fence=True)
        t_comb = s_ve.n
        if self.dbg:
            cfm32 = self.vop(lambda e: e.tensor_copy(self.lgs[:], self.comb_fm[:]),
                             waits=[(s_ve, t_comb)], fence=True)
            self.dma("sync", self.dbg["dcomb"], self.lgs[:], self.s_st, waits=[(s_ve, cfm32)])

        wsl, wgt2 = ra.load(self.wseg_d, 0, 16, 0, 128)
        for k in range(16):
            for ts in range(2):
                wl = [(s_w, wgt2)] if (k == 0 and ts == 0) else []
                self.mm(lambda e, k=k, ts=ts, wsl=wsl: e.matmul(
                    self.P8[0:1, 3072 + ts * 512: 3072 + (ts + 1) * 512],
                    ra.tile(wsl, k, 128)[:, 0:1],
                    self.xn1[:, k * QT + ts * 512: k * QT + (ts + 1) * 512],
                    start=(k == 0), stop=(k == 15)), waits=wl, inc=(k == 15 and ts == 1))
        t_seg = s_mm.n
        ra.used(wsl, t_seg)
        fs = self.sop(lambda e: e.activation(self.grow[:], self.P8[0:1, 3072:3072 + QT],
                                             AF.Sigmoid), waits=[(s_mm, t_seg)], fence=True)
        for ts in range(2):
            self.mm(lambda e, ts=ts: e.matmul(
                self.P8[:, 3072 + ts * 512: 3072 + (ts + 1) * 512], self.ones_row_f[:],
                self.grow[0:1, ts * 512:(ts + 1) * 512], start=True, stop=True),
                waits=[(s_se, fs)] if ts == 0 else [], inc=(ts == 1))
        tb = s_mm.n
        f_grep = self.vop(lambda e: e.tensor_copy(self.grep[:], self.P8[:, 3072:3072 + QT]),
                          waits=[(s_mm, tb)], fence=True)

        for th in range(2):
            t0c = th * 512
            for mcp in range(32):
                slot = mcp % 2
                wslg, wg = ra.load(self.wsg_d, 0, 16, mcp, 128)
                wslu, wu = rb.load(self.wsu_d, 0, 16, mcp, 128)
                for k in range(16):
                    wl = []
                    if k == 0:
                        wl = [(s_w, wg)]
                        if (th, slot, "g") in psum_free:
                            wl.append((s_ve, psum_free[(th, slot, "g")]))
                        elif slot in psum_free:
                            wl.append((s_ve, psum_free[slot]))
                        if mcp == 0 and th == 0:
                            wl.append((s_ve, t_comb))
                        if mcp == 0 and th == 1:
                            wl.append((s_ve, self.s_ve.n))
                    self.mm(lambda e, k=k, slot=slot, wslg=wslg, t0c=t0c: e.matmul(
                        self.P8[:, slot * 1024: slot * 1024 + 512],
                        ra.tile(wslg, k, 128),
                        self.xn1[:, k * QT + t0c: k * QT + t0c + 512],
                        start=(k == 0), stop=(k == 15)), waits=wl)
                    wl2 = [(s_w, wu)] if k == 0 else []
                    self.mm(lambda e, k=k, slot=slot, wslu=wslu, t0c=t0c: e.matmul(
                        self.P8[:, slot * 1024 + 512: slot * 1024 + 1024],
                        rb.tile(wslu, k, 128),
                        self.xn1[:, k * QT + t0c: k * QT + t0c + 512],
                        start=(k == 0), stop=(k == 15)), waits=wl2, inc=(k == 15))
                t_gu = s_mm.n
                ra.used(wslg, t_gu)
                rb.used(wslu, t_gu)
                fs = self.sop(lambda e, slot=slot, mcp=mcp: e.activation(
                    self.uni[:, mcp * 512: mcp * 512 + 512],
                    self.P8[:, slot * 1024: slot * 1024 + 512], AF.Silu),
                    waits=[(s_mm, t_gu)], fence=True)
                fv = self.vop(lambda e, slot=slot, mcp=mcp: e.tensor_mul(
                    self.uni[:, mcp * 512: mcp * 512 + 512],
                    self.uni[:, mcp * 512: mcp * 512 + 512],
                    self.P8[:, slot * 1024 + 512: slot * 1024 + 1024]),
                    waits=[(s_se, fs)], fence=True)
                psum_free[(th, slot, "g")] = fv
                psum_free[slot] = fv
            t_sh = s_ve.n
            if not hasattr(self, "edn_free"):
                self.edn_free = {}
            for oc in range(16):
                slot = oc % 2
                used_slots = []
                for kg in range(2):
                    wsld, wd = ra.load(self.wsd_d, kg * 16, 16, oc, 128)
                    used_slots.append(wsld)
                    for k in range(16):
                        mcp = kg * 16 + k
                        wl = []
                        if k == 0:
                            wl = [(s_w, wd)]
                            if kg == 0:
                                wl.append((s_ve, t_sh))
                                if slot in self.edn_free:
                                    wl.append((s_ve, self.edn_free[slot]))
                        self.mm(lambda e, k=k, mcp=mcp, slot=slot, wsld=wsld: e.matmul(
                            self.P8[:, 2048 + slot * 512: 2048 + (slot + 1) * 512],
                            ra.tile(wsld, k, 128),
                            self.uni[:, mcp * 512: mcp * 512 + 512],
                            start=(mcp == 0), stop=(mcp == 31)),
                            waits=wl, inc=(mcp == 31))
                tmm = s_mm.n
                for ws_ in used_slots:
                    ra.used(ws_, tmm)
                f1 = self.vop(lambda e, oc=oc, slot=slot, t0c=t0c: e.tensor_mul(
                    self.stage[:, slot * 512: slot * 512 + 512],
                    self.P8[:, 2048 + slot * 512: 2048 + (slot + 1) * 512],
                    self.grep[:, t0c:t0c + 512]),
                    waits=[(s_mm, tmm), (s_ve, f_grep)], fence=True)
                f2 = self.vop(lambda e, oc=oc, slot=slot, t0c=t0c: e.tensor_add(
                    self.acc[:, oc * QT + t0c: oc * QT + t0c + 512],
                    self.acc[:, oc * QT + t0c: oc * QT + t0c + 512],
                    self.stage[:, slot * 512: slot * 512 + 512]), fence=True)
                self.edn_free[slot] = f2

        if self.dbg:
            vdone = self.s_ve.n
            for kk in range(16):
                self.dma("sync", self.dbg["dacc_sh"][kk * 128:(kk + 1) * 128, :],
                         self.acc[:, kk * QT:(kk + 1) * QT], self.s_st,
                         waits=[(s_ve, vdone)] if kk == 0 else [])

        rd = WRing(self, self.wrd, 2, stride=768)
        crep_free = {}
        for e_ in range(NEXP):
            cslot = e_ % 2
            wl = [(s_ve, crep_free[cslot])] if cslot in crep_free else []
            if 1 in getattr(self, "edn_free", {}):
                wl.append((s_ve, self.edn_free[1]))
            for ts in range(2):
                self.mm(lambda e, e_=e_, ts=ts: e.matmul(
                    self.P8[:, 3072 + ts * 512: 3072 + (ts + 1) * 512],
                    self.e32[:, e_ * 128:(e_ + 1) * 128],
                    self.comb_fm[:, ts * 512:(ts + 1) * 512], start=True, stop=True),
                    waits=wl if ts == 0 else [], inc=(ts == 1))
            tcr = s_mm.n
            fcr = self.vop(lambda e, cslot=cslot: e.tensor_copy(
                self.crep[:, cslot * QT:(cslot + 1) * QT], self.P8[:, 3072:3072 + QT]),
                waits=[(s_mm, tcr)], fence=True)
            for mp in range(6):
                slot = mp % 2
                wslg, wg = ra.load(self.gupt_d[e_], 0, 16, mp, 128)
                wslu, wu = rb.load(self.gupt_d[e_], 0, 16, 6 + mp, 128)
                for k in range(16):
                    for ts in range(2):
                        wl = []
                        if k == 0 and ts == 0:
                            wl = [(s_w, wg)]
                            if ("eg", slot) in psum_free:
                                wl.append((s_ve, psum_free[("eg", slot)]))
                            else:
                                wl.append((s_ve, self.s_ve.n))
                        self.mm(lambda e, k=k, ts=ts, slot=slot, wslg=wslg: e.matmul(
                            self.P8[:, slot * 1024 + ts * 512: slot * 1024 + (ts + 1) * 512],
                            ra.tile(wslg, k, 128),
                            self.xn1[:, k * QT + ts * 512: k * QT + (ts + 1) * 512],
                            start=(k == 0), stop=(k == 15)), waits=wl)
                        wl2 = [(s_w, wu)] if (k == 0 and ts == 0) else []
                        if k == 0 and ts == 0:
                            for sl2 in (0, 1):
                                if sl2 in getattr(self, "edn_free", {}):
                                    wl2.append((s_ve, self.edn_free[sl2]))
                        self.mm(lambda e, k=k, ts=ts, slot=slot, wslu=wslu: e.matmul(
                            self.P8[:, 2048 + slot * 1024 + ts * 512: 2048 + slot * 1024 + (ts + 1) * 512],
                            rb.tile(wslu, k, 128),
                            self.xn1[:, k * QT + ts * 512: k * QT + (ts + 1) * 512],
                            start=(k == 0), stop=(k == 15)), waits=wl2,
                            inc=(k == 15 and ts == 1))
                t_gu = s_mm.n
                ra.used(wslg, t_gu)
                rb.used(wslu, t_gu)
                fs = self.sop(lambda e, slot=slot, mp=mp: e.activation(
                    self.uni[:, mp * QT: mp * QT + QT],
                    self.P8[:, slot * 1024: slot * 1024 + QT], AF.Silu),
                    waits=[(s_mm, t_gu)] + ([(s_mm, self.hconsumed.get((e_ - 1, mp), 0))] if hasattr(self, "hconsumed") and e_ > 0 else []),
                    fence=True)
                fv = self.vop(lambda e, slot=slot, mp=mp: e.tensor_mul(
                    self.uni[:, mp * QT: mp * QT + QT],
                    self.uni[:, mp * QT: mp * QT + QT],
                    self.P8[:, 2048 + slot * 1024: 2048 + slot * 1024 + QT]),
                    waits=[(s_se, fs)], fence=True)
                fv = self.vop(lambda e, mp=mp, cslot=cslot: e.tensor_mul(
                    self.uni[:, mp * QT: mp * QT + QT],
                    self.uni[:, mp * QT: mp * QT + QT],
                    self.crep[:, cslot * QT:(cslot + 1) * QT]),
                    waits=[(s_ve, fcr)], fence=True)
                psum_free[("eg", slot)] = fv
            t_h = s_ve.n
            crep_free[cslot] = t_h
            if not hasattr(self, "hconsumed"):
                self.hconsumed = {}
            for oc in range(16):
                slot = oc % 2
                wsld, wd = rd.load(self.gdnt_d[e_], 0, 6, oc, 128)
                for mp in range(6):
                    for ts in range(2):
                        wl = []
                        if mp == 0 and ts == 0:
                            wl = [(s_w, wd), (s_ve, t_h)]
                            if slot in self.edn_free:
                                wl.append((s_ve, self.edn_free[slot]))
                        self.mm(lambda e, mp=mp, ts=ts, slot=slot, wsld=wsld: e.matmul(
                            self.P8[:, 2048 + slot * 1024 + ts * 512: 2048 + slot * 1024 + (ts + 1) * 512],
                            rd.tile(wsld, mp, 128),
                            self.uni[:, mp * QT + ts * 512: mp * QT + (ts + 1) * 512],
                            start=(mp == 0), stop=(mp == 5)),
                            waits=wl, inc=(mp == 5 and ts == 1))
                tdn = s_mm.n
                rd.used(wsld, tdn)
                for mp in range(6):
                    self.hconsumed[(e_, mp)] = tdn
                f2 = self.vop(lambda e, oc=oc, slot=slot: e.tensor_add(
                    self.acc[:, oc * QT:(oc + 1) * QT],
                    self.acc[:, oc * QT:(oc + 1) * QT],
                    self.P8[:, 2048 + slot * 1024: 2048 + slot * 1024 + QT]),
                    waits=[(s_mm, tdn)], fence=True)
                self.edn_free[slot] = f2

        if self.dbg:
            vdone = self.s_ve.n
            for kk in range(16):
                self.dma("sync", self.dbg["dacc_ex"][kk * 128:(kk + 1) * 128, :],
                         self.acc[:, kk * QT:(kk + 1) * QT], self.s_st,
                         waits=[(s_ve, vdone)] if kk == 0 else [])

        tld = {}
        v_all_done = s_ve.n
        for oc in range(16):
            slot = oc % 2
            wl = [(s_ve, tld[slot])] if slot in tld else [(s_ve, v_all_done)]
            t = self.dma("sync", self.x1r[:, slot * QT:(slot + 1) * QT],
                         self.x1_d[oc * 128:(oc + 1) * 128, :], self.s_ld, waits=wl)
            f = self.vop(lambda e, oc=oc, slot=slot: e.tensor_add(
                self.acc[:, oc * QT:(oc + 1) * QT], self.acc[:, oc * QT:(oc + 1) * QT],
                self.x1r[:, slot * QT:(slot + 1) * QT]),
                waits=[(self.s_ld, t)], fence=True)
            tld[slot] = f
        t_fin = s_ve.n
        stage_busy = 0
        for jt in range(8):
            fprev = None
            for oc in range(16):
                tt = self.mm(lambda e, jt=jt, oc=oc: e.transpose(
                    self.P8[:, 3072:3200], self.acc[:, oc * QT + jt * 128: oc * QT + (jt + 1) * 128],
                    self.ident_f[:]),
                    waits=[(s_ve, t_fin)] + ([(s_ve, fprev)] if fprev else []) +
                          ([(s_st, stage_busy)] if (oc == 0 and stage_busy) else []),
                    inc=True)
                fprev = self.vop(lambda e, oc=oc: e.tensor_copy(
                    self.stage[:, oc * 128:(oc + 1) * 128], self.P8[:, 3072:3200]),
                    waits=[(s_mm, tt)], fence=True)
            stage_busy = self.dma("sync", self.outp[jt * 128:(jt + 1) * 128, :], self.stage[:],
                                  self.s_st, waits=[(s_ve, fprev)])
        pr.wait("gpsimd", self.s_st)
        pr.wait("sync", self.s_st)


def build_l1c2_program():
    nc = bass.Bass("TRN2", target_bir_lowering=False)
    c = L1C2(nc)
    c.build()
    c.pr.finish(nc)
    c.ctx.close()
    return nc


def l1c2_in_maps(inp, attn_fm, xn1_fm, x1_fm_full):
    wo = np.ascontiguousarray(inp["L1_attn_W_o"].astype(np.float32).T)
    wgate = np.ascontiguousarray(inp["L1_ffn_gate_w"].astype(np.float32).T)
    wsg = np.ascontiguousarray(inp["L1_ffn_shared_gate_w"].astype(np.float32).T)
    wsu = np.ascontiguousarray(inp["L1_ffn_shared_up_w"].astype(np.float32).T)
    wsd = np.ascontiguousarray(inp["L1_ffn_shared_down_w"].astype(np.float32).T)
    wseg = np.zeros((D, 128), np.float32)
    wseg[:, 0] = inp["L1_ffn_shared_expert_gate_w"].astype(np.float32)[0]
    gupt = np.ascontiguousarray(inp["L1_ffn_gate_up_proj"].astype(np.float32).transpose(0, 2, 1))
    gdnt = np.ascontiguousarray(inp["L1_ffn_down_proj"].astype(np.float32).transpose(0, 2, 1))
    e32 = np.zeros((32, 32 * 128), np.float32)
    for e_ in range(32):
        e32[e_, e_ * 128:(e_ + 1) * 128] = 1.0
    ident = np.eye(128, dtype=np.float32)
    ones_row = np.ones((1, 128), np.float32)
    ones_col = np.ones((128, 1), np.float32)
    maps = []
    for c in range(8):
        bc = c // 4
        cols = own_token_cols(c)
        maps.append({
            "attnTc": np.ascontiguousarray(attn_fm[bc][:, cols]),
            "xn1c": np.ascontiguousarray(xn1_fm[bc][:, cols]),
            "x1own": np.ascontiguousarray(x1_fm_full[bc][:, cols]),
            "wo": wo, "wgate": wgate, "wsg": wsg, "wsu": wsu, "wsd": wsd, "wseg": wseg,
            "gupt": gupt, "gdnt": gdnt, "e32sel": e32,
            "ident": ident, "ones_row": ones_row, "ones_col": ones_col,
        })
    return maps


from concourse.bass_utils import run_bass_kernel_spmd

_PROGS = {}


def _prog(name, builder):
    if name not in _PROGS:
        _PROGS[name] = builder()
    return _PROGS[name]


def kernel(**inputs):
    import ml_dtypes
    inp = {k: np.asarray(v) for k, v in inputs.items()}
    cores = list(range(8))

    nc_a = _prog("a", build_l0_program)
    res_a = run_bass_kernel_spmd(nc_a, l0_in_maps(inp), core_ids=cores)
    x1_fm = np.zeros((2, D, S), np.float32)
    for c in range(8):
        sl = res_a.results[c]["x1p"]
        x1_fm[0][c * DSL:(c + 1) * DSL, :] = sl[:, :S]
        x1_fm[1][c * DSL:(c + 1) * DSL, :] = sl[:, S:]

    nc_b = _prog("b", build_l1b_program)
    res_b = run_bass_kernel_spmd(nc_b, l1b_in_maps(inp, x1_fm), core_ids=cores)
    BFD = ml_dtypes.bfloat16
    q_fm = [np.zeros((2048, S), BFD) for _ in range(2)]
    k_fm = [np.zeros((2048, S), BFD) for _ in range(2)]
    v_tm = [np.zeros((S, 2048), BFD) for _ in range(2)]
    xn1_fm = [np.zeros((D, S), BFD) for _ in range(2)]
    for c in range(8):
        bc = c // 4
        cols = own_token_cols(c)
        rr = res_b.results[c]
        q_fm[bc][:, cols] = rr["qout"]
        k_fm[bc][:, cols] = rr["kout"]
        v_tm[bc][cols, :] = rr["vout"]
        xn1_fm[bc][:, cols] = rr["xnout"]

    nc_c1 = _prog("c1", build_l1c1_program)
    res_c1 = run_bass_kernel_spmd(nc_c1, l1c1_in_maps(q_fm, k_fm, v_tm), core_ids=cores)
    attn_fm = [np.zeros((2048, S), BFD) for _ in range(2)]
    for c in range(8):
        bc = c // 4
        h0 = (c % 4) * 4
        attn_fm[bc][h0 * 128:(h0 + 4) * 128, :] = res_c1.results[c]["aout"]

    nc_c2 = _prog("c2", build_l1c2_program)
    res_c2 = run_bass_kernel_spmd(nc_c2, l1c2_in_maps(inp, attn_fm, xn1_fm, x1_fm),
                                  core_ids=cores)
    out = np.zeros((2, S, D), np.float32)
    for c in range(8):
        bc = c // 4
        cols = own_token_cols(c)
        out[bc][cols, :] = res_c2.results[c]["outp"]
    return out


ECAP = 1536
AUGW = 2056
DMOE_X = 768


class MoEX:
    def __init__(self, nc):
        self.nc = nc
        self.ctx = ExitStack()
        self.pr = Prog()
        def din(name, shape, dt=F32):
            return nc.dram_tensor(name, list(shape), dt, kind="ExternalInput").ap()
        self.aug_d = din("aug", [NTOK, AUGW], BF16)
        self.comb4_d = din("comb4", [4, NTOK], BF16)
        self.gup4_d = din("gup4", [4, D, 2 * DMOE_X])
        self.gdn4_d = din("gdn4", [4, DMOE_X, D])
        self.ident_d = din("ident", [128, 128])
        self.pmoe = nc.dram_tensor("pmoe", [NTOK + 128, D], F32,
                                   kind="ExternalOutput").ap()
        self.xg_dram = [nc.dram_tensor(f"xg{i}", [ECAP, AUGW], BF16, kind="Internal").ap()
                        for i in range(2)]
        ctx = self.ctx
        sbuf = lambda name, shape, dt=F32: ctx.enter_context(
            nc.sbuf_tensor(name, list(shape), dt))
        self.crow = sbuf("z_crow", [1, NTOK], BF16)
        self.mrow = sbuf("z_mrow", [1, NTOK], BF16)
        self.prow = sbuf("z_prow", [1, NTOK], F32)
        self.ones8k = sbuf("z_ones8k", [1, NTOK], BF16)
        self.slotc = sbuf("z_slotc", [128, 64], F32)
        self.sloti = sbuf("z_sloti", [128, 64], mybir.dt.int32)
        self.augrow = sbuf("z_augrow", [128, 2 * AUGW], BF16)
        self.xgt = sbuf("z_xgt", [128, 2 * AUGW], BF16)
        self.xgf = sbuf("z_xgf", [128, 16 * ECAP], BF16)
        self.wcol = sbuf("z_wcol", [128, 16], F32)
        self.tokidc = sbuf("z_tokidc", [128, 16], mybir.dt.int32)
        self.he = sbuf("z_he", [128, 6 * ECAP], BF16)
        self.gs = sbuf("z_gs", [128, 2 * 512], BF16)
        self.ydf = sbuf("z_ydf", [128, 2 * 128], F32)
        self.ytm = sbuf("z_ytm", [128, 2 * D], F32)
        self.wra_t = sbuf("z_wra", [128, 2 * 16 * 128], BF16)
        self.wrb_t = sbuf("z_wrb", [128, 2 * 16 * 128], BF16)
        self.wrd_t = sbuf("z_wrd", [128, 2 * 6 * 128], BF16)
        self.ident_f = sbuf("z_identf", [128, 128], F32)
        self.ident_b = sbuf("z_identb", [128, 128], BF16)
        self.zero_t = sbuf("z_zerot", [128, 2056], BF16)
        self.P8 = ctx.enter_context(nc.psum_tensor("P8", [128, 7 * 512], F32))
        self.PT = ctx.enter_context(nc.psum_tensor("PT", [128, 512], BF16))
        ss = lambda name: Sem(nc, ctx, name)
        self.s_ld = ss("s_ld")
        self.s_lp = ss("s_lp")
        self.s_w = ss("s_w")
        self.s_mm = ss("s_mm")
        self.s_ve = ss("s_ve")
        self.s_se = ss("s_se")
        self.s_st = ss("s_st")
        self.s_sc = ss("s_sc")
        self.s_s2 = [ss("s_s2a"), ss("s_s2b")]
        self.s_la = [ss("s_la0"), ss("s_la1")]
        self.s_lg = [ss("s_lg0"), ss("s_lg1")]

    def dma(self, eng, out_ap, in_ap, sem, waits=()):
        return self.pr.op(eng, lambda e, o=out_ap, i=in_ap: e.dma_start(out=o, in_=i),
                          waits=waits, inc=sem, inc_k=16)

    def vop(self, fn, waits=(), fence=False):
        w = list(waits)
        if fence and self.s_ve.n > 0:
            w.append((self.s_ve, self.s_ve.n))
        return self.pr.op("vector", fn, waits=w, inc=self.s_ve)

    def sop(self, fn, waits=(), fence=False):
        w = list(waits)
        if fence and self.s_se.n > 0:
            w.append((self.s_se, self.s_se.n))
        return self.pr.op("scalar", fn, waits=w, inc=self.s_se)

    def mm(self, fn, waits=(), inc=False):
        return self.pr.op("tensor", fn, waits=waits, inc=self.s_mm if inc else None)

    def build(self):
        pr = self.pr
        s_mm, s_ve, s_se, s_st, s_w, s_sc, s_ld, s_lp = (
            self.s_mm, self.s_ve, self.s_se, self.s_st, self.s_w, self.s_sc,
            self.s_ld, self.s_lp)
        s_s2 = self.s_s2
        self.dma("sync", self.ident_f[:], self.ident_d, s_ld)
        self.dma("gpsimd", self.ident_b[:], self.ident_d, s_lp)
        f0 = self.vop(lambda e: e.memset(self.ones8k[:], 1.0))
        f0 = self.vop(lambda e: e.memset(self.zero_t[:], 0.0))
        f0 = self.vop(lambda e: e.memset(self.zero_t[:, 2052:2053], 128.0),
                      fence=True)
        for r in range(NTOK // 128):
            self.dma("gpsimd", self.pmoe[r * 128:(r + 1) * 128, :],
                     self.zero_t[:, 0:2048], s_st, waits=[(s_ve, f0)] if r == 0 else [])
        t_pz = s_st.n

        ra = WRing(self, self.wra_t, 2)
        rb = WRing(self, self.wrb_t, 2)
        rd = WRing(self, self.wrd_t, 2, stride=768)
        PH = getattr(MoEX, "PH", 5)
        xg_busy = {}
        prev_slotc = {}
        fi_prev = 0
        last_scat2 = None
        scat2_hist = {}
        fxg_prev = {}
        pfree = {}
        ydf_tty = {}
        self.xgt_free = {}
        for el in range(4):
            xgb = self.xg_dram[el % 2]
            tcr = self.dma("gpsimd", self.crow[:], self.comb4_d[el:el + 1, :], s_lp,
                           waits=[(s_ve, self.s_ve.n)])
            f = self.vop(lambda e: e.tensor_single_scalar(self.mrow[:], self.crow[:],
                                                          0.0, ALU.is_gt),
                         waits=[(s_lp, tcr)])
            f = self.vop(lambda e: e.tensor_tensor_scan(self.prow[:], self.ones8k[:],
                                                        self.mrow[:], 0.0, ALU.mult,
                                                        ALU.add), fence=True)
            f = self.vop(lambda e: e.tensor_scalar_add(self.prow[:], self.prow[:],
                                                       -(1.0 + 1e6)), fence=True)
            f = self.vop(lambda e: e.tensor_mul(self.prow[:], self.prow[:],
                                                self.mrow[:]), fence=True)
            t_sel = self.vop(lambda e: e.tensor_scalar_add(self.prow[:], self.prow[:],
                                                           1e6), fence=True)
            slotc_sc = {}
            for tt in range(64):
                ph = 3072 + (tt % 2)
                wts = [(s_ve, t_sel)]
                if el == 0 and tt == 0:
                    wts.append((s_ld, 16))
                if tt < 2:
                    if el > 0:
                        wts.append((s_se, prev_slotc[tt % 2]))
                else:
                    wts.append((s_se, slotc_sc[tt % 2]))
                tts = self.mm(lambda e, tt=tt, ph=ph: e.matmul(
                    self.P8[:, ph:ph + 1], self.prow[0:1, tt * 128:(tt + 1) * 128],
                    self.ident_f[0:1, 0:1], start=True, stop=True),
                    waits=wts, inc=True)
                scw = [(s_mm, tts)]
                if tt == 0 and el > 0:
                    scw.append((s_ve, fi_prev))
                fsc = self.sop(lambda e, tt=tt, ph=ph: e.activation(
                    self.slotc[:, tt:tt + 1], self.P8[:, ph:ph + 1], AF.Copy),
                    waits=scw)
                slotc_sc[tt % 2] = fsc
            prev_slotc = dict(slotc_sc)
            fi = self.vop(lambda e: e.tensor_copy(self.sloti[:], self.slotc[:]),
                          waits=[(s_se, self.s_se.n)])
            fi_prev = fi
            wz = []
            if el == 0:
                wz = [(s_ve, f0)]
            elif (el % 2) in xg_busy:
                wz = [xg_busy[el % 2], xg_busy[(el % 2, 1)]]
            for r in range(ECAP // 128):
                self.dma("gpsimd", xgb[r * 128:(r + 1) * 128, :], self.zero_t[:],
                         s_sc, waits=wz if r == 0 else [])
            t_xgz = s_sc.n
            prev_stage = {}
            for tt in range(64):
                aslot = tt % 2
                wls = []
                if aslot in prev_stage:
                    wls.append((s_sc, prev_stage[aslot]))
                if tt == 0:
                    wls += [(s_sc, t_xgz), (s_ve, fi)]
                tld = self.dma("sync", self.augrow[:, aslot * AUGW:(aslot + 1) * AUGW],
                               self.aug_d[tt * 128:(tt + 1) * 128, :],
                               self.s_la[aslot], waits=wls)
                def _scat(e, tt=tt, aslot=aslot, xgb=xgb):
                    if not hasattr(self, "_bc_cap"):
                        self._bc_cap = e.to_reg(ECAP - 1)
                    return e.indirect_dma_start(
                        xgb, IndirectOffsetOnAxis(self.sloti[:, tt:tt + 1], 0),
                        self.augrow[:, aslot * AUGW:(aslot + 1) * AUGW], None,
                        bounds_check=self._bc_cap, oob_is_err=False)
                tsc = self.pr.op("gpsimd", _scat, waits=[(self.s_la[aslot], tld)],
                                 inc=s_sc, inc_k=16)
                prev_stage[aslot] = tsc
            t_scat = s_sc.n
            if PH <= 1:
                continue

            for st in range(ECAP // 128):
                gslot = st % 2
                tgw = ([(s_sc, t_scat)] if st < 2 else [])
                if gslot in self.xgt_free:
                    tgw.append((s_ve, self.xgt_free[gslot]))
                if gslot in getattr(self, "xgt_pe_free", {}):
                    tgw.append((s_mm, self.xgt_pe_free[gslot]))
                tg = self.dma("sync", self.xgt[:, gslot * AUGW:(gslot + 1) * AUGW],
                              xgb[st * 128:(st + 1) * 128, :], self.s_lg[gslot],
                              waits=tgw)
                fw = self.vop(lambda e, st=st, gslot=gslot, el=el: e.tensor_copy(
                    self.wcol[:, st:st + 1],
                    self.xgt[:, gslot * AUGW + 2048 + el: gslot * AUGW + 2049 + el]),
                    waits=[(self.s_lg[gslot], tg)])
                fhw = [(self.s_lg[gslot], tg)]
                if st == 0 and last_scat2 is not None:
                    fhw += [(s_s2[0], last_scat2[0]), (s_s2[1], last_scat2[1])]
                fh = self.vop(lambda e, st=st, gslot=gslot: e.scalar_tensor_tensor(
                    self.tokidc[:, st:st + 1],
                    self.xgt[:, gslot * AUGW + 2052: gslot * AUGW + 2053], 64.0,
                    self.xgt[:, gslot * AUGW + 2053: gslot * AUGW + 2054],
                    ALU.mult, ALU.add), waits=fhw)
                self.xgt_free[gslot] = fh
                for kk in range(16):
                    ph = (kk % 2) * 1024
                    wtt = []
                    if kk == 0:
                        wtt.append((s_ve, fh))
                        if el == 0 and st == 0:
                            wtt.append((s_lp, 16))
                        if 0 in fxg_prev:
                            wtt.append((s_se, fxg_prev[0]))
                    elif kk == 1:
                        if 1 in fxg_prev:
                            wtt.append((s_se, fxg_prev[1]))
                    else:
                        wtt.append((s_se, fxg_sc[kk % 2]))
                    ttx = self.mm(lambda e, gslot=gslot, kk=kk, ph=ph: e.matmul(
                        self.P8[:, ph:ph + 128],
                        self.xgt[:, gslot * AUGW + kk * 128: gslot * AUGW + (kk + 1) * 128],
                        self.ident_b[:], start=True, stop=True),
                        waits=wtt, inc=True)
                    fxg = self.sop(lambda e, st=st, kk=kk, ph=ph: e.activation(
                        self.xgf[:, kk * ECAP + st * 128: kk * ECAP + (st + 1) * 128],
                        self.P8[:, ph:ph + 128], AF.Copy),
                        waits=[(s_mm, ttx)])
                    if kk == 0:
                        fxg_sc = {}
                    fxg_sc[kk % 2] = fxg
                fxg_prev = {0: fxg_sc[0], 1: fxg_sc[1]}
                if not hasattr(self, "xgt_pe_free"):
                    self.xgt_pe_free = {}
                self.xgt_pe_free[gslot] = ttx
            t_xgf = self.s_se.n
            xg_busy[el % 2] = (self.s_lg[0], self.s_lg[0].n)
            xg_busy[(el % 2, 1)] = (self.s_lg[1], self.s_lg[1].n)
            if PH <= 2:
                continue

            NSC = ECAP // 512
            for mp in range(6):
                wslg, wg = ra.load(self.gup4_d[el], 0, 16, mp, 128)
                wslu, wu = rb.load(self.gup4_d[el], 0, 16, 6 + mp, 128)
                for sc in range(NSC):
                    ii = mp * NSC + sc
                    pa = 512 * (ii % 2)
                    pb = 1024 + 512 * (ii % 2)
                    for k in range(16):
                        wl = []
                        if k == 0:
                            wl = [(s_w, wg), (s_se, t_xgf)]
                            if ("a", ii % 2) in pfree:
                                wl.append((s_se, pfree[("a", ii % 2)]))
                        self.mm(lambda e, k=k, sc=sc, pa=pa, wslg=wslg: e.matmul(
                            self.P8[:, pa:pa + 512], ra.tile(wslg, k, 128),
                            self.xgf[:, k * ECAP + sc * 512: k * ECAP + (sc + 1) * 512],
                            start=(k == 0), stop=(k == 15)), waits=wl)
                    for k in range(16):
                        wl = []
                        if k == 0:
                            wl = [(s_w, wu)]
                            if ("b", ii % 2) in pfree:
                                wl.append((s_ve, pfree[("b", ii % 2)]))
                        self.mm(lambda e, k=k, sc=sc, pb=pb, wslu=wslu: e.matmul(
                            self.P8[:, pb:pb + 512], rb.tile(wslu, k, 128),
                            self.xgf[:, k * ECAP + sc * 512: k * ECAP + (sc + 1) * 512],
                            start=(k == 0), stop=(k == 15)), waits=wl,
                            inc=(k == 15))
                    t_gu = s_mm.n
                    fsw = [(s_mm, t_gu)]
                    if ("b", ii % 2) in pfree:
                        fsw.append((s_ve, pfree[("b", ii % 2)]))
                    fs = self.sop(lambda e, ii=ii, pa=pa: e.activation(
                        self.gs[:, (ii % 2) * 512:(ii % 2) * 512 + 512],
                        self.P8[:, pa:pa + 512], AF.Silu),
                        waits=fsw)
                    pfree[("a", ii % 2)] = fs
                    fv = self.vop(lambda e, mp=mp, sc=sc, ii=ii, pb=pb: e.tensor_mul(
                        self.he[:, mp * ECAP + sc * 512: mp * ECAP + (sc + 1) * 512],
                        self.gs[:, (ii % 2) * 512:(ii % 2) * 512 + 512],
                        self.P8[:, pb:pb + 512]),
                        waits=[(s_se, fs)])
                    pfree[("b", ii % 2)] = fv
                t_last_gu = s_mm.n
                ra.used(wslg, t_last_gu)
                rb.used(wslu, t_last_gu)
            t_h = self.s_ve.n
            if PH <= 3:
                continue

            ytw_prev = None
            for st in range(ECAP // 128):
                ybuf = st % 2
                for oc in range(16):
                    wsld, wd = rd.load(self.gdn4_d[el], 0, 6, oc, 128)
                    pd = 2048 + 512 * (oc % 2)
                    for mpk in range(6):
                        wl = []
                        if mpk == 0:
                            wl = [(s_w, wd), (s_ve, t_h)]
                            if ("d", oc % 2) in pfree:
                                wl.append((s_ve, pfree[("d", oc % 2)]))
                        self.mm(lambda e, mpk=mpk, st=st, pd=pd, wsld=wsld: e.matmul(
                            self.P8[:, pd:pd + 128], rd.tile(wsld, mpk, 128),
                            self.he[:, mpk * ECAP + st * 128: mpk * ECAP + (st + 1) * 128],
                            start=(mpk == 0), stop=(mpk == 5)), waits=wl,
                            inc=(mpk == 5))
                    t_dn = s_mm.n
                    rd.used(wsld, t_dn)
                    fyw = [(s_mm, t_dn)]
                    if (oc % 2) in ydf_tty:
                        fyw.append((s_mm, ydf_tty[oc % 2]))
                    fy = self.vop(lambda e, oc=oc, pd=pd: e.tensor_copy(
                        self.ydf[:, (oc % 2) * 128:(oc % 2) * 128 + 128],
                        self.P8[:, pd:pd + 128]),
                        waits=fyw)
                    pfree[("d", oc % 2)] = fy
                    ttw = [(s_ve, fy)]
                    if ytw_prev is not None:
                        ttw.append((s_ve, ytw_prev))
                    tty = self.mm(lambda e, oc=oc: e.transpose(
                        self.P8[:, 3328:3456],
                        self.ydf[:, (oc % 2) * 128:(oc % 2) * 128 + 128],
                        self.ident_f[:]), waits=ttw, inc=True)
                    ydf_tty[oc % 2] = tty
                    yw = [(s_mm, tty)]
                    if oc == 0 and ybuf in scat2_hist:
                        yw.append((s_s2[ybuf], scat2_hist[ybuf]))
                    ytw = self.vop(lambda e, st=st, oc=oc, ybuf=ybuf: e.tensor_scalar(
                        self.ytm[:, ybuf * D + oc * 128: ybuf * D + (oc + 1) * 128],
                        self.P8[:, 3328:3456], self.wcol[:, st:st + 1], None,
                        ALU.mult),
                        waits=yw)
                    ytw_prev = ytw
                sw = [(s_ve, ytw_prev)]
                if getattr(MoEX, "SER2", 0) and (st > 0 or last_scat2 is not None):
                    sw.append((s_s2[1 - ybuf], self.s_s2[1 - ybuf].n))
                if st == 0:
                    sw.append((s_st, t_pz))
                    if last_scat2 is not None:
                        sw += [(s_s2[0], last_scat2[0]), (s_s2[1], last_scat2[1])]
                def _scat2(e, st=st, ybuf=ybuf):
                    if not hasattr(self, "_bc_tok"):
                        self._bc_tok = e.to_reg(NTOK + 127)
                    return e.indirect_dma_start(
                        self.pmoe, IndirectOffsetOnAxis(self.tokidc[:, st:st + 1], 0),
                        self.ytm[:, ybuf * D:(ybuf + 1) * D], None,
                        bounds_check=self._bc_tok, oob_is_err=False,
                        compute_op=ALU.add)
                if PH <= 4:
                    continue
                tsc2 = self.pr.op("gpsimd", _scat2, waits=sw, inc=s_s2[ybuf],
                                  inc_k=16)
                scat2_hist[ybuf] = tsc2
            last_scat2 = (s_s2[0].n, s_s2[1].n)
        pr.wait("gpsimd", s_sc)
        pr.wait("gpsimd", s_s2[0])
        pr.wait("gpsimd", s_s2[1])
        pr.wait("sync", self.s_la[0])
        pr.wait("sync", self.s_la[1])
        pr.wait("sync", self.s_lg[0])
        pr.wait("sync", self.s_lg[1])
        pr.wait("gpsimd", s_st)
        pr.wait("gpsimd", s_lp)
        pr.wait("sync", s_ld)




def build_moex_program():
    nc = bass.Bass("TRN2", target_bir_lowering=False)
    m = MoEX(nc)
    m.build()
    m.pr.finish(nc)
    m.ctx.close()
    return nc


def moex_in_maps(inp, xn1_fm, comb_fm_full):
    import ml_dtypes
    BFD = ml_dtypes.bfloat16
    gupt = np.ascontiguousarray(inp["L1_ffn_gate_up_proj"].astype(np.float32).transpose(0, 2, 1))
    gdnt = np.ascontiguousarray(inp["L1_ffn_down_proj"].astype(np.float32).transpose(0, 2, 1))
    ident = np.eye(128, dtype=np.float32)
    xn1tm = np.concatenate([np.asarray(x).T for x in xn1_fm], axis=0)
    tok = np.arange(NTOK)
    hi = (tok // 64).astype(np.float32).astype(BFD)
    lo = (tok % 64).astype(np.float32).astype(BFD)
    maps = []
    for c in range(8):
        es = slice(4 * c, 4 * c + 4)
        aug = np.zeros((NTOK, AUGW), BFD)
        aug[:, 0:2048] = xn1tm
        aug[:, 2048:2052] = comb_fm_full[es].T
        aug[:, 2052] = hi
        aug[:, 2053] = lo
        maps.append({
            "aug": aug,
            "comb4": np.ascontiguousarray(comb_fm_full[es]),
            "gup4": np.ascontiguousarray(gupt[es]),
            "gdn4": np.ascontiguousarray(gdnt[es]),
            "ident": ident,
        })
    return maps
